# version 02
# baseline (speedup 1.0000x reference)
import numpy as np
import ml_dtypes
import concourse.bass as bass
import concourse.mybir as mybir
from concourse.bass_utils import run_bass_kernel_spmd
from contextlib import ExitStack

F32 = mybir.dt.float32
BF16 = mybir.dt.bfloat16
I32 = mybir.dt.int32
AF = mybir.ActivationFunctionType
ALU = mybir.AluOpType
BF = ml_dtypes.bfloat16
AX = mybir.AxisListType

P = 128
D = 4096
NCHUNK = D // P
NT = 8
NTOK = NT * P
KD = 1024
NQH = 32
NKVH = 8
DH = 128
S = 4096
EPS = 1e-5
ROPE_THETA = 10000.0
DBG_NO_HSUM = False
DBG_NO_ZS = False


def _pretile(w, tile=P):
    K, M = w.shape
    return np.ascontiguousarray(w.reshape(K, M // tile, tile).transpose(1, 0, 2))


def _rope_tables(pos, scale):
    inv_freq = 1.0 / ROPE_THETA ** (np.arange(0, 64, dtype=np.float64) / 64.0)
    ang = inv_freq[:, None] * np.asarray(pos, np.float64)[None, :]
    return (np.cos(ang) * scale).astype(np.float32), \
           (np.sin(ang) * scale).astype(np.float32)


def _own_rows(j):
    idx = []
    for i in range(NT):
        t = 4 * i + j
        idx.extend(range(P * t, P * t + P))
    return np.asarray(idx)


def _masks(j):
    m = np.zeros((4, P, P), np.float32)
    for kk in range(4):
        if kk < j:
            m[kk] = 1.0
        elif kk == j:
            m[kk] = np.tril(np.ones((P, P), np.float32)).T
    return m.astype(BF)


def _pack_reorder_gu(wq_packed):
    t = np.ascontiguousarray(wq_packed.T)
    J, I = t.shape
    return np.ascontiguousarray(
        t.reshape(J // 32, 32, I).transpose(1, 0, 2).reshape(J, I))


def _pack_l1(packed):
    O = packed.shape[0]
    t = np.ascontiguousarray(packed.reshape(O, 2048).T)
    return np.ascontiguousarray(
        t.reshape(32, 64, O).transpose(1, 0, 2).reshape(2048, O))



class Emit:

    def __init__(self, nc, stack):
        self.nc = nc
        self.stack = stack
        self.block = stack.enter_context(nc.Block())
        self.sems = {}
        self.cnt = {}
        for name in ("ld", "lg", "pe", "dv", "ac", "so"):
            self.sems[name] = stack.enter_context(nc.semaphore(name))
            self.cnt[name] = 0

    def sem(self, name):
        if name not in self.sems:
            self.sems[name] = self.stack.enter_context(self.nc.semaphore(name))
            self.cnt[name] = 0
        return self.sems[name]

    def dma(self, q, name, out, in_):
        s = self.sem(name)
        q.dma_start(out=out, in_=in_).then_inc(s, 16)
        return s, self.inc(name, 16)

    def wv(self, name):
        return self.sem(name), self.cnt[name]

    def sec(self, eng, fn):
        getattr(self.block, eng)(fn)

    def inc(self, name, n=1):
        self.cnt[name] += n
        return self.cnt[name]

    def val(self, name):
        return self.cnt[name]


def sb_tiles(nc, stack, name, shape, dt=F32):
    return stack.enter_context(nc.sbuf_tensor(name, shape, dt))



def build_qkv_l0():
    nc = bass.Bass("TRN2", target_bir_lowering=False, debug=False)
    x = nc.dram_tensor("x", [NTOK, D], F32, kind="ExternalInput").ap()
    wq = nc.dram_tensor("wq", [32, D, P], F32, kind="ExternalInput").ap()
    wkv = nc.dram_tensor("wkv", [16, D, P], F32, kind="ExternalInput").ap()
    cq = nc.dram_tensor("cq", [64, NTOK], F32, kind="ExternalInput").ap()
    sq = nc.dram_tensor("sq", [64, NTOK], F32, kind="ExternalInput").ap()
    ck = nc.dram_tensor("ck", [64, NTOK], F32, kind="ExternalInput").ap()
    sk = nc.dram_tensor("sk", [64, NTOK], F32, kind="ExternalInput").ap()
    ident = nc.dram_tensor("ident", [P, P], BF16, kind="ExternalInput").ap()
    qf = nc.dram_tensor("qf", [D, NTOK], BF16, kind="ExternalOutput").ap()
    kf = nc.dram_tensor("kf", [KD, NTOK], BF16, kind="ExternalOutput").ap()
    vt = nc.dram_tensor("vt", [NTOK, KD], BF16, kind="ExternalOutput").ap()

    with ExitStack() as stk:
        em = Emit(nc, stk)
        _emit_ln_xnf(nc, em, stk, x, ident)
        xnf = em.xnf
        _emit_qkv_common(nc, em, stk, wq, wkv, cq, sq, ck, sk, qf, kf, vt)
    return nc


def _emit_ln_xnf(nc, em, stk, x, ident, xwidth=D):
    xt = [sb_tiles(nc, stk, f"xt{i}", [P, D]) for i in range(2)]
    xnf = sb_tiles(nc, stk, "xnf", [P, NCHUNK * NTOK], BF16)
    xnt = sb_tiles(nc, stk, "xnt", [P, D], BF16)
    idn = sb_tiles(nc, stk, "idn", [P, P], BF16)
    bns = sb_tiles(nc, stk, "bns", [P, 48])
    bna = sb_tiles(nc, stk, "bna", [P, 2])
    lnv = sb_tiles(nc, stk, "lnv", [P, 1])
    negm = sb_tiles(nc, stk, "negm", [P, 1])
    rstd = sb_tiles(nc, stk, "rstd", [P, 1])
    epsv = sb_tiles(nc, stk, "epsv", [P, 1])
    pt = [stk.enter_context(nc.psum_tensor(f"pt{i}", [P, 1024], BF16))
          for i in range(2)]
    em.xnf = xnf
    em.idn = idn
    em.pt = pt

    dv, ac, pe = em.sems["dv"], em.sems["ac"], em.sems["pe"]

    xw = {}

    def load_x(s, tt):
        em.dma(s, f"x{tt % 2}", xt[tt % 2][:], x[bass.ts(tt, P), :])
        xw[tt] = em.wv(f"x{tt % 2}")

    def sec_init(s):
        em.dma(s, "cst", idn[:], ident)
        load_x(s, 0)
        load_x(s, 1)
    em.sec("sync", sec_init)
    em.cstw = em.wv("cst")

    def init_eps(v):
        v.memset(epsv[:], EPS)
    em.sec("vector", init_eps)

    for tt in range(NT):
        buf = xt[tt % 2]

        def stats(v, tt=tt, buf=buf, w=xw[tt]):
            v.wait_ge(*w)
            for hh in range(8):
                i_ = v.bn_stats(out=bns[:, 6 * hh:6 * hh + 6],
                                in_=buf[:, 512 * hh:512 * hh + 512])
            i_.then_inc(dv, 1)
            v.wait_ge(dv, em.inc("dv"))
            v.bn_aggr(out=bna[:], in_=bns[:]).then_inc(dv, 1)
            v.wait_ge(dv, em.inc("dv"))
            v.tensor_scalar(out=negm[:], in0=bna[:, 0:1], scalar1=-1.0,
                            scalar2=None, op0=ALU.mult).then_inc(dv, 1)
        em.sec("vector", stats)
        sdv = em.inc("dv")

        def rstd_f(sc, sdv=sdv):
            sc.wait_ge(dv, sdv)
            sc.activation(out=lnv[:], in_=bna[:, 1:2], func=AF.Ln,
                          bias=epsv[:, 0:1], scale=1.0).then_inc(ac, 1)
            sc.wait_ge(ac, em.inc("ac"))
            sc.activation(out=rstd[:], in_=lnv[:], func=AF.Exp,
                          bias=0.0, scale=-0.5).then_inc(ac, 1)
        em.sec("scalar", rstd_f)
        sac = em.inc("ac")

        def norm(v, buf=buf, sac=sac):
            v.wait_ge(ac, sac)
            v.tensor_scalar(out=xnt[:], in0=buf[:], scalar1=negm[:, 0:1],
                            scalar2=rstd[:, 0:1], op0=ALU.add,
                            op1=ALU.mult).then_inc(dv, 1)
        em.sec("vector", norm)
        norm_dv = em.inc("dv")

        if tt + 2 < NT:
            def prefetch(s, tt=tt, w=norm_dv):
                s.wait_ge(dv, w)
                load_x(s, tt + 2)
            em.sec("sync", prefetch)

        def trans(t, w=norm_dv, base_dv=em.val("dv"), cst=em.cstw):
            t.wait_ge(*cst)
            t.wait_ge(dv, w)
            for c in range(NCHUNK):
                if c >= 2:
                    t.wait_ge(dv, base_dv + c - 1)
                t.transpose(out=pt[c % 2][:, 0:P], in_=xnt[:, bass.ts(c, P)],
                            identity=idn[:]).then_inc(pe, 1)
        em.sec("tensor", trans)
        tr_base = em.val("pe")
        em.inc("pe", NCHUNK)

        def evac(v, tt=tt, tr_base=tr_base):
            for c in range(NCHUNK):
                v.wait_ge(pe, tr_base + c + 1)
                v.tensor_copy(out=xnf[:, c * NTOK + tt * P:c * NTOK + (tt + 1) * P],
                              in_=pt[c % 2][:, 0:P]).then_inc(dv, 1)
        em.sec("vector", evac)
        em.inc("dv", NCHUNK)
    em.ln_done = em.val("dv")


def _emit_qkv_common(nc, em, stk, wq, wkv, cq, sq, ck, sk, qf, kf, vt):
    xnf, idn, pt = em.xnf, em.idn, em.pt
    pe, dv, ac = em.sems["pe"], em.sems["dv"], em.sems["ac"]
    wb = [sb_tiles(nc, stk, f"wb{i}", [P, D], BF16) for i in range(2)]
    cqs = sb_tiles(nc, stk, "cqs", [64, NTOK])
    sqs = sb_tiles(nc, stk, "sqs", [64, NTOK])
    cks = sb_tiles(nc, stk, "cks", [64, NTOK])
    sks = sb_tiles(nc, stk, "sks", [64, NTOK])
    rt1 = sb_tiles(nc, stk, "rt1", [64, NTOK])
    rt2 = sb_tiles(nc, stk, "rt2", [64, NTOK])
    rout = [sb_tiles(nc, stk, f"rout{i}", [P, NTOK], BF16) for i in range(2)]
    vf = sb_tiles(nc, stk, "vf", [P, 8 * NTOK], BF16)
    vts = sb_tiles(nc, stk, "vts", [P, 8 * KD], BF16)
    pq = [stk.enter_context(nc.psum_tensor(f"pq{i}", [P, 1024])) for i in range(2)]

    def tables(s):
        for t_, d_ in ((cqs, cq), (sqs, sq), (cks, ck), (sks, sk)):
            em.dma(s, "cst", t_[:], d_)
    em.sec("sync", tables)
    tabw = em.wv("cst")

    ww = {}

    def wload_one(g, dram, idx, slot):
        em.dma(g, f"w{slot}",
               wb[slot][:].rearrange("p (c m) -> p c m", m=P),
               dram[idx, :, :].rearrange("(c p) m -> p c m", p=P))
        ww[idx if dram is wq else 32 + idx] = em.wv(f"w{slot}")

    def init_wload(g):
        wload_one(g, wq, 0, 0)
        wload_one(g, wq, 1, 1)
    em.sec("gpsimd", init_wload)

    mm_pe = []
    rope_dv = []
    stw = {}

    for m in range(48):
        slot = m % 2
        is_q = m < 32
        is_k = 32 <= m < 40

        def mm(t, m=m, slot=slot, w=ww[m]):
            t.wait_ge(*w)
            t.wait_ge(dv, em.ln_done)
            if m >= 2:
                t.wait_ge(dv, rope_dv[m - 2])
            for c in range(NCHUNK):
                last = c == NCHUNK - 1
                i1 = t.matmul(out=pq[slot][:, 0:512],
                              lhsT=wb[slot][:, bass.ts(c, P)],
                              rhs=xnf[:, c * NTOK:c * NTOK + 512],
                              start=(c == 0), stop=last)
                i2 = t.matmul(out=pq[slot][:, 512:1024],
                              lhsT=wb[slot][:, bass.ts(c, P)],
                              rhs=xnf[:, c * NTOK + 512:c * NTOK + 1024],
                              start=(c == 0), stop=last)
                if last:
                    i1.then_inc(pe, 1)
                    i2.then_inc(pe, 1)
        em.sec("tensor", mm)
        mm_pe.append(em.inc("pe", 2))

        if m + 2 < 48:
            nm = m + 2

            def pre(g, nm=nm, slot=slot, w=mm_pe[m]):
                g.wait_ge(pe, w)
                if nm < 32:
                    wload_one(g, wq, nm, slot)
                else:
                    wload_one(g, wkv, nm - 32, slot)
            em.sec("gpsimd", pre)

        if is_q or is_k:
            c_, s_ = (cqs, sqs) if is_q else (cks, sks)

            def rope(v, m=m, slot=slot, c_=c_, s_=s_, tw=tabw,
                     sw=stw.get(m - 2)):
                v.wait_ge(pe, mm_pe[m])
                v.wait_ge(*tw)
                if sw:
                    v.wait_ge(*sw)
                p_ = pq[slot]
                ro = rout[m % 2]
                v.tensor_tensor(out=rt1[:], in0=p_[0:64, :], in1=c_[:], op=ALU.mult)
                v.tensor_tensor(out=rt2[:], in0=p_[64:128, :], in1=s_[:], op=ALU.mult)
                v.tensor_tensor(out=ro[0:64, :], in0=rt1[:], in1=rt2[:],
                                op=ALU.subtract)
                v.tensor_tensor(out=rt1[:], in0=p_[0:64, :], in1=s_[:], op=ALU.mult)
                v.tensor_tensor(out=rt2[:], in0=p_[64:128, :], in1=c_[:], op=ALU.mult)
                v.tensor_tensor(out=ro[64:128, :], in0=rt1[:], in1=rt2[:],
                                op=ALU.add).then_inc(dv, 1)
            em.sec("vector", rope)
            rope_dv.append(em.inc("dv"))

            def store(s, m=m):
                s.wait_ge(dv, rope_dv[m])
                dst = qf[bass.ts(m, P), :] if m < 32 else kf[bass.ts(m - 32, P), :]
                em.dma(s, f"st{m % 2}", dst, rout[m % 2][:])
            em.sec("sync", store)
            stw[m] = em.wv(f"st{m % 2}")
        else:
            def vcopy(v, m=m, slot=slot):
                v.wait_ge(pe, mm_pe[m])
                v.tensor_copy(out=vf[:, bass.ts(m - 40, NTOK)],
                              in_=pq[slot][:]).then_inc(dv, 1)
            em.sec("vector", vcopy)
            rope_dv.append(em.inc("dv"))
            stw[m] = stw.get(m - 1)

    v_done = rope_dv[-1]

    for kk in range(8):
        def trans(t, kk=kk, base=em.val("dv")):
            t.wait_ge(dv, v_done)
            t.wait_ge(dv, base)
            for tt in range(NT):
                if tt >= 2:
                    t.wait_ge(dv, base + tt - 1)
                t.transpose(out=pt[tt % 2][:, 0:P],
                            in_=vf[:, kk * NTOK + tt * P:kk * NTOK + (tt + 1) * P],
                            identity=idn[:]).then_inc(pe, 1)
        em.sec("tensor", trans)
        tb = em.val("pe")
        em.inc("pe", NT)

        def evac(v, kk=kk, tb=tb):
            for tt in range(NT):
                v.wait_ge(pe, tb + tt + 1)
                v.tensor_copy(out=vts[:, tt * KD + kk * P:tt * KD + (kk + 1) * P],
                              in_=pt[tt % 2][:, 0:P]).then_inc(dv, 1)
        em.sec("vector", evac)
        em.inc("dv", NT)

    def store_v(s, w=em.val("dv")):
        s.wait_ge(dv, w)
        em.dma(s, "stv", vt.rearrange("(tt p) d -> p tt d", p=P),
               vts[:].rearrange("p (tt d) -> p tt d", d=KD))
    em.sec("sync", store_v)

    def fin(s, w0=em.wv("st0"), w1=em.wv("st1"), wv_=em.wv("stv")):
        s.wait_ge(*w0)
        s.wait_ge(*w1)
        s.wait_ge(*wv_)
    em.sec("sync", fin)



_IDENT = np.eye(P, dtype=BF)


def _prep_qkv_l0(inputs):
    Xf = np.asarray(inputs["L0_attn_X"], np.float32)
    wq_t = _pretile(np.asarray(inputs["L0_attn_Wq"], np.float32))
    wkv = np.concatenate([np.asarray(inputs["L0_attn_Wk"], np.float32),
                          np.asarray(inputs["L0_attn_Wv"], np.float32)], axis=1)
    wkv_t = _pretile(wkv)
    maps = []
    for c in range(8):
        b, j = c // 4, c % 4
        rows = _own_rows(j)
        cqs, sqs = _rope_tables(rows, 1.0 / np.sqrt(DH))
        cks, sks = _rope_tables(rows, 1.0)
        maps.append({
            "x": np.ascontiguousarray(Xf[b][rows]),
            "wq": wq_t, "wkv": wkv_t,
            "cq": cqs, "sq": sqs, "ck": cks, "sk": sks,
            "ident": _IDENT,
        })
    return maps


def kernel(**inputs):
    raise NotImplementedError("built incrementally; see tests")



def build_attn():
    nc = bass.Bass("TRN2", target_bir_lowering=False, debug=False)
    qf = nc.dram_tensor("qf", [D, NTOK], BF16, kind="ExternalInput").ap()
    kf = nc.dram_tensor("kf", [KD, S], BF16, kind="ExternalInput").ap()
    vtok = nc.dram_tensor("vtok", [S, KD], BF16, kind="ExternalInput").ap()
    x = nc.dram_tensor("x", [NTOK, D], F32, kind="ExternalInput").ap()
    wo = nc.dram_tensor("wo", [D, D], F32, kind="ExternalInput").ap()
    masks = nc.dram_tensor("masks", [4, P, P], BF16, kind="ExternalInput").ap()
    ones = nc.dram_tensor("ones", [P, P], BF16, kind="ExternalInput").ap()
    y = nc.dram_tensor("y", [NTOK, D], F32, kind="ExternalOutput").ap()

    with ExitStack() as stk:
        em = Emit(nc, stk)
        ld, lg, pe, dv, ac, so = (em.sems[k] for k in
                                  ("ld", "lg", "pe", "dv", "ac", "so"))

        kbuf = [sb_tiles(nc, stk, f"kbuf{i}", [P, S], BF16) for i in range(2)]
        vbuf = [sb_tiles(nc, stk, f"vbuf{i}", [P, S], BF16) for i in range(2)]
        qbuf = [sb_tiles(nc, stk, f"qbuf{i}", [P, NTOK], BF16) for i in range(2)]
        sqk = sb_tiles(nc, stk, "sqk", [P, S], BF16)
        sqq = sb_tiles(nc, stk, "sqq", [P, NTOK], BF16)
        ptb = [sb_tiles(nc, stk, f"ptb{i}", [P, 512], BF16) for i in range(2)]
        attn = sb_tiles(nc, stk, "attn", [P, NQH * NTOK], BF16)
        mko = sb_tiles(nc, stk, "mko", [P, 5 * P], BF16)
        negb = sb_tiles(nc, stk, "negb", [P, NQH])
        nrm = sb_tiles(nc, stk, "nrm", [1, 16])
        nbf = sb_tiles(nc, stk, "nbf", [1, 16], BF16)
        rs = sb_tiles(nc, stk, "rs", [1, P])
        rsb = [sb_tiles(nc, stk, f"rsb{i}", [1, P], BF16) for i in range(2)]
        rb = [sb_tiles(nc, stk, f"rb{i}", [P, P]) for i in range(2)]
        xb = [sb_tiles(nc, stk, f"xb{i}", [P, 1024]) for i in range(2)]
        yb = [sb_tiles(nc, stk, f"yb{i}", [P, 1024]) for i in range(2)]
        wob = sb_tiles(nc, stk, "wob", [P, NCHUNK * 1024], BF16)

        py0 = stk.enter_context(nc.psum_tensor("py0", [P, 1024]))
        py1 = stk.enter_context(nc.psum_tensor("py1", [P, 1024]))
        pbc = stk.enter_context(nc.psum_tensor("pbc", [P, 1024]))
        pnq = stk.enter_context(nc.psum_tensor("pnq", [1, 1024]))

        def c_loads(s):
            em.dma(s, "cst", mko[:, 0:4 * P].rearrange("p (k q) -> p k q", q=P),
                   masks.rearrange("k p q -> p k q"))
            em.dma(s, "cst", mko[:, 4 * P:5 * P], ones)
        em.sec("sync", c_loads)
        cstw = em.wv("cst")

        kload, qload = {}, {}
        kfree_pe = {}
        qfree_pe = {}

        def load_kv(g, h):
            em.dma(g, f"kv{h % 2}", kbuf[h % 2][:], kf[bass.ts(h, P), :])
            em.dma(g, f"kv{h % 2}",
                   vbuf[h % 2][:].rearrange("p (t d) -> p t d", d=P),
                   vtok.rearrange("(t p) d -> p t d", p=P)[:, :, bass.ts(h, P)])
            kload[h] = em.wv(f"kv{h % 2}")

        def load_q(s, qg):
            em.dma(s, f"q{qg % 2}", qbuf[qg % 2][:], qf[bass.ts(qg, P), :])
            qload[qg] = em.wv(f"q{qg % 2}")

        em.sec("gpsimd", lambda g: (load_kv(g, 0), load_kv(g, 1)))
        em.sec("sync", lambda s: (load_q(s, 0), load_q(s, 1)))

        exp_ac = {}
        sc_bank_ac = [0, 0]
        ptb_free = [0, 0]
        mask_dv = {}
        pv_pe = {}
        evac_dv = {}
        rsb_dv = {}
        negb_ac = {}
        bc_par = [0]

        def kcol(h):
            return 1 + (h % 2)

        def emit_norm_k(h):
            def sq(sc, h=h, pw=em.val("pe"), w=kload[h]):
                sc.wait_ge(*w)
                sc.wait_ge(pe, pw)
                sc.activation(out=sqk[:], in_=kbuf[h % 2][:],
                              func=AF.Square).then_inc(ac, 1)
            em.sec("scalar", sq)
            sqa = em.inc("ac")
            for it in range(8):
                def mmk(t, it=it, sqa=sqa, w=em.val("dv"), cw=cstw):
                    t.wait_ge(ac, sqa)
                    t.wait_ge(*cw)
                    t.wait_ge(dv, w)
                    t.matmul(out=pnq[0:1, 0:512],
                             lhsT=mko[:, 4 * P:4 * P + 1],
                             rhs=sqk[:, it * 512:(it + 1) * 512],
                             start=True, stop=True).then_inc(pe, 1)
                em.sec("tensor", mmk)
                mk = em.inc("pe")

                def mx(v, it=it, mk=mk, h=h):
                    v.wait_ge(pe, mk)
                    v.reduce_max(out=nrm[0:1, 4:5], in_=pnq[0:1, 0:512],
                                 axis=AX.X).then_inc(dv, 1)
                    v.wait_ge(dv, em.inc("dv"))
                    kc = kcol(h)
                    if it == 0:
                        v.tensor_copy(out=nrm[0:1, kc:kc + 1],
                                      in_=nrm[0:1, 4:5]).then_inc(dv, 1)
                    else:
                        v.tensor_tensor(out=nrm[0:1, kc:kc + 1],
                                        in0=nrm[0:1, kc:kc + 1],
                                        in1=nrm[0:1, 4:5],
                                        op=ALU.max).then_inc(dv, 1)
                    em.inc("dv")
                    v.wait_ge(dv, em.val("dv"))
                em.sec("vector", mx)

        def emit_norm_q(qg):
            h = qg // 4

            def sq(sc, qg=qg, pw=em.val("pe"), w=qload[qg]):
                sc.wait_ge(*w)
                sc.wait_ge(pe, pw)
                sc.activation(out=sqq[:], in_=qbuf[qg % 2][:],
                              func=AF.Square).then_inc(ac, 1)
            em.sec("scalar", sq)
            sqa = em.inc("ac")

            def mmq(t, sqa=sqa, w=em.val("dv"), cw=cstw):
                t.wait_ge(ac, sqa)
                t.wait_ge(*cw)
                t.wait_ge(dv, w)
                t.matmul(out=pnq[0:1, 512:1024],
                         lhsT=mko[:, 4 * P:4 * P + 1],
                         rhs=sqq[:, 0:512], start=True, stop=False)
                t.matmul(out=pnq[0:1, 512:1024],
                         lhsT=mko[:, 4 * P:4 * P + 1],
                         rhs=sqq[:, 512:1024], start=False,
                         stop=True).then_inc(pe, 1)
            em.sec("tensor", mmq)
            mq = em.inc("pe")

            def chain(v, mq=mq, h=h):
                v.wait_ge(pe, mq)
                v.reduce_max(out=nrm[0:1, 5:6], in_=pnq[0:1, 512:1024],
                             axis=AX.X).then_inc(dv, 1)
                v.wait_ge(dv, em.inc("dv"))
                kc = kcol(h)
                v.tensor_tensor(out=nrm[0:1, 6:7], in0=nrm[0:1, 5:6],
                                in1=nrm[0:1, kc:kc + 1],
                                op=ALU.mult).then_inc(dv, 1)
                em.inc("dv")
                v.wait_ge(dv, em.val("dv"))
            em.sec("vector", chain)
            w2 = em.val("dv")

            def lnexp(sc, w2=w2):
                sc.wait_ge(dv, w2)
                sc.activation(out=nrm[0:1, 7:8], in_=nrm[0:1, 6:7],
                              func=AF.Ln, bias=0.0, scale=1.0).then_inc(ac, 1)
                sc.wait_ge(ac, em.inc("ac"))
                sc.activation(out=nrm[0:1, 8:9], in_=nrm[0:1, 7:8],
                              func=AF.Exp, bias=0.0, scale=0.5).then_inc(ac, 1)
            em.sec("scalar", lnexp)
            w3 = em.inc("ac")

            def neg(v, w3=w3, pw=em.val("pe")):
                v.wait_ge(ac, w3)
                v.wait_ge(pe, pw)
                v.tensor_scalar(out=nbf[0:1, 0:1], in0=nrm[0:1, 8:9],
                                scalar1=-1.0, scalar2=None,
                                op0=ALU.mult).then_inc(dv, 1)
            em.sec("vector", neg)
            nb = em.inc("dv")

            par = bc_par[0] % 2
            bc_par[0] += 1

            def bcmm(t, nb=nb, par=par):
                t.wait_ge(dv, nb)
                t.matmul(out=pbc[:, par * 512:par * 512 + 1],
                         lhsT=mko[0:1, 4 * P:5 * P],
                         rhs=nbf[0:1, 0:1],
                         start=True, stop=True).then_inc(pe, 1)
            em.sec("tensor", bcmm)
            bm = em.inc("pe")

            def cpy(sc, qg=qg, bm=bm, par=par):
                sc.wait_ge(pe, bm)
                sc.activation(out=negb[:, qg:qg + 1],
                              in_=pbc[:, par * 512:par * 512 + 1],
                              func=AF.Copy, bias=0.0, scale=1.0).then_inc(ac, 1)
            em.sec("scalar", cpy)
            negb_ac[qg] = em.inc("ac")

        emit_norm_k(0)
        emit_norm_q(0)
        for h in range(NKVH):
            for qh in range(4):
                qg = 4 * h + qh
                for i in range(NT):
                    ngroups = i + 1
                    par_s = i % 2
                    for g4 in range(ngroups):
                        sp = g4 % 2

                        def score(t, h=h, qg=qg, i=i, g4=g4, sp=sp,
                                  bw=sc_bank_ac[g4 % 2], kw=kload[h],
                                  qw=qload[qg]):
                            t.wait_ge(*kw)
                            t.wait_ge(*qw)
                            t.wait_ge(ac, bw)
                            for kk in range(4):
                                kt = 4 * g4 + kk
                                t.matmul(out=py0[:, sp * 512 + kk * P:
                                                 sp * 512 + (kk + 1) * P],
                                         lhsT=kbuf[h % 2][:, bass.ts(kt, P)],
                                         rhs=qbuf[qg % 2][:, bass.ts(i, P)],
                                         start=True, stop=True).then_inc(pe, 1)
                        em.sec("tensor", score)
                        sc_pe = em.inc("pe", 4)
                        kfree_pe[h] = sc_pe
                        qfree_pe[qg] = sc_pe

                        def expf(sc, qg=qg, sp=sp, sc_pe=sc_pe,
                                 pw=ptb_free[g4 % 2]):
                            sc.wait_ge(pe, sc_pe)
                            sc.wait_ge(ac, negb_ac[qg])
                            sc.wait_ge(pe, pw)
                            sc.activation(out=ptb[sp][:],
                                          in_=py0[:, sp * 512:sp * 512 + 512],
                                          func=AF.Exp, bias=negb[:, qg:qg + 1],
                                          scale=1.0).then_inc(ac, 1)
                        em.sec("scalar", expf)
                        exp_ac[(qg, i, g4)] = em.inc("ac")
                        sc_bank_ac[sp] = exp_ac[(qg, i, g4)]

                        is_diag = g4 == ngroups - 1
                        if is_diag:
                            def maskf(v, qg=qg, i=i, g4=g4, sp=sp):
                                v.wait_ge(ac, exp_ac[(qg, i, g4)])
                                for kk in range(4):
                                    v.tensor_tensor(
                                        out=ptb[sp][:, kk * P:(kk + 1) * P],
                                        in0=ptb[sp][:, kk * P:(kk + 1) * P],
                                        in1=mko[:, kk * P:(kk + 1) * P],
                                        op=ALU.mult).then_inc(dv, 1)
                            em.sec("vector", maskf)
                            mask_dv[(qg, i)] = em.inc("dv", 4)

                        def pvsum(t, qg=qg, h=h, i=i, g4=g4, sp=sp,
                                  par_s=par_s, is_diag=is_diag,
                                  ngroups=ngroups,
                                  bankw=evac_dv.get((qg, i - 2),
                                                    evac_dv.get((qg - 1,
                                                                 NT - 2 + (i % 2)),
                                                                0)) if g4 == 0 else 0):
                            t.wait_ge(ac, exp_ac[(qg, i, g4)])
                            if is_diag:
                                t.wait_ge(dv, mask_dv[(qg, i)])
                            if g4 == 0 and bankw:
                                t.wait_ge(dv, bankw)
                            first = g4 == 0
                            last = g4 == ngroups - 1
                            base = par_s * 512
                            for kk in range(4):
                                kt = 4 * g4 + kk
                                mm1 = t.matmul(out=py1[:, base:base + P],
                                               lhsT=vbuf[h % 2][:, bass.ts(kt, P)],
                                               rhs=ptb[sp][:, bass.ts(kk, P)],
                                               start=(first and kk == 0),
                                               stop=(last and kk == 3))
                                mm2 = t.matmul(out=py1[0:1, base + P:base + 2 * P],
                                               lhsT=mko[:, 4 * P:4 * P + 1],
                                               rhs=ptb[sp][:, bass.ts(kk, P)],
                                               start=False,
                                               stop=(last and kk == 3),
                                               skip_group_check=True)
                                if kk == 3:
                                    if last:
                                        mm1.then_inc(pe, 1)
                                        em.inc("pe")
                                    mm2.then_inc(pe, 1)
                        em.sec("tensor", pvsum)
                        pv_pe[(qg, i)] = em.inc("pe")
                        kfree_pe[h] = pv_pe[(qg, i)]
                        ptb_free[sp] = pv_pe[(qg, i)]

                    if i == 0:
                        if qg + 1 < NQH:
                            if (qg + 1) % 4 == 0:
                                emit_norm_k(h + 1)
                            emit_norm_q(qg + 1)

                    def ev1(v, qg=qg, i=i, par_s=par_s):
                        v.wait_ge(pe, pv_pe[(qg, i)])
                        base = par_s * 512
                        v.reciprocal(out=rs[0:1, :],
                                     in_=py1[0:1, base + P:base + 2 * P]
                                     ).then_inc(dv, 1)
                        v.wait_ge(dv, em.inc("dv"))
                        v.tensor_copy(out=rsb[par_s][0:1, :],
                                      in_=rs[0:1, :]).then_inc(dv, 1)
                    em.sec("vector", ev1)
                    rsb_dv[(qg, i)] = em.inc("dv")

                    par = bc_par[0] % 2
                    bc_par[0] += 1

                    def bc2(t, qg=qg, i=i, par=par, par_s=par_s):
                        t.wait_ge(dv, rsb_dv[(qg, i)])
                        t.matmul(out=pbc[:, par * 512:par * 512 + P],
                                 lhsT=mko[0:1, 4 * P:5 * P],
                                 rhs=rsb[par_s][0:1, :],
                                 start=True, stop=True).then_inc(pe, 1)
                    em.sec("tensor", bc2)
                    bcp = em.inc("pe")

                    def cp2(sc, bcp=bcp, par=par, par_s=par_s,
                            rw=evac_dv.get((qg, i - 2),
                                           evac_dv.get((qg - 1, NT - 2 + (i % 2)), 0))):
                        sc.wait_ge(pe, bcp)
                        if rw:
                            sc.wait_ge(dv, rw)
                        sc.activation(out=rb[par_s][:],
                                      in_=pbc[:, par * 512:par * 512 + P],
                                      func=AF.Copy, bias=0.0,
                                      scale=1.0).then_inc(ac, 1)
                    em.sec("scalar", cp2)
                    cpa = em.inc("ac")

                    def ev2(v, qg=qg, i=i, par_s=par_s, cpa=cpa):
                        v.wait_ge(ac, cpa)
                        base = par_s * 512
                        v.tensor_tensor(out=attn[:, qg * NTOK + i * P:
                                                 qg * NTOK + (i + 1) * P],
                                        in0=py1[:, base:base + P],
                                        in1=rb[par_s][:],
                                        op=ALU.mult).then_inc(dv, 1)
                    em.sec("vector", ev2)
                    evac_dv[(qg, i)] = em.inc("dv")

                if qg + 2 < NQH:
                    def pre_q(s, qg=qg, w=qfree_pe[qg]):
                        s.wait_ge(pe, w)
                        load_q(s, qg + 2)
                    em.sec("sync", pre_q)
            if h + 2 < NKVH:
                def pre_kv(g, h=h, w=kfree_pe[h]):
                    g.wait_ge(pe, w)
                    load_kv(g, h + 2)
                em.sec("gpsimd", pre_kv)

        attn_done = em.val("dv")

        woload = {}
        xload2 = {}
        ydone_dv = {}
        ystore_so = {}

        for oq in range(4):
            def wol(g, oq=oq, w=em.val("pe")):
                if oq == 0:
                    g.wait_ge(dv, attn_done)
                else:
                    g.wait_ge(pe, w)
                em.dma(g, "wol", wob[:].rearrange("p (c o) -> p c o", o=1024),
                       wo[:, oq * 1024:(oq + 1) * 1024].rearrange(
                           "(c p) o -> p c o", p=P))
            em.sec("gpsimd", wol)
            woload[oq] = em.wv("wol")

            for sl in range(NT):
                def xslice(s, oq=oq, sl=sl,
                           w=ydone_dv.get((oq, sl - 2),
                                          ydone_dv.get((oq - 1, sl + NT - 2), 0))):
                    if w:
                        s.wait_ge(dv, w)
                    em.dma(s, f"xs{sl % 2}", xb[sl % 2][:],
                           x[bass.ts(sl, P), oq * 1024:(oq + 1) * 1024])
                em.sec("sync", xslice)
                xload2[(oq, sl)] = em.wv(f"xs{sl % 2}")

                pyx = py0 if sl % 2 == 0 else py1

                def womm(t, oq=oq, sl=sl, pyx=pyx,
                         w=ydone_dv.get((oq, sl - 2),
                                        ydone_dv.get((oq - 1, sl + NT - 2), 0)),
                         ww_=woload[oq]):
                    t.wait_ge(*ww_)
                    if w:
                        t.wait_ge(dv, w)
                    for c in range(NCHUNK):
                        for half in range(2):
                            mm = t.matmul(
                                out=pyx[:, half * 512:half * 512 + 512],
                                lhsT=attn[:, c * NTOK + sl * P:
                                          c * NTOK + (sl + 1) * P],
                                rhs=wob[:, c * 1024 + half * 512:
                                        c * 1024 + (half + 1) * 512],
                                start=(c == 0), stop=(c == NCHUNK - 1))
                            if c == NCHUNK - 1:
                                mm.then_inc(pe, 1)
                em.sec("tensor", womm)
                wmm = em.inc("pe", 2)

                def yev(v, oq=oq, sl=sl, pyx=pyx, wmm=wmm,
                        sw=ystore_so.get((oq, sl - 2),
                                         ystore_so.get((oq - 1, sl + NT - 2))),
                        xw_=xload2[(oq, sl)]):
                    v.wait_ge(pe, wmm)
                    v.wait_ge(*xw_)
                    if sw:
                        v.wait_ge(*sw)
                    v.tensor_tensor(out=yb[sl % 2][:], in0=pyx[:],
                                    in1=xb[sl % 2][:],
                                    op=ALU.add).then_inc(dv, 1)
                em.sec("vector", yev)
                ydone_dv[(oq, sl)] = em.inc("dv")

                def yst(s, oq=oq, sl=sl, w=ydone_dv[(oq, sl)]):
                    s.wait_ge(dv, w)
                    em.dma(s, f"y{sl % 2}",
                           y[bass.ts(sl, P), oq * 1024:(oq + 1) * 1024],
                           yb[sl % 2][:])
                em.sec("sync", yst)
                ystore_so[(oq, sl)] = em.wv(f"y{sl % 2}")

        def fin(s, w0=em.wv("y0"), w1=em.wv("y1")):
            s.wait_ge(*w0)
            s.wait_ge(*w1)
        em.sec("sync", fin)
    return nc


def _prep_attn(inputs_x_own, qf, kf_all, vt_all, wo_natural, j):
    return {
        "qf": qf, "kf": kf_all, "vtok": vt_all, "x": inputs_x_own,
        "wo": wo_natural, "masks": _masks(j), "ones": np.ones((P, P), BF),
    }



def build_ffn0():
    nc = bass.Bass("TRN2", target_bir_lowering=False, debug=False)
    x = nc.dram_tensor("x", [NTOK, D], F32, kind="ExternalInput").ap()
    gate_p = nc.dram_tensor("gate_p", [2048, 11008], I32, kind="ExternalInput").ap()
    up_p = nc.dram_tensor("up_p", [2048, 11008], I32, kind="ExternalInput").ap()
    down_p = nc.dram_tensor("down_p", [5504, D], I32, kind="ExternalInput").ap()
    sg = nc.dram_tensor("sg", [64, 11008], F32, kind="ExternalInput").ap()
    zg = nc.dram_tensor("zg", [64, 11008], F32, kind="ExternalInput").ap()
    su = nc.dram_tensor("su", [64, 11008], F32, kind="ExternalInput").ap()
    zu = nc.dram_tensor("zu", [64, 11008], F32, kind="ExternalInput").ap()
    sd = nc.dram_tensor("sd", [43, 4, D], F32, kind="ExternalInput").ap()
    zd = nc.dram_tensor("zd", [43, 4, D], F32, kind="ExternalInput").ap()
    ind4 = nc.dram_tensor("ind4", [P, 4], BF16, kind="ExternalInput").ap()
    ind4t = nc.dram_tensor("ind4t", [4, P], BF16, kind="ExternalInput").ap()
    ident = nc.dram_tensor("ident", [P, P], BF16, kind="ExternalInput").ap()
    y = nc.dram_tensor("y", [D, NTOK], BF16, kind="ExternalOutput").ap()

    h_dram = nc.dram_tensor("h_dram", [2, 43, P, NTOK], BF16, kind="Internal").ap()
    wd_dram = nc.dram_tensor("wd_dram", [2, 43, P, D], BF16, kind="Internal").ap()
    hs_dram = nc.dram_tensor("hs_dram", [176, NTOK], F32, kind="Internal").ap()
    zsd_dram = nc.dram_tensor("zsd_dram", [176, D], F32, kind="Internal").ap()

    NIB = 43

    with ExitStack() as stk:
        em = Emit(nc, stk)
        ld, lg, pe, dv, ac, so = (em.sems[k] for k in
                                  ("ld", "lg", "pe", "dv", "ac", "so"))
        pg = stk.enter_context(nc.psum_tensor("pg", [P, 1024]))
        pu = stk.enter_context(nc.psum_tensor("pu", [P, 1024]))
        pH = stk.enter_context(nc.psum_tensor("pH", [4, 1024]))
        ptp = stk.enter_context(nc.psum_tensor("ptp", [P, 1024], BF16))
        pse = stk.enter_context(nc.psum_tensor("pse", [P, 512]))
        idn = sb_tiles(nc, stk, "idn", [P, P], BF16)
        i4 = sb_tiles(nc, stk, "i4", [P, 4], BF16)
        i4t = sb_tiles(nc, stk, "i4t", [4, P], BF16)
        epsv = sb_tiles(nc, stk, "epsv", [P, 1])

        def c_loads(s):
            em.dma(s, "cst", idn[:], ident)
            em.dma(s, "cst", i4[:], ind4)
            em.dma(s, "cst", i4t[:], ind4t)
        em.sec("sync", c_loads)
        cstw = em.wv("cst")

        mm_pe = {}
        hst_so = {}
        Hst_so = {}

        with ExitStack() as ska:
            xt = [sb_tiles(nc, ska, f"xt{i}", [P, D]) for i in range(2)]
            xnt = sb_tiles(nc, ska, "xnt", [P, D], BF16)
            xnE = sb_tiles(nc, ska, "xnE", [P, 16 * NTOK], BF16)
            xnO = sb_tiles(nc, ska, "xnO", [P, 16 * NTOK], BF16)
            bns = sb_tiles(nc, ska, "bns", [P, 48])
            bna = sb_tiles(nc, ska, "bna", [P, 2])
            lnv = sb_tiles(nc, ska, "lnv", [P, 1])
            negm = sb_tiles(nc, ska, "negm", [P, 1])
            rstd = sb_tiles(nc, ska, "rstd", [P, 1])
            xg = sb_tiles(nc, ska, "xg", [P, 64], BF16)
            Gf = sb_tiles(nc, ska, "Gf", [64, NTOK], BF16)
            sgl = [sb_tiles(nc, ska, f"sgl{i}", [64, 512], BF16) for i in range(2)]
            zgl = [sb_tiles(nc, ska, f"zgl{i}", [64, 512], BF16) for i in range(2)]
            zsl = [sb_tiles(nc, ska, f"zsl{i}", [64, 512], BF16) for i in range(2)]
            sesl = sb_tiles(nc, ska, "sesl", [P, 512], BF16)
            gwb = [sb_tiles(nc, ska, f"gwb{i}", [P, 4096], BF16) for i in range(2)]
            uwb = [sb_tiles(nc, ska, f"uwb{i}", [P, 4096], BF16) for i in range(2)]
            t1 = sb_tiles(nc, ska, "t1", [P, 2048])
            him = sb_tiles(nc, ska, "him", [P, 2048], BF16)
            Hg = sb_tiles(nc, ska, "Hg", [P, 4096], BF16)
            Lg = sb_tiles(nc, ska, "Lg", [P, 4096], BF16)
            Hu = sb_tiles(nc, ska, "Hu", [P, 4096], BF16)
            Lu = sb_tiles(nc, ska, "Lu", [P, 4096], BF16)
            sgb = sb_tiles(nc, ska, "sgb", [P, NTOK], BF16)
            hb = sb_tiles(nc, ska, "hb", [P, NTOK], BF16)
            htmpb = sb_tiles(nc, ska, "htmpb", [4, NTOK])

            def zp0(v):
                v.memset(epsv[:], EPS)
                v.memset(htmpb[:], 0.0)
                v.memset(t1[0:4, :], 0.0).then_inc(dv, 1)
            em.sec("vector", zp0)
            zp_dv = em.inc("dv")

            def zpad_st(s, w=zp_dv):
                s.wait_ge(dv, w)
                em.dma(s, "zp", hs_dram[172:176, :], htmpb[:])
                em.dma(s, "zp", zsd_dram[172:176, 0:2048], t1[0:4, :])
                em.dma(s, "zp", zsd_dram[172:176, 2048:4096], t1[0:4, :])
            em.sec("sync", zpad_st)
            zpad_so = em.wv("zp")

            xload = []

            def load_x(s, tt):
                em.dma(s, f"x{tt % 2}", xt[tt % 2][:], x[bass.ts(tt, P), :])
                xload.append(em.wv(f"x{tt % 2}"))

            em.sec("sync", lambda s: (load_x(s, 0), load_x(s, 1)))

            for tt in range(NT):
                buf = xt[tt % 2]

                def stats(v, tt=tt, buf=buf, w=xload[tt]):
                    v.wait_ge(*w)
                    for hh in range(8):
                        i_ = v.bn_stats(out=bns[:, 6 * hh:6 * hh + 6],
                                        in_=buf[:, 512 * hh:512 * hh + 512])
                    i_.then_inc(dv, 1)
                    v.wait_ge(dv, em.inc("dv"))
                    v.bn_aggr(out=bna[:], in_=bns[:]).then_inc(dv, 1)
                    v.wait_ge(dv, em.inc("dv"))
                    v.tensor_scalar(out=negm[:], in0=bna[:, 0:1], scalar1=-1.0,
                                    scalar2=None, op0=ALU.mult).then_inc(dv, 1)
                em.sec("vector", stats)
                sdv = em.inc("dv")

                def rstd_f(sc, sdv=sdv):
                    sc.wait_ge(dv, sdv)
                    sc.activation(out=lnv[:], in_=bna[:, 1:2], func=AF.Ln,
                                  bias=epsv[:, 0:1], scale=1.0).then_inc(ac, 1)
                    sc.wait_ge(ac, em.inc("ac"))
                    sc.activation(out=rstd[:], in_=lnv[:], func=AF.Exp,
                                  bias=0.0, scale=-0.5).then_inc(ac, 1)
                em.sec("scalar", rstd_f)
                sac = em.inc("ac")

                def norm(v, buf=buf, sac=sac):
                    v.wait_ge(ac, sac)
                    v.tensor_scalar(out=xnt[:], in0=buf[:], scalar1=negm[:, 0:1],
                                    scalar2=rstd[:, 0:1], op0=ALU.add,
                                    op1=ALU.mult).then_inc(dv, 1)
                    v.wait_ge(dv, em.inc("dv"))
                    v.reduce_sum(out=t1[:, 0:64],
                                 in_=xnt[:].rearrange("p (g k) -> p g k", g=64),
                                 axis=AX.X).then_inc(dv, 1)
                    v.wait_ge(dv, em.inc("dv"))
                    v.tensor_copy(out=xg[:], in_=t1[:, 0:64]).then_inc(dv, 1)
                em.sec("vector", norm)
                norm_dv = em.inc("dv")

                if tt + 2 < NT:
                    def prefetch(s, tt=tt, w=norm_dv):
                        s.wait_ge(dv, w)
                        load_x(s, tt + 2)
                    em.sec("sync", prefetch)

                def gtr(t, w=norm_dv, cw=cstw):
                    t.wait_ge(*cw)
                    t.wait_ge(dv, w)
                    t.transpose(out=ptp[0:64, 0:P], in_=xg[:],
                                identity=idn[:]).then_inc(pe, 1)
                em.sec("tensor", gtr)
                gpe = em.inc("pe")

                def gcp(v, tt=tt, gpe=gpe):
                    v.wait_ge(pe, gpe)
                    v.tensor_copy(out=Gf[:, bass.ts(tt, P)],
                                  in_=ptp[0:64, 0:P]).then_inc(dv, 1)
                em.sec("vector", gcp)
                gcp_dv = em.inc("dv")

                for c in range(16):
                    def ptr(t, c=c, w=gcp_dv, base=em.val("dv")):
                        t.wait_ge(dv, w if c == 0 else base)
                        for hh in range(2):
                            for par in range(2):
                                off = 4 * c + 2 * hh + par
                                t.transpose(
                                    out=ptp[0:64, (2 * hh + par) * P:
                                            (2 * hh + par + 1) * P],
                                    in_=xnt[:, off:D:64],
                                    identity=idn[:]).then_inc(pe, 1)
                    em.sec("tensor", ptr)
                    tb = em.val("pe")
                    em.inc("pe", 4)

                    def pcp(v, tt=tt, c=c, tb=tb):
                        v.wait_ge(pe, tb + 4)
                        for hh in range(2):
                            for par in range(2):
                                dst = xnE if par == 0 else xnO
                                i_ = v.tensor_copy(
                                    out=dst[64 * hh:64 * hh + 64,
                                            c * NTOK + tt * P:
                                            c * NTOK + (tt + 1) * P],
                                    in_=ptp[0:64, (2 * hh + par) * P:
                                            (2 * hh + par + 1) * P])
                        i_.then_inc(dv, 1)
                    em.sec("vector", pcp)
                    em.inc("dv")
            ln_done = em.val("dv")

            wload = {}
            sload = {}

            def wl(g, ib):
                sn = f"w{ib % 2}"
                em.dma(g, sn, gwb[ib % 2][:].rearrange("p (c i) -> p c i", i=256),
                       gate_p[:, 256 * ib:256 * (ib + 1)].rearrange(
                           "(c p) i -> p c i", p=P))
                em.dma(g, sn, uwb[ib % 2][:].rearrange("p (c i) -> p c i", i=256),
                       up_p[:, 256 * ib:256 * (ib + 1)].rearrange(
                           "(c p) i -> p c i", p=P))
                sl = sgl[ib % 2]
                zl_ = zgl[ib % 2]
                em.dma(g, sn, sl[:, 0:256], sg[:, 256 * ib:256 * (ib + 1)])
                em.dma(g, sn, sl[:, 256:512], su[:, 256 * ib:256 * (ib + 1)])
                em.dma(g, sn, zl_[:, 0:256], zg[:, 256 * ib:256 * (ib + 1)])
                em.dma(g, sn, zl_[:, 256:512], zu[:, 256 * ib:256 * (ib + 1)])
                wload[ib] = em.wv(sn)

            em.sec("gpsimd", lambda g: (wl(g, 0), wl(g, 1)))

            deq_dv = {}
            zsl_dv = {}
            ev_dv = {}
            ev_ac = {}

            for ib in range(NIB):
                def zslf(v, ib=ib, w=wload[ib]):
                    v.wait_ge(*w)
                    v.scalar_tensor_tensor(out=zsl[ib % 2][:],
                                           in0=zgl[ib % 2][:], scalar=-1.0,
                                           in1=sgl[ib % 2][:], op0=ALU.mult,
                                           op1=ALU.mult).then_inc(dv, 1)
                em.sec("vector", zslf)
                zsl_dv[ib] = em.inc("dv")

                for wi in range(2):
                    wb_ = gwb if wi == 0 else uwb
                    Hd = Hg if wi == 0 else Hu
                    Ld = Lg if wi == 0 else Lu

                    def sesl_f(v, ib=ib, wi=wi, w=wload[ib],
                               mmw=mm_pe.get(ib - 1, 0)):
                        v.wait_ge(*w)
                        if mmw:
                            v.wait_ge(pe, mmw)
                        src = sgl[ib % 2][:, wi * 256:wi * 256 + 256]
                        v.tensor_copy(out=sesl[0:64, 0:256], in_=src)
                        v.tensor_copy(out=sesl[64:128, 0:256],
                                      in_=src).then_inc(dv, 1)
                    em.sec("vector", sesl_f)
                    sw = em.inc("dv")

                    for hv in range(2):
                        HV = slice(hv * 2048, hv * 2048 + 2048)

                        def deq(v, ib=ib, wb_=wb_, sw=sw, HV=HV):
                            v.wait_ge(dv, sw)
                            v.tensor_scalar(out=t1[:], in0=wb_[ib % 2][:, HV],
                                            scalar1=1.0 / 16.0,
                                            scalar2=-0.46875, op0=ALU.mult,
                                            op1=ALU.add).then_inc(dv, 1)
                        em.sec("vector", deq)
                        d1 = em.inc("dv")

                        def deqa(sc, d1=d1):
                            sc.wait_ge(dv, d1)
                            sc.activation(out=him[:], in_=t1[:], func=AF.Copy,
                                          bias=129.0, scale=1.0).then_inc(ac, 1)
                        em.sec("scalar", deqa)
                        d2 = em.inc("ac")

                        def deq2(v, ib=ib, wb_=wb_, Hd=Hd, Ld=Ld, d2=d2,
                                 hv=hv, HV=HV):
                            v.wait_ge(ac, d2)
                            if ib == 0:
                                v.wait_ge(*zpad_so)
                            se3 = sesl[:, 0:256].unsqueeze(1).broadcast_to(
                                (P, 8, 256))
                            v.scalar_tensor_tensor(
                                out=Hd[:, HV].rearrange("p (c i) -> p c i", i=256),
                                in0=him[:].rearrange("p (c i) -> p c i", i=256),
                                scalar=-129.0, in1=se3, op0=ALU.add,
                                op1=ALU.mult)
                            v.scalar_tensor_tensor(out=t1[:], in0=him[:],
                                                   scalar=-16.0,
                                                   in1=wb_[ib % 2][:, HV],
                                                   op0=ALU.mult, op1=ALU.add)
                            v.scalar_tensor_tensor(
                                out=Ld[:, HV].rearrange("p (c i) -> p c i", i=256),
                                in0=t1[:].rearrange("p (c i) -> p c i", i=256),
                                scalar=2064.0, in1=se3, op0=ALU.add,
                                op1=ALU.mult).then_inc(dv, 1)
                        em.sec("vector", deq2)
                        deq_dv[(ib, wi)] = em.inc("dv")

                if ib + 2 < NIB:
                    def wpre(g, w=deq_dv[(ib, 1)]):
                        g.wait_ge(dv, w)
                        wl(g, ib + 2)
                    em.sec("gpsimd", wpre)

                for par in range(2):
                    for wi in range(2):
                        Hd = Hg if wi == 0 else Hu
                        Ld = Lg if wi == 0 else Lu
                        pdst = pg if wi == 0 else pu

                        def mm(t, ib=ib, par=par, wi=wi, Hd=Hd, Ld=Ld,
                               pdst=pdst, dw=deq_dv[(ib, wi)], lnw=ln_done,
                               zw=zsl_dv[ib],
                               evw=ev_dv.get((ib, par - 1),
                                             ev_dv.get((ib - 1, 1), 0)),
                               eaw=ev_ac.get((ib, par - 1),
                                             ev_ac.get((ib - 1, 1), 0))):
                            t.wait_ge(dv, dw)
                            t.wait_ge(dv, lnw)
                            t.wait_ge(dv, zw)
                            if evw:
                                t.wait_ge(dv, evw)
                            if eaw:
                                t.wait_ge(ac, eaw)
                            for c in range(16):
                                for HL, xn_ in ((Hd, xnE), (Ld, xnO)):
                                    lhsT = HL[:, c * 256 + par:(c + 1) * 256:2]
                                    st_ = (c == 0 and HL is Hd)
                                    t.matmul(out=pdst[:, 0:512], lhsT=lhsT,
                                             rhs=xn_[:, c * NTOK:c * NTOK + 512],
                                             start=st_, stop=False)
                                    t.matmul(out=pdst[:, 512:1024], lhsT=lhsT,
                                             rhs=xn_[:, c * NTOK + 512:
                                                     c * NTOK + 1024],
                                             start=st_, stop=False)
                            zl2 = zsl[ib % 2][:, wi * 256 + par:
                                              wi * 256 + 256:2]
                            if True:
                                t.matmul(out=pdst[:, 0:512], lhsT=zl2,
                                         rhs=Gf[:, 0:512], start=False,
                                         stop=True).then_inc(pe, 1)
                                t.matmul(out=pdst[:, 512:1024], lhsT=zl2,
                                         rhs=Gf[:, 512:1024], start=False,
                                         stop=True).then_inc(pe, 1)
                        em.sec("tensor", mm)
                        em.inc("pe", 2)
                    mm_pe[ib] = em.val("pe")
                    mmw = mm_pe[ib]

                    def silu_f(sc, w=mmw):
                        sc.wait_ge(pe, w)
                        sc.activation(out=sgb[:], in_=pg[:],
                                      func=AF.Silu).then_inc(ac, 1)
                    em.sec("scalar", silu_f)
                    sia = em.inc("ac")

                    def hmul(v, sia=sia, sw=hst_so.get((ib, par - 1),
                                                       hst_so.get((ib - 1, 1)))):
                        v.wait_ge(ac, sia)
                        if sw:
                            v.wait_ge(*sw)
                        v.tensor_tensor(out=hb[:], in0=sgb[:], in1=pu[:],
                                        op=ALU.mult).then_inc(dv, 1)
                    em.sec("vector", hmul)
                    ev_dv[(ib, par)] = em.inc("dv")
                    ev_ac[(ib, par)] = sia

                    if DBG_NO_HSUM:
                        hst2 = None

                    def hsum(t, ib=ib, par=par, w=ev_dv[(ib, par)],
                             hw=Hst_so.get(ib - 1)):
                        if DBG_NO_HSUM:
                            return
                        t.wait_ge(dv, w)
                        if par == 0 and hw:
                            t.wait_ge(*hw)
                        t.matmul(out=pH[:, 0:512], lhsT=i4[:],
                                 rhs=hb[:, 0:512], start=(par == 0),
                                 stop=(par == 1)).then_inc(pe, 1)
                        t.matmul(out=pH[:, 512:1024], lhsT=i4[:],
                                 rhs=hb[:, 512:1024], start=(par == 0),
                                 stop=(par == 1)).then_inc(pe, 1)
                    em.sec("tensor", hsum)
                    hsw = em.inc("pe", 2)

                    def hstore(s, ib=ib, par=par, w=ev_dv[(ib, par)]):
                        s.wait_ge(dv, w)
                        em.dma(s, "hst", h_dram[par, ib], hb[:])
                    em.sec("sync", hstore)
                    hst_so[(ib, par)] = em.wv("hst")

                    if par == 1:
                        def Hev(sc, w=hsw, zw=zpad_so):
                            if DBG_NO_HSUM:
                                return
                            sc.wait_ge(pe, w)
                            if zw:
                                sc.wait_ge(*zw)
                            sc.activation(out=htmpb[:], in_=pH[:],
                                          func=AF.Copy, bias=0.0,
                                          scale=1.0).then_inc(ac, 1)
                        em.sec("scalar", Hev)
                        hea = em.inc("ac")

                        def Hst(s, ib=ib, w=hea):
                            s.wait_ge(ac, w)
                            em.dma(s, "Hs", hs_dram[4 * ib:4 * ib + 4, :],
                                   htmpb[:])
                        em.sec("sync", Hst)
                        Hst_so[ib] = em.wv("Hs")

            gate_done_pe = em.val("pe")

        with ExitStack() as skb:
            dwb = [sb_tiles(nc, skb, f"dwb{i}", [P, D], BF16) for i in range(2)]
            sdsl = [sb_tiles(nc, skb, f"sdsl{i}", [4, D], BF16) for i in range(2)]
            zdsl = sb_tiles(nc, skb, "zdsl", [4, D])
            zdsl2 = sb_tiles(nc, skb, "zdsl2", [4, D])
            zsdsl = sb_tiles(nc, skb, "zsdsl", [4, D])
            sed = sb_tiles(nc, skb, "sed", [P, D], BF16)
            t1b = sb_tiles(nc, skb, "t1b", [P, D])
            himb = sb_tiles(nc, skb, "himb", [P, D], BF16)
            HdB = sb_tiles(nc, skb, "HdB", [P, D], BF16)
            LdB = sb_tiles(nc, skb, "LdB", [P, D], BF16)

            dload = {}
            zload = {}

            def dl(g, cd):
                sn = f"d{cd % 2}"
                em.dma(g, sn, dwb[cd % 2][:], down_p[bass.ts(cd, P), :])
                em.dma(g, sn, sdsl[cd % 2][:], sd[cd])
                dload[cd] = em.wv(sn)

            def dzl(g, cd):
                em.dma(g, "zl", zdsl[:], zd[cd])
                zload[cd] = em.wv("zl")

            def init_d(g, w=gate_done_pe, dw=em.val("dv"), aw=em.val("ac"),
                       h1=em.wv("hst"), h2=em.wv("Hs"), zp_=em.wv("zp")):
                g.wait_ge(pe, w)
                g.wait_ge(dv, dw)
                g.wait_ge(ac, aw)
                g.wait_ge(*h1)
                g.wait_ge(*h2)
                g.wait_ge(*zp_)
                dl(g, 0)
                dzl(g, 0)
                dl(g, 1)
            em.sec("gpsimd", init_d)

            dst_so = {}
            sed_ac = {}
            ddq = {}

            for cd in range(43):
                def zss(v, cd=cd, w=zload[cd], pw=dload[cd]):
                    v.wait_ge(*w)
                    v.wait_ge(*pw)
                    v.tensor_scalar(out=zdsl2[:], in0=zdsl[:], scalar1=-1.0,
                                    scalar2=8.0, op0=ALU.mult,
                                    op1=ALU.add).then_inc(dv, 1)
                    v.wait_ge(dv, em.inc("dv"))
                    v.tensor_tensor(out=zsdsl[:], in0=zdsl2[:],
                                    in1=sdsl[cd % 2][:],
                                    op=ALU.mult).then_inc(dv, 1)
                em.sec("vector", zss)
                zw = em.inc("dv")

                def zst(s, cd=cd, w=zw):
                    s.wait_ge(dv, w)
                    em.dma(s, "zs", zsd_dram[4 * cd:4 * cd + 4, :], zsdsl[:])
                em.sec("sync", zst)
                zso = em.wv("zs")

                if cd + 1 < 43:
                    def dz2(g, cd=cd, w=zso):
                        g.wait_ge(*w)
                        dzl(g, cd + 1)
                    em.sec("gpsimd", dz2)

                for hf in range(8):
                    def bch(t, cd=cd, hf=hf, w=dload[cd],
                            aw=sed_ac.get((cd, hf - 1),
                                          sed_ac.get((cd - 1, 7), 0)),
                            dq=ddq.get(cd - 1, 0), cw=cstw):
                        t.wait_ge(*w)
                        t.wait_ge(*cw)
                        if aw:
                            t.wait_ge(ac, aw)
                        if hf == 0 and dq:
                            t.wait_ge(dv, dq)
                        t.matmul(out=pse[:, 0:512],
                                 lhsT=i4t[:],
                                 rhs=sdsl[cd % 2][0:4,
                                                  hf * 512:hf * 512 + 512],
                                 start=True, stop=True).then_inc(pe, 1)
                    em.sec("tensor", bch)
                    bw = em.inc("pe")

                    def sevac(sc, cd=cd, hf=hf, w=bw):
                        sc.wait_ge(pe, w)
                        sc.activation(out=sed[:, hf * 512:hf * 512 + 512],
                                      in_=pse[:, 0:512],
                                      func=AF.Copy, bias=0.0,
                                      scale=1.0).then_inc(ac, 1)
                    em.sec("scalar", sevac)
                    sed_ac[(cd, hf)] = em.inc("ac")

                def ddeq(v, cd=cd, w=dload[cd], stw=dst_so.get(cd - 1)):
                    v.wait_ge(*w)
                    if stw:
                        v.wait_ge(*stw)
                    v.tensor_scalar(out=t1b[:], in0=dwb[cd % 2][:],
                                    scalar1=1.0 / 16.0, scalar2=-0.46875,
                                    op0=ALU.mult, op1=ALU.add).then_inc(dv, 1)
                em.sec("vector", ddeq)
                dd1 = em.inc("dv")

                def ddeqa(sc, dd1=dd1):
                    sc.wait_ge(dv, dd1)
                    sc.activation(out=himb[:], in_=t1b[:], func=AF.Copy,
                                  bias=129.0, scale=1.0).then_inc(ac, 1)
                em.sec("scalar", ddeqa)
                dd2 = em.inc("ac")

                def ddeq2(v, cd=cd, dd2=dd2, saw=sed_ac[(cd, 7)]):
                    v.wait_ge(ac, dd2)
                    v.wait_ge(ac, saw)
                    v.scalar_tensor_tensor(out=HdB[:], in0=himb[:],
                                           scalar=-137.0, in1=sed[:],
                                           op0=ALU.add, op1=ALU.mult)
                    v.scalar_tensor_tensor(out=t1b[:], in0=himb[:],
                                           scalar=-16.0, in1=dwb[cd % 2][:],
                                           op0=ALU.mult, op1=ALU.add)
                    v.scalar_tensor_tensor(out=LdB[:], in0=t1b[:],
                                           scalar=2056.0, in1=sed[:],
                                           op0=ALU.add,
                                           op1=ALU.mult).then_inc(dv, 1)
                em.sec("vector", ddeq2)
                dqw = em.inc("dv")
                ddq[cd] = dqw

                def dstore(s, cd=cd, w=dqw):
                    s.wait_ge(dv, w)
                    em.dma(s, "ws", wd_dram[0, cd], HdB[:])
                    em.dma(s, "ws", wd_dram[1, cd], LdB[:])
                em.sec("sync", dstore)
                dst_so[cd] = em.wv("ws")

                if cd + 2 < 43:
                    def dpre(g, cd=cd, w=dqw):
                        g.wait_ge(dv, w)
                        dl(g, cd + 2)
                    em.sec("gpsimd", dpre)
            down_deq_ws = em.wv("ws")
            down_deq_zs = em.wv("zs")

        with ExitStack() as skc:
            hres = sb_tiles(nc, skc, "hres", [P, 86 * 512], BF16)
            Hh = sb_tiles(nc, skc, "Hh", [88, 2 * 512])
            zsdh = sb_tiles(nc, skc, "zsdh", [88, 2 * D])
            dwc = [sb_tiles(nc, skc, f"dwc{i}", [P, 86 * P], BF16)
                   for i in range(2)]
            yo = [sb_tiles(nc, skc, f"yo{i}", [P, 512], BF16) for i in range(2)]

            def zsdh_load(g, w1=down_deq_ws, w2=down_deq_zs):
                g.wait_ge(*w1)
                g.wait_ge(*w2)
                em.dma(g, "zh", zsdh[:].rearrange("p (a n) -> p a n", n=D),
                       zsd_dram.rearrange("(a p) n -> p a n", a=2))
            em.sec("gpsimd", zsdh_load)
            zh_ld = em.wv("zh")

            hload = {}

            def h_load(g, nh):
                for a_ in range(2):
                    em.dma(g, "hl",
                           hres[:].rearrange("p (c n) -> p c n", n=512)[:, a_::2, :],
                           h_dram[a_].rearrange("c p n -> p c n")
                           [:, :, nh * 512:nh * 512 + 512])
                em.dma(g, "hl", Hh[:].rearrange("p (a n) -> p a n", n=512),
                       hs_dram.rearrange("(a p) n -> p a n", a=2)
                       [:, :, nh * 512:nh * 512 + 512])
                hload[nh] = em.wv("hl")

            wcl = {}

            def wc_load(g, ot):
                sn = f"wc{ot % 2}"
                for a_ in range(2):
                    em.dma(g, sn,
                           dwc[ot % 2][:].rearrange("p (c o) -> p c o",
                                                    o=P)[:, a_::2, :],
                           wd_dram[a_].rearrange("c p o -> p c o")
                           [:, :, bass.ts(ot, P)])
                wcl[ot] = em.wv(sn)

            def wc0(g, w1=down_deq_ws, hw=em.wv("hst"), Hw=em.wv("Hs")):
                g.wait_ge(*w1)
                g.wait_ge(*hw)
                g.wait_ge(*Hw)
                h_load(g, 0)
                wc_load(g, 0)
                wc_load(g, 1)
            em.sec("gpsimd", wc0)

            yev_dv = {}
            ymm_pe = {}
            yst_so = {}
            for nh in range(2):
                for oti in range(32):
                    ot = oti if nh == 0 else 31 - oti
                    key = (nh, ot)
                    pdst = pg if oti % 2 == 0 else pu

                    def ymm(t, nh=nh, ot=ot, oti=oti, pdst=pdst,
                            w=wcl[ot], bw=zh_ld, hw2=hload[nh]):
                        t.wait_ge(*w)
                        t.wait_ge(*bw)
                        t.wait_ge(*hw2)
                        kprev = (nh, ot - 2) if nh == 0 else (nh, ot + 2)
                        if kprev in yev_dv:
                            t.wait_ge(dv, yev_dv[kprev])
                        elif nh == 1 and (0, ot) in yev_dv and oti < 2:
                            t.wait_ge(dv, yev_dv[(0, ot)])
                        for cc in range(86):
                            t.matmul(out=pdst[:, 0:512],
                                     lhsT=dwc[ot % 2][:, bass.ts(cc, P)],
                                     rhs=hres[:, cc * 512:(cc + 1) * 512],
                                     start=(cc == 0), stop=False)
                        for a in range(2):
                            mmc = t.matmul(out=pdst[:, 0:512],
                                           lhsT=zsdh[:, a * D + ot * P:
                                                     a * D + (ot + 1) * P],
                                           rhs=Hh[:, a * 512:(a + 1) * 512],
                                           start=False, stop=(a == 1))
                            if a == 1:
                                mmc.then_inc(pe, 1)
                    em.sec("tensor", ymm)
                    ymm_pe[key] = em.inc("pe")

                    nxt = ot + 2 if nh == 0 else ot - 2
                    if 0 <= nxt < 32 and (nh == 0 or oti < 30):
                        def wcp(g, nxt=nxt, w=ymm_pe[key]):
                            g.wait_ge(pe, w)
                            wc_load(g, nxt)
                        em.sec("gpsimd", wcp)
                    if nh == 0 and oti == 30:
                        def hl2(g, w=ymm_pe[key]):
                            g.wait_ge(pe, w)
                        em.sec("gpsimd", hl2)

                    def yev(v, key=key, pdst=pdst, w=ymm_pe[key],
                            sw=yst_so.get((key[0], key[1] - 2 if nh == 0
                                           else key[1] + 2))):
                        v.wait_ge(pe, w)
                        if sw:
                            v.wait_ge(*sw)
                        v.tensor_copy(out=yo[key[1] % 2][:],
                                      in_=pdst[:, 0:512]).then_inc(dv, 1)
                    em.sec("vector", yev)
                    yev_dv[key] = em.inc("dv")

                    def yst(s, nh=nh, ot=ot, w=yev_dv[key]):
                        s.wait_ge(dv, w)
                        em.dma(s, f"ys{ot % 2}",
                               y[bass.ts(ot, P), nh * 512:nh * 512 + 512],
                               yo[ot % 2][:])
                    em.sec("sync", yst)
                    yst_so[key] = em.wv(f"ys{ot % 2}")

                if nh == 0:
                    def hswap(g, w=em.val("pe")):
                        g.wait_ge(pe, w)
                        h_load(g, 1)
                    em.sec("gpsimd", hswap)

            def fin(s, w0=em.wv("ys0"), w1=em.wv("ys1")):
                s.wait_ge(*w0)
                s.wait_ge(*w1)
            em.sec("sync", fin)
    return nc



def build_qkv_l1():
    nc = bass.Bass("TRN2", target_bir_lowering=False, debug=False)
    x = nc.dram_tensor("x", [NTOK, D], F32, kind="ExternalInput").ap()
    f0 = nc.dram_tensor("f0", [NTOK, D], BF16, kind="ExternalInput").ap()
    wq1 = nc.dram_tensor("wq1", [32, D, P], F32, kind="ExternalInput").ap()
    wkr = nc.dram_tensor("wkr", [4, D, P], F32, kind="ExternalInput").ap()
    wdk = nc.dram_tensor("wdk", [4, D, P], F32, kind="ExternalInput").ap()
    wupk = nc.dram_tensor("wupk", [4, 512, P], F32, kind="ExternalInput").ap()
    wupv = nc.dram_tensor("wupv", [8, 512, P], F32, kind="ExternalInput").ap()
    cq = nc.dram_tensor("cq", [64, NTOK], F32, kind="ExternalInput").ap()
    sq = nc.dram_tensor("sq", [64, NTOK], F32, kind="ExternalInput").ap()
    ck = nc.dram_tensor("ck", [64, NTOK], F32, kind="ExternalInput").ap()
    sk = nc.dram_tensor("sk", [64, NTOK], F32, kind="ExternalInput").ap()
    ident = nc.dram_tensor("ident", [P, P], BF16, kind="ExternalInput").ap()
    x2 = nc.dram_tensor("x2", [NTOK, D], F32, kind="ExternalOutput").ap()
    qf = nc.dram_tensor("qf", [D, NTOK], BF16, kind="ExternalOutput").ap()
    kf = nc.dram_tensor("kf", [KD, NTOK], BF16, kind="ExternalOutput").ap()
    vt = nc.dram_tensor("vt", [NTOK, KD], BF16, kind="ExternalOutput").ap()

    with ExitStack() as stk:
        em = Emit(nc, stk)
        pe, dv, ac = em.sems["pe"], em.sems["dv"], em.sems["ac"]

        xt = [sb_tiles(nc, stk, f"xt{i}", [P, D]) for i in range(2)]
        ft = [sb_tiles(nc, stk, f"ft{i}", [P, D], BF16) for i in range(2)]
        xnf = sb_tiles(nc, stk, "xnf", [P, NCHUNK * NTOK], BF16)
        xnt = sb_tiles(nc, stk, "xnt", [P, D], BF16)
        idn = sb_tiles(nc, stk, "idn", [P, P], BF16)
        bns = sb_tiles(nc, stk, "bns", [P, 48])
        bna = sb_tiles(nc, stk, "bna", [P, 2])
        lnv = sb_tiles(nc, stk, "lnv", [P, 1])
        negm = sb_tiles(nc, stk, "negm", [P, 1])
        rstd = sb_tiles(nc, stk, "rstd", [P, 1])
        epsv = sb_tiles(nc, stk, "epsv", [P, 1])
        wb = [sb_tiles(nc, stk, f"wb{i}", [P, D], BF16) for i in range(2)]
        cqs = sb_tiles(nc, stk, "cqs", [64, NTOK])
        sqs = sb_tiles(nc, stk, "sqs", [64, NTOK])
        cks = sb_tiles(nc, stk, "cks", [64, NTOK])
        sks = sb_tiles(nc, stk, "sks", [64, NTOK])
        rt1 = sb_tiles(nc, stk, "rt1", [64, NTOK])
        rt2 = sb_tiles(nc, stk, "rt2", [64, NTOK])
        rout = [sb_tiles(nc, stk, f"rout{i}", [P, NTOK], BF16) for i in range(2)]
        kfb = sb_tiles(nc, stk, "kfb", [P, 8 * NTOK], BF16)
        cbuf = sb_tiles(nc, stk, "cbuf", [P, 4 * NTOK], BF16)
        vf = xnf[:, 0:8 * NTOK]
        vts = xnf[:, 8 * NTOK:16 * NTOK]
        pt = [stk.enter_context(nc.psum_tensor(f"pt{i}", [P, 1024], BF16))
              for i in range(2)]
        pq = [stk.enter_context(nc.psum_tensor(f"pq{i}", [P, 1024]))
              for i in range(2)]

        xw, fw = {}, {}

        def load_x(s, tt):
            em.dma(s, f"x{tt % 2}", xt[tt % 2][:], x[bass.ts(tt, P), :])
            xw[tt] = em.wv(f"x{tt % 2}")

        def load_f(g, tt):
            em.dma(g, f"f{tt % 2}", ft[tt % 2][:], f0[bass.ts(tt, P), :])
            fw[tt] = em.wv(f"f{tt % 2}")

        def c_loads(s):
            em.dma(s, "cst", idn[:], ident)
            for t_, d_ in ((cqs, cq), (sqs, sq), (cks, ck), (sks, sk)):
                em.dma(s, "cst", t_[:], d_)
            load_x(s, 0)
            load_x(s, 1)
        em.sec("sync", c_loads)
        cstw = em.wv("cst")
        em.sec("gpsimd", lambda g: (load_f(g, 0), load_f(g, 1)))

        def init_eps(v):
            v.memset(epsv[:], EPS)
        em.sec("vector", init_eps)

        x2st = {}
        for tt in range(NT):
            buf = xt[tt % 2]

            def addres(v, tt=tt, buf=buf, w=xw[tt], w2=fw[tt]):
                v.wait_ge(*w)
                v.wait_ge(*w2)
                v.tensor_tensor(out=buf[:], in0=buf[:], in1=ft[tt % 2][:],
                                op=ALU.add).then_inc(dv, 1)
            em.sec("vector", addres)
            adv = em.inc("dv")

            def x2store(s, tt=tt, w=adv):
                s.wait_ge(dv, w)
                em.dma(s, f"sx{tt % 2}", x2[bass.ts(tt, P), :], xt[tt % 2][:])
            em.sec("sync", x2store)
            x2st[tt] = em.wv(f"sx{tt % 2}")

            def stats(v, tt=tt, buf=buf, w=adv):
                v.wait_ge(dv, w)
                for hh in range(8):
                    i_ = v.bn_stats(out=bns[:, 6 * hh:6 * hh + 6],
                                    in_=buf[:, 512 * hh:512 * hh + 512])
                i_.then_inc(dv, 1)
                v.wait_ge(dv, em.inc("dv"))
                v.bn_aggr(out=bna[:], in_=bns[:]).then_inc(dv, 1)
                v.wait_ge(dv, em.inc("dv"))
                v.tensor_scalar(out=negm[:], in0=bna[:, 0:1], scalar1=-1.0,
                                scalar2=None, op0=ALU.mult).then_inc(dv, 1)
            em.sec("vector", stats)
            sdv = em.inc("dv")

            def rstd_f(sc, sdv=sdv):
                sc.wait_ge(dv, sdv)
                sc.activation(out=lnv[:], in_=bna[:, 1:2], func=AF.Ln,
                              bias=epsv[:, 0:1], scale=1.0).then_inc(ac, 1)
                sc.wait_ge(ac, em.inc("ac"))
                sc.activation(out=rstd[:], in_=lnv[:], func=AF.Exp,
                              bias=0.0, scale=-0.5).then_inc(ac, 1)
            em.sec("scalar", rstd_f)
            sac = em.inc("ac")

            def norm(v, buf=buf, sac=sac, sw=x2st[tt]):
                v.wait_ge(ac, sac)
                v.wait_ge(*sw)
                v.tensor_scalar(out=xnt[:], in0=buf[:], scalar1=negm[:, 0:1],
                                scalar2=rstd[:, 0:1], op0=ALU.add,
                                op1=ALU.mult).then_inc(dv, 1)
            em.sec("vector", norm)
            norm_dv = em.inc("dv")

            if tt + 2 < NT:
                def prefetch(s, tt=tt, w=norm_dv):
                    s.wait_ge(dv, w)
                    load_x(s, tt + 2)
                em.sec("sync", prefetch)

                def prefetch2(g, tt=tt, w=norm_dv):
                    g.wait_ge(dv, w)
                    load_f(g, tt + 2)
                em.sec("gpsimd", prefetch2)

            def trans(t, w=norm_dv, base_dv=em.val("dv"), cw=cstw):
                t.wait_ge(*cw)
                t.wait_ge(dv, w)
                for c in range(NCHUNK):
                    if c >= 2:
                        t.wait_ge(dv, base_dv + c - 1)
                    t.transpose(out=pt[c % 2][:, 0:P], in_=xnt[:, bass.ts(c, P)],
                                identity=idn[:]).then_inc(pe, 1)
            em.sec("tensor", trans)
            tr_base = em.val("pe")
            em.inc("pe", NCHUNK)

            def evac(v, tt=tt, tr_base=tr_base):
                for c in range(NCHUNK):
                    v.wait_ge(pe, tr_base + c + 1)
                    v.tensor_copy(out=xnf[:, c * NTOK + tt * P:
                                          c * NTOK + (tt + 1) * P],
                                  in_=pt[c % 2][:, 0:P]).then_inc(dv, 1)
            em.sec("vector", evac)
            em.inc("dv", NCHUNK)
        ln_done = em.val("dv")

        ww = {}

        def wload_one(g, dram, idx, slot, key):
            em.dma(g, f"w{slot}",
                   wb[slot][:, 0:dram.shape[1] // P * P * (P // P)].rearrange(
                       "p (c m) -> p c m", m=P)[:, 0:dram.shape[1] // P, :],
                   dram[idx, :, :].rearrange("(c p) m -> p c m", p=P))
            ww[key] = em.wv(f"w{slot}")

        seq = []
        for mm_ in range(32):
            seq.append(("wq1", mm_, "q", mm_))
        for mm_ in range(4):
            seq.append(("wkr", mm_, "kr", mm_))
        for mm_ in range(4):
            seq.append(("wdk", mm_, "dk", mm_))
        for mm_ in range(4):
            seq.append(("wupk", mm_, "upk", mm_))
        for mm_ in range(8):
            seq.append(("wupv", mm_, "upv", mm_))
        drams = {"wq1": wq1, "wkr": wkr, "wdk": wdk, "wupk": wupk, "wupv": wupv}

        def emit_load(g, n, slot):
            nm, idx, kind, _ = seq[n]
            em.dma(g, f"w{slot}",
                   wb[slot][:].rearrange("p (c m) -> p c m", m=P)
                   [:, 0:drams[nm].shape[1] // P, :],
                   drams[nm][idx, :, :].rearrange("(c p) m -> p c m", p=P))
            ww[n] = em.wv(f"w{slot}")

        em.sec("gpsimd", lambda g: (emit_load(g, 0, 0), emit_load(g, 1, 1)))

        mm_pe = {}
        done_dv = {}
        stw = {}
        cb_dv = {}

        for n, (nm, idx, kind, oi) in enumerate(seq):
            slot = n % 2
            nchk = drams[nm].shape[1] // P

            def mm(t, n=n, slot=slot, kind=kind, nchk=nchk, w=ww[n],
                   pw=done_dv.get(n - 2, 0)):
                t.wait_ge(*w)
                t.wait_ge(dv, em.ln_done if False else ln_done)
                if pw:
                    t.wait_ge(dv, pw)
                for c in range(nchk):
                    last = c == nchk - 1
                    if kind in ("upk", "upv"):
                        rhs0 = cbuf[:, c * NTOK:c * NTOK + 512]
                        rhs1 = cbuf[:, c * NTOK + 512:c * NTOK + 1024]
                        if c == 0:
                            t.wait_ge(dv, cb_dv[3])
                    else:
                        rhs0 = xnf[:, c * NTOK:c * NTOK + 512]
                        rhs1 = xnf[:, c * NTOK + 512:c * NTOK + 1024]
                    i1 = t.matmul(out=pq[slot][:, 0:512],
                                  lhsT=wb[slot][:, bass.ts(c, P)],
                                  rhs=rhs0, start=(c == 0), stop=last)
                    i2 = t.matmul(out=pq[slot][:, 512:1024],
                                  lhsT=wb[slot][:, bass.ts(c, P)],
                                  rhs=rhs1, start=(c == 0), stop=last)
                    if last:
                        i1.then_inc(pe, 1)
                        i2.then_inc(pe, 1)
            em.sec("tensor", mm)
            mm_pe[n] = em.inc("pe", 2)

            if n + 2 < len(seq):
                def pre(g, n=n, slot=slot, w=mm_pe[n]):
                    g.wait_ge(pe, w)
                    emit_load(g, n + 2, slot)
                em.sec("gpsimd", pre)

            if kind == "q":
                def rope(v, n=n, slot=slot, oi=oi, tw=cstw,
                         sw=stw.get(n - 2)):
                    v.wait_ge(pe, mm_pe[n])
                    v.wait_ge(*tw)
                    if sw:
                        v.wait_ge(*sw)
                    p_ = pq[slot]
                    ro = rout[n % 2]
                    v.tensor_tensor(out=rt1[:], in0=p_[0:64, :], in1=cqs[:],
                                    op=ALU.mult)
                    v.tensor_tensor(out=rt2[:], in0=p_[64:128, :], in1=sqs[:],
                                    op=ALU.mult)
                    v.tensor_tensor(out=ro[0:64, :], in0=rt1[:], in1=rt2[:],
                                    op=ALU.subtract)
                    v.tensor_tensor(out=rt1[:], in0=p_[0:64, :], in1=sqs[:],
                                    op=ALU.mult)
                    v.tensor_tensor(out=rt2[:], in0=p_[64:128, :], in1=cqs[:],
                                    op=ALU.mult)
                    v.tensor_tensor(out=ro[64:128, :], in0=rt1[:], in1=rt2[:],
                                    op=ALU.add).then_inc(dv, 1)
                em.sec("vector", rope)
                done_dv[n] = em.inc("dv")

                def store(s, n=n, oi=oi):
                    s.wait_ge(dv, done_dv[n])
                    em.dma(s, f"st{n % 2}", qf[bass.ts(oi, P), :], rout[n % 2][:])
                em.sec("sync", store)
                stw[n] = em.wv(f"st{n % 2}")
            elif kind == "kr":
                def roper(v, n=n, slot=slot, oi=oi, tw=cstw):
                    v.wait_ge(pe, mm_pe[n])
                    v.wait_ge(*tw)
                    p_ = pq[slot]
                    for blk in range(2):
                        h = 2 * oi + blk
                        b0 = 64 * blk
                        dst = kfb[:, h * NTOK:(h + 1) * NTOK]
                        v.tensor_tensor(out=rt1[0:32, :], in0=p_[b0:b0 + 32, :],
                                        in1=cks[0:32, :], op=ALU.mult)
                        v.tensor_tensor(out=rt2[0:32, :],
                                        in0=p_[b0 + 32:b0 + 64, :],
                                        in1=sks[0:32, :], op=ALU.mult)
                        v.tensor_tensor(out=rt1[32:64, :], in0=p_[b0:b0 + 32, :],
                                        in1=sks[0:32, :], op=ALU.mult)
                        v.tensor_tensor(out=rt2[32:64, :],
                                        in0=p_[b0 + 32:b0 + 64, :],
                                        in1=cks[0:32, :], op=ALU.mult)
                        i_ = v.tensor_tensor(out=rt1[0:64, :].bitcast(F32),
                                             in0=rt1[0:64, :].bitcast(F32),
                                             in1=rt1[0:64, :].bitcast(F32),
                                             op=ALU.bypass) if False else None
                        v.tensor_sub(out=dst.bitcast(BF16)[0:32, :]
                                     if False else kfb[0:32, h * NTOK:(h + 1) * NTOK],
                                     in0=rt1[0:32, :], in1=rt2[0:32, :])
                        v.tensor_add(out=kfb[64:96, h * NTOK:(h + 1) * NTOK],
                                     in0=rt1[32:64, :], in1=rt2[32:64, :])
                    v.engine_nop().then_inc(dv, 1)
                em.sec("vector", roper)
                done_dv[n] = em.inc("dv")
            elif kind == "dk":
                def cevac(v, n=n, slot=slot, oi=oi):
                    v.wait_ge(pe, mm_pe[n])
                    v.tensor_copy(out=cbuf[:, bass.ts(oi, NTOK)],
                                  in_=pq[slot][:]).then_inc(dv, 1)
                em.sec("vector", cevac)
                done_dv[n] = em.inc("dv")
                cb_dv[oi] = done_dv[n]
            elif kind == "upk":
                def kevac(v, n=n, slot=slot, oi=oi):
                    v.wait_ge(pe, mm_pe[n])
                    p_ = pq[slot]
                    for blk in range(2):
                        h = 2 * oi + blk
                        v.tensor_copy(out=kfb[32:64, h * NTOK:(h + 1) * NTOK],
                                      in_=p_[64 * blk:64 * blk + 32, :])
                        i_ = v.tensor_copy(
                            out=kfb[96:128, h * NTOK:(h + 1) * NTOK],
                            in_=p_[64 * blk + 32:64 * blk + 64, :])
                    i_.then_inc(dv, 1)
                em.sec("vector", kevac)
                done_dv[n] = em.inc("dv")
            else:
                def vevac(v, n=n, slot=slot, oi=oi):
                    v.wait_ge(pe, mm_pe[n])
                    v.tensor_copy(out=vf[:, bass.ts(oi, NTOK)],
                                  in_=pq[slot][:]).then_inc(dv, 1)
                em.sec("vector", vevac)
                done_dv[n] = em.inc("dv")

        kv_done = em.val("dv")

        def kstore(s, w=kv_done):
            s.wait_ge(dv, w)
            em.dma(s, "stk", kf.rearrange("(h p) n -> p h n", p=P),
                   kfb[:].rearrange("p (h n) -> p h n", n=NTOK))
        em.sec("sync", kstore)

        for kk in range(8):
            def trans(t, kk=kk, base=em.val("dv"), w=kv_done):
                t.wait_ge(dv, w)
                t.wait_ge(dv, base)
                for tt in range(NT):
                    if tt >= 2:
                        t.wait_ge(dv, base + tt - 1)
                    t.transpose(out=pt[tt % 2][:, 0:P],
                                in_=vf[:, kk * NTOK + tt * P:
                                       kk * NTOK + (tt + 1) * P],
                                identity=idn[:]).then_inc(pe, 1)
            em.sec("tensor", trans)
            tb = em.val("pe")
            em.inc("pe", NT)

            def evacv(v, kk=kk, tb=tb):
                for tt in range(NT):
                    v.wait_ge(pe, tb + tt + 1)
                    v.tensor_copy(out=vts[:, tt * KD + kk * P:
                                          tt * KD + (kk + 1) * P],
                                  in_=pt[tt % 2][:, 0:P]).then_inc(dv, 1)
            em.sec("vector", evacv)
            em.inc("dv", NT)

        def store_v(s, w=em.val("dv")):
            s.wait_ge(dv, w)
            em.dma(s, "stv", vt.rearrange("(tt p) d -> p tt d", p=P),
                   vts[:].rearrange("p (tt d) -> p tt d", d=KD))
        em.sec("sync", store_v)

        def fin(s, ws=[em.wv(k) for k in
                       ("st0", "st1", "stk", "stv", "sx0", "sx1")]):
            for w in ws:
                s.wait_ge(*w)
        em.sec("sync", fin)
    return nc



def build_ffn1():
    nc = bass.Bass("TRN2", target_bir_lowering=False, debug=False)
    x = nc.dram_tensor("x", [NTOK, D], F32, kind="ExternalInput").ap()
    w_p = nc.dram_tensor("w_p", [2048, D], I32, kind="ExternalInput").ap()
    sc_in = nc.dram_tensor("sc_in", [32, D], F32, kind="ExternalInput").ap()
    ident = nc.dram_tensor("ident", [P, P], BF16, kind="ExternalInput").ap()
    y = nc.dram_tensor("y", [NTOK, D], F32, kind="ExternalOutput").ap()
    wl_dram = nc.dram_tensor("wl_dram", [2, 16, P, D], BF16, kind="Internal").ap()

    with ExitStack() as stk:
        em = Emit(nc, stk)
        pe, dv, ac = em.sems["pe"], em.sems["dv"], em.sems["ac"]

        xt = [sb_tiles(nc, stk, "xtS", [P, D])] * 2
        xnt = sb_tiles(nc, stk, "xnt", [P, D], BF16)
        xnE = sb_tiles(nc, stk, "xnE", [P, 16 * NTOK], BF16)
        xnO = sb_tiles(nc, stk, "xnO", [P, 16 * NTOK], BF16)
        idn = sb_tiles(nc, stk, "idn", [P, P], BF16)
        bns = sb_tiles(nc, stk, "bns", [P, 48])
        bna = sb_tiles(nc, stk, "bna", [P, 2])
        lnv = sb_tiles(nc, stk, "lnv", [P, 1])
        negm = sb_tiles(nc, stk, "negm", [P, 1])
        rstd = sb_tiles(nc, stk, "rstd", [P, 1])
        epsv = sb_tiles(nc, stk, "epsv", [P, 1])
        se1 = sb_tiles(nc, stk, "se1", [P, D], BF16)
        nsc = sb_tiles(nc, stk, "nsc", [32, D], BF16)
        xgs = sb_tiles(nc, stk, "xgs", [P, 32], BF16)
        XB = sb_tiles(nc, stk, "XB", [32, NTOK], BF16)
        wwb = [sb_tiles(nc, stk, "wwbS", [P, D], BF16)] * 2
        t1 = sb_tiles(nc, stk, "t1", [P, 2048])
        him = sb_tiles(nc, stk, "him", [P, 2048], BF16)
        wsl = [sb_tiles(nc, stk, f"wsl{i}", [P, 16 * 1024], BF16)
               for i in range(2)]
        xb = [sb_tiles(nc, stk, f"xbf{i}", [P, 1024]) for i in range(2)]
        yb = [sb_tiles(nc, stk, f"yb{i}", [P, 1024]) for i in range(2)]
        H1 = wsl[0][:][:, 0:D]
        L1b = wsl[1][:][:, 0:D]
        scb = xnE[:][0:32, 0:D]
        pg = stk.enter_context(nc.psum_tensor("pg", [P, 1024]))
        pu = stk.enter_context(nc.psum_tensor("pu", [P, 1024]))
        ptp = stk.enter_context(nc.psum_tensor("ptp", [P, 1024], BF16))

        def c_loads(s):
            em.dma(s, "cst", idn[:], ident)
        em.sec("sync", c_loads)
        cstw = em.wv("cst")

        def sc_load(g):
            em.dma(g, "csg", scb[:], sc_in)
        em.sec("gpsimd", sc_load)
        scw = em.wv("csg")

        def se_build(v, w=scw):
            v.memset(epsv[:], EPS)
            v.wait_ge(*w)
            for b4 in range(4):
                i_ = v.tensor_copy(out=se1[32 * b4:32 * b4 + 32, :], in_=scb[:])
            i_.then_inc(dv, 1)
            v.wait_ge(dv, em.inc("dv"))
            v.tensor_scalar(out=nsc[:], in0=scb[:], scalar1=-8.0,
                            scalar2=None, op0=ALU.mult).then_inc(dv, 1)
        em.sec("vector", se_build)
        se_dv = em.inc("dv")

        xload = []

        def load_x(s, tt):
            em.dma(s, "x0", xt[0][:], x[bass.ts(tt, P), :])
            xload.append(em.wv("x0"))
        em.sec("sync", lambda s: load_x(s, 0))

        xb_dv = {}
        for tt in range(NT):
            buf = xt[tt % 2]

            def stats(v, tt=tt, buf=buf, w=xload[tt]):
                v.wait_ge(*w)
                for hh in range(8):
                    i_ = v.bn_stats(out=bns[:, 6 * hh:6 * hh + 6],
                                    in_=buf[:, 512 * hh:512 * hh + 512])
                i_.then_inc(dv, 1)
                v.wait_ge(dv, em.inc("dv"))
                v.bn_aggr(out=bna[:], in_=bns[:]).then_inc(dv, 1)
                v.wait_ge(dv, em.inc("dv"))
                v.tensor_scalar(out=negm[:], in0=bna[:, 0:1], scalar1=-1.0,
                                scalar2=None, op0=ALU.mult).then_inc(dv, 1)
            em.sec("vector", stats)
            sdv = em.inc("dv")

            def rstd_f(sc2, sdv=sdv):
                sc2.wait_ge(dv, sdv)
                sc2.activation(out=lnv[:], in_=bna[:, 1:2], func=AF.Ln,
                               bias=epsv[:, 0:1], scale=1.0).then_inc(ac, 1)
                sc2.wait_ge(ac, em.inc("ac"))
                sc2.activation(out=rstd[:], in_=lnv[:], func=AF.Exp,
                               bias=0.0, scale=-0.5).then_inc(ac, 1)
            em.sec("scalar", rstd_f)
            sac = em.inc("ac")

            def norm(v, buf=buf, sac=sac):
                v.wait_ge(ac, sac)
                v.tensor_scalar(out=xnt[:], in0=buf[:], scalar1=negm[:, 0:1],
                                scalar2=rstd[:, 0:1], op0=ALU.add,
                                op1=ALU.mult).then_inc(dv, 1)
                v.wait_ge(dv, em.inc("dv"))
                v.reduce_sum(out=t1[:, 0:32],
                             in_=xnt[:].rearrange("p (g k) -> p g k", g=32),
                             axis=AX.X).then_inc(dv, 1)
                v.wait_ge(dv, em.inc("dv"))
                v.tensor_copy(out=xgs[:], in_=t1[:, 0:32]).then_inc(dv, 1)
            em.sec("vector", norm)
            norm_dv = em.inc("dv")

            if tt + 1 < NT:
                def prefetch(s, tt=tt, w=norm_dv):
                    s.wait_ge(dv, w)
                    load_x(s, tt + 1)
                em.sec("sync", prefetch)

            def gtr(t, w=norm_dv, cw=cstw):
                t.wait_ge(*cw)
                t.wait_ge(dv, w)
                t.transpose(out=ptp[0:32, 0:P], in_=xgs[:],
                            identity=idn[:]).then_inc(pe, 1)
            em.sec("tensor", gtr)
            gpe = em.inc("pe")

            def gcp(v, tt=tt, gpe=gpe):
                v.wait_ge(pe, gpe)
                v.tensor_copy(out=XB[:, bass.ts(tt, P)],
                              in_=ptp[0:32, 0:P]).then_inc(dv, 1)
            em.sec("vector", gcp)
            gcp_dv = em.inc("dv")

            for c in range(16):
                def ptr(t, c=c, w=gcp_dv, base=em.val("dv")):
                    t.wait_ge(dv, w if c == 0 else base)
                    for hh in range(4):
                        for par in range(2):
                            off = 8 * c + 2 * hh + par
                            t.transpose(
                                out=ptp[0:32, (2 * hh + par) * P:
                                        (2 * hh + par + 1) * P],
                                in_=xnt[:, off:D:P],
                                identity=idn[:]).then_inc(pe, 1)
                em.sec("tensor", ptr)
                tb = em.val("pe")
                em.inc("pe", 8)

                def pcp(v, tt=tt, c=c, tb=tb):
                    v.wait_ge(pe, tb + 8)
                    for hh in range(4):
                        for par in range(2):
                            dst = xnE if par == 0 else xnO
                            i_ = v.tensor_copy(
                                out=dst[32 * hh:32 * hh + 32,
                                        c * NTOK + tt * P:c * NTOK + (tt + 1) * P],
                                in_=ptp[0:32, (2 * hh + par) * P:
                                        (2 * hh + par + 1) * P])
                    i_.then_inc(dv, 1)
                em.sec("vector", pcp)
                em.inc("dv")
        ln_done = em.val("dv")

        wload = {}

        def wload_f(g, c):
            em.dma(g, "w0", wwb[0][:], w_p[bass.ts(c, P), :])
            wload[c] = em.wv("w0")
        em.sec("gpsimd", lambda g: wload_f(g, 0))

        dst_so = {}
        for c in range(16):
            for hv in range(2):
                HV = slice(hv * 2048, hv * 2048 + 2048)

                def deq(v, c=c, hv=hv, HV=HV, w=wload[c], sw=se_dv,
                        stw=dst_so.get(c - 1)):
                    v.wait_ge(*w)
                    v.wait_ge(dv, sw)
                    if hv == 0 and stw:
                        v.wait_ge(*stw)
                    v.tensor_scalar(out=t1[:], in0=wwb[c % 2][:, HV],
                                    scalar1=1.0 / 16.0, scalar2=-0.46875,
                                    op0=ALU.mult, op1=ALU.add).then_inc(dv, 1)
                em.sec("vector", deq)
                d1 = em.inc("dv")

                def deqa(sc2, d1=d1):
                    sc2.wait_ge(dv, d1)
                    sc2.activation(out=him[:], in_=t1[:], func=AF.Copy,
                                   bias=129.0, scale=1.0).then_inc(ac, 1)
                em.sec("scalar", deqa)
                d2 = em.inc("ac")

                def deq2(v, c=c, hv=hv, HV=HV, d2=d2):
                    v.wait_ge(ac, d2)
                    v.scalar_tensor_tensor(out=H1[:, HV], in0=him[:],
                                           scalar=-137.0, in1=se1[:, HV],
                                           op0=ALU.add, op1=ALU.mult)
                    v.scalar_tensor_tensor(out=t1[:], in0=him[:],
                                           scalar=-16.0, in1=wwb[c % 2][:, HV],
                                           op0=ALU.mult, op1=ALU.add)
                    v.scalar_tensor_tensor(out=L1b[:, HV], in0=t1[:],
                                           scalar=2056.0, in1=se1[:, HV],
                                           op0=ALU.add,
                                           op1=ALU.mult).then_inc(dv, 1)
                em.sec("vector", deq2)
                dqw = em.inc("dv")

            def wstore(s, c=c, w=dqw):
                s.wait_ge(dv, w)
                em.dma(s, "ws", wl_dram[0, c], H1[:])
                em.dma(s, "ws", wl_dram[1, c], L1b[:])
            em.sec("sync", wstore)
            dst_so[c] = em.wv("ws")

            if c + 1 < 16:
                def wpre(g, c=c, w=dqw):
                    g.wait_ge(dv, w)
                    wload_f(g, c + 1)
                em.sec("gpsimd", wpre)
        deq_done = em.wv("ws")

        wsload = {}

        def wsl_load(g, np_, oq):
            for par in range(2):
                em.dma(g, f"wq{par}",
                       wsl[par][:].rearrange("p (c o) -> p c o", o=1024),
                       wl_dram[par, :, :, oq * 1024:(oq + 1) * 1024].rearrange(
                           "c p o -> p c o"))
            wsload[(np_, oq)] = (em.wv("wq0"), em.wv("wq1"))

        def wc0(g, w=deq_done):
            g.wait_ge(*w)
            wsl_load(g, 0, 0)
        em.sec("gpsimd", wc0)

        ymm_pe = {}
        yev_dv = {}
        yst_so = {}
        xlw = {}
        for np_ in range(4):
            for oq in range(4):
                key = (np_, oq)

                def xl2(s, np_=np_, oq=oq, key=key,
                        w=yev_dv.get((np_ - 1, oq), 0)):
                    if w:
                        s.wait_ge(dv, w)
                    for half in range(2):
                        tt = 2 * np_ + half
                        em.dma(s, f"xs{half}", xb[half][:, 0:1024],
                               x[bass.ts(tt, P), oq * 1024:(oq + 1) * 1024])
                    xlw[key] = (em.wv("xs0"), em.wv("xs1"))
                em.sec("sync", xl2)

                def ymm(t, np_=np_, oq=oq, key=key, w=wsload[key],
                        lnw=ln_done, pw=yev_dv.get((np_ - 1, oq), 0)):
                    t.wait_ge(*w[0])
                    t.wait_ge(*w[1])
                    t.wait_ge(dv, lnw)
                    if pw:
                        t.wait_ge(dv, pw)
                    for nt2 in range(2):
                        tt = 2 * np_ + nt2
                        pA = (pg, pu)[nt2]
                        for c in range(16):
                            for par in range(2):
                                xn_ = xnE if par == 0 else xnO
                                lhsT = xn_[:, c * NTOK + tt * P:
                                           c * NTOK + (tt + 1) * P]
                                rhsw = wsl[par][:, c * 1024:(c + 1) * 1024]
                                st_ = (c == 0 and par == 0)
                                sp_ = (c == 15 and par == 1)
                                m0 = t.matmul(out=pA[:, 0:512], lhsT=lhsT,
                                              rhs=rhsw[:, 0:512], start=st_,
                                              stop=sp_)
                                m1 = t.matmul(out=pA[:, 512:1024], lhsT=lhsT,
                                              rhs=rhsw[:, 512:1024], start=st_,
                                              stop=sp_)
                                if sp_:
                                    m0.then_inc(pe, 1)
                                    m1.then_inc(pe, 1)
                em.sec("tensor", ymm)
                ymm_pe[key] = em.inc("pe", 4)

                nxt = (np_, oq + 1) if oq < 3 else (np_ + 1, 0)
                if nxt[0] < 4:
                    def wpre2(g, nxt=nxt, w=ymm_pe[key]):
                        g.wait_ge(pe, w)
                        wsl_load(g, nxt[0], nxt[1])
                    em.sec("gpsimd", wpre2)

                for nt2 in range(2):
                    tt = 2 * np_ + nt2
                    pA = (pg, pu)[nt2]

                    def yev(v, key=key, nt2=nt2, pA=pA,
                            w=ymm_pe[key], xw2=xlw[key],
                            sw=yst_so.get((np_ - 1, oq, nt2))):
                        v.wait_ge(pe, w)
                        v.wait_ge(*xw2[nt2])
                        if sw:
                            v.wait_ge(*sw)
                        v.tensor_tensor(out=yb[nt2][:], in0=pA[:],
                                        in1=xb[nt2][:, 0:1024],
                                        op=ALU.add).then_inc(dv, 1)
                    em.sec("vector", yev)
                    yev_dv[key] = em.inc("dv")

                    def yst(s, key=key, nt2=nt2, tt=tt, oq=oq,
                            w=yev_dv[key]):
                        s.wait_ge(dv, w)
                        em.dma(s, f"y{nt2}",
                               y[bass.ts(tt, P), oq * 1024:(oq + 1) * 1024],
                               yb[nt2][:])
                    em.sec("sync", yst)
                    yst_so[(np_, oq, nt2)] = em.wv(f"y{nt2}")

        def fin(s, w0=em.wv("y0"), w1=em.wv("y1")):
            s.wait_ge(*w0)
            s.wait_ge(*w1)
        em.sec("sync", fin)
    return nc



def _gather_kv(res1):
    kf_all = [np.zeros((KD, S), BF) for _ in range(2)]
    vt_all = [np.zeros((S, KD), BF) for _ in range(2)]
    for c in range(8):
        b, j = c // 4, c % 4
        rows = _own_rows(j)
        kf_all[b][:, rows] = np.asarray(res1.results[c]["kf"])
        vt_all[b][rows, :] = np.asarray(res1.results[c]["vt"])
    return kf_all, vt_all


def kernel(**inputs):
    f32 = np.float32
    X = np.asarray(inputs["L0_attn_X"], f32)

    wq0 = _pretile(np.asarray(inputs["L0_attn_Wq"], f32))
    wkv0 = _pretile(np.concatenate(
        [np.asarray(inputs["L0_attn_Wk"], f32),
         np.asarray(inputs["L0_attn_Wv"], f32)], axis=1))
    wo0 = np.ascontiguousarray(np.asarray(inputs["L0_attn_Wo"], f32))
    gate_r = _pack_reorder_gu(np.asarray(inputs["L0_ffn_gate_Wq"], np.int32))
    up_r = _pack_reorder_gu(np.asarray(inputs["L0_ffn_up_Wq"], np.int32))
    down_t = np.ascontiguousarray(np.asarray(inputs["L0_ffn_down_Wq"],
                                             np.int32).T)
    sgT = np.ascontiguousarray(np.asarray(inputs["L0_ffn_gate_scale"], f32).T)
    zgT = np.ascontiguousarray(np.asarray(inputs["L0_ffn_gate_zero"], f32).T)
    suT = np.ascontiguousarray(np.asarray(inputs["L0_ffn_up_scale"], f32).T)
    zuT = np.ascontiguousarray(np.asarray(inputs["L0_ffn_up_zero"], f32).T)
    sdT = np.ascontiguousarray(
        np.asarray(inputs["L0_ffn_down_scale"], f32).T.reshape(43, 4, D))
    zdT = np.ascontiguousarray(
        np.asarray(inputs["L0_ffn_down_zero"], f32).T.reshape(43, 4, D))
    wq1 = _pretile(np.ascontiguousarray(np.asarray(inputs["L1_attn_Wq"], f32).T))
    wkr1 = _pretile(np.ascontiguousarray(np.asarray(inputs["L1_attn_Wkr"], f32).T))
    wdk1 = _pretile(np.ascontiguousarray(np.asarray(inputs["L1_attn_Wdk"], f32).T))
    wupk1 = _pretile(np.ascontiguousarray(np.asarray(inputs["L1_attn_Wupk"], f32).T))
    wupv1 = _pretile(np.ascontiguousarray(np.asarray(inputs["L1_attn_Wupv"], f32).T))
    wo1 = np.ascontiguousarray(np.asarray(inputs["L1_attn_Wo"], f32).T)
    w1p = _pack_l1(np.asarray(inputs["L1_ffn_packed_weight"], np.int32))
    sc1 = np.ascontiguousarray(
        np.asarray(inputs["L1_ffn_scales"], f32).reshape(D, 32).T)
    i4 = np.zeros((P, 4), BF)
    for p in range(P):
        i4[p, p // 32] = 1
    i4t = np.ascontiguousarray(i4.T)

    ropes = []
    for j in range(4):
        rows = _own_rows(j)
        cqs, sqs = _rope_tables(rows, 1.0 / np.sqrt(DH))
        cks, sks = _rope_tables(rows, 1.0)
        ropes.append((cqs, sqs, cks, sks))

    nc = build_qkv_l0()
    maps = []
    for c in range(8):
        b, j = c // 4, c % 4
        cqs, sqs, cks, sks = ropes[j]
        maps.append({"x": np.ascontiguousarray(X[b][_own_rows(j)]),
                     "wq": wq0, "wkv": wkv0,
                     "cq": cqs, "sq": sqs, "ck": cks, "sk": sks,
                     "ident": _IDENT})
    r1 = run_bass_kernel_spmd(nc, maps, core_ids=list(range(8)))
    kf_all, vt_all = _gather_kv(r1)

    nc = build_attn()
    maps2 = []
    for c in range(8):
        b, j = c // 4, c % 4
        maps2.append({"qf": np.asarray(r1.results[c]["qf"]),
                      "kf": kf_all[b], "vtok": vt_all[b],
                      "x": maps[c]["x"], "wo": wo0,
                      "masks": _masks(j), "ones": np.ones((P, P), BF)})
    r2 = run_bass_kernel_spmd(nc, maps2, core_ids=list(range(8)))

    nc = build_ffn0()
    maps3 = []
    for c in range(8):
        maps3.append({"x": np.asarray(r2.results[c]["y"], f32),
                      "gate_p": gate_r, "up_p": up_r, "down_p": down_t,
                      "sg": sgT, "zg": zgT, "su": suT, "zu": zuT,
                      "sd": sdT, "zd": zdT,
                      "ind4": i4, "ind4t": i4t, "ident": _IDENT})
    r3 = run_bass_kernel_spmd(nc, maps3, core_ids=list(range(8)))

    nc = build_qkv_l1()
    maps4 = []
    for c in range(8):
        b, j = c // 4, c % 4
        cqs, sqs, cks, sks = ropes[j]
        maps4.append({"x": maps3[c]["x"],
                      "f0": np.ascontiguousarray(
                          np.asarray(r3.results[c]["y"]).T),
                      "wq1": wq1, "wkr": wkr1, "wdk": wdk1,
                      "wupk": wupk1, "wupv": wupv1,
                      "cq": cqs, "sq": sqs, "ck": cks, "sk": sks,
                      "ident": _IDENT})
    r4 = run_bass_kernel_spmd(nc, maps4, core_ids=list(range(8)))
    kf_all, vt_all = _gather_kv(r4)

    nc = build_attn()
    maps5 = []
    for c in range(8):
        b, j = c // 4, c % 4
        maps5.append({"qf": np.asarray(r4.results[c]["qf"]),
                      "kf": kf_all[b], "vtok": vt_all[b],
                      "x": np.asarray(r4.results[c]["x2"], f32), "wo": wo1,
                      "masks": _masks(j), "ones": np.ones((P, P), BF)})
    r5 = run_bass_kernel_spmd(nc, maps5, core_ids=list(range(8)))

    nc = build_ffn1()
    maps6 = []
    for c in range(8):
        maps6.append({"x": np.asarray(r5.results[c]["y"], f32),
                      "w_p": w1p, "sc_in": sc1, "ident": _IDENT})
    r6 = run_bass_kernel_spmd(nc, maps6, core_ids=list(range(8)))

    out = np.zeros((2, S, D), f32)
    for c in range(8):
        b, j = c // 4, c % 4
        out[b][_own_rows(j)] = np.asarray(r6.results[c]["y"], f32)
    return out


# version 03
# speedup vs baseline: 1.1215x; 1.1215x over previous
import numpy as np
import ml_dtypes
import concourse.bass as bass
import concourse.mybir as mybir
from concourse.bass_utils import run_bass_kernel_spmd
from contextlib import ExitStack

F32 = mybir.dt.float32
BF16 = mybir.dt.bfloat16
I32 = mybir.dt.int32
AF = mybir.ActivationFunctionType
ALU = mybir.AluOpType
BF = ml_dtypes.bfloat16
AX = mybir.AxisListType

P = 128
D = 4096
NCHUNK = D // P
NT = 8
NTOK = NT * P
KD = 1024
NQH = 32
NKVH = 8
DH = 128
S = 4096
EPS = 1e-5
ROPE_THETA = 10000.0
DBG_NO_HSUM = False
DBG_NO_ZS = False


def _pretile(w, tile=P):
    K, M = w.shape
    return np.ascontiguousarray(w.reshape(K, M // tile, tile).transpose(1, 0, 2))


def _rope_tables(pos, scale):
    inv_freq = 1.0 / ROPE_THETA ** (np.arange(0, 64, dtype=np.float64) / 64.0)
    ang = inv_freq[:, None] * np.asarray(pos, np.float64)[None, :]
    return (np.cos(ang) * scale).astype(np.float32), \
           (np.sin(ang) * scale).astype(np.float32)


def _own_rows(j):
    idx = []
    for i in range(NT):
        t = 4 * i + j
        idx.extend(range(P * t, P * t + P))
    return np.asarray(idx)


def _masks(j):
    m = np.zeros((4, P, P), np.float32)
    for kk in range(4):
        if kk < j:
            m[kk] = 1.0
        elif kk == j:
            m[kk] = np.tril(np.ones((P, P), np.float32)).T
    return m.astype(BF)


def _pack_reorder_gu(wq_packed):
    t = np.ascontiguousarray(wq_packed.T)
    J, I = t.shape
    return np.ascontiguousarray(
        t.reshape(J // 32, 32, I).transpose(1, 0, 2).reshape(J, I))


def _pack_l1(packed):
    O = packed.shape[0]
    t = np.ascontiguousarray(packed.reshape(O, 2048).T)
    return np.ascontiguousarray(
        t.reshape(32, 64, O).transpose(1, 0, 2).reshape(2048, O))



class Emit:

    def __init__(self, nc, stack):
        self.nc = nc
        self.stack = stack
        self.block = stack.enter_context(nc.Block())
        self.sems = {}
        self.cnt = {}
        for name in ("ld", "lg", "pe", "dv", "ac", "so"):
            self.sems[name] = stack.enter_context(nc.semaphore(name))
            self.cnt[name] = 0

    def sem(self, name):
        if name not in self.sems:
            self.sems[name] = self.stack.enter_context(self.nc.semaphore(name))
            self.cnt[name] = 0
        return self.sems[name]

    def dma(self, q, name, out, in_):
        s = self.sem(name)
        q.dma_start(out=out, in_=in_).then_inc(s, 16)
        return s, self.inc(name, 16)

    def wv(self, name):
        return self.sem(name), self.cnt[name]

    def sec(self, eng, fn):
        getattr(self.block, eng)(fn)

    def inc(self, name, n=1):
        self.cnt[name] += n
        return self.cnt[name]

    def val(self, name):
        return self.cnt[name]


def sb_tiles(nc, stack, name, shape, dt=F32):
    return stack.enter_context(nc.sbuf_tensor(name, shape, dt))



def build_qkv_l0():
    nc = bass.Bass("TRN2", target_bir_lowering=False, debug=False)
    x = nc.dram_tensor("x", [NTOK, D], F32, kind="ExternalInput").ap()
    wq = nc.dram_tensor("wq", [32, D, P], F32, kind="ExternalInput").ap()
    wkv = nc.dram_tensor("wkv", [16, D, P], F32, kind="ExternalInput").ap()
    cq = nc.dram_tensor("cq", [64, NTOK], F32, kind="ExternalInput").ap()
    sq = nc.dram_tensor("sq", [64, NTOK], F32, kind="ExternalInput").ap()
    ck = nc.dram_tensor("ck", [64, NTOK], F32, kind="ExternalInput").ap()
    sk = nc.dram_tensor("sk", [64, NTOK], F32, kind="ExternalInput").ap()
    ident = nc.dram_tensor("ident", [P, P], BF16, kind="ExternalInput").ap()
    qf = nc.dram_tensor("qf", [D, NTOK], BF16, kind="ExternalOutput").ap()
    kf = nc.dram_tensor("kf", [KD, NTOK], BF16, kind="ExternalOutput").ap()
    vt = nc.dram_tensor("vt", [NTOK, KD], BF16, kind="ExternalOutput").ap()

    with ExitStack() as stk:
        em = Emit(nc, stk)
        _emit_ln_xnf(nc, em, stk, x, ident)
        xnf = em.xnf
        _emit_qkv_common(nc, em, stk, wq, wkv, cq, sq, ck, sk, qf, kf, vt)
    return nc


def _emit_ln_xnf(nc, em, stk, x, ident, xwidth=D):
    xt = [sb_tiles(nc, stk, f"xt{i}", [P, D]) for i in range(2)]
    xnf = sb_tiles(nc, stk, "xnf", [P, NCHUNK * NTOK], BF16)
    xnt = sb_tiles(nc, stk, "xnt", [P, D], BF16)
    idn = sb_tiles(nc, stk, "idn", [P, P], BF16)
    bns = sb_tiles(nc, stk, "bns", [P, 48])
    bna = sb_tiles(nc, stk, "bna", [P, 2])
    lnv = sb_tiles(nc, stk, "lnv", [P, 1])
    negm = sb_tiles(nc, stk, "negm", [P, 1])
    rstd = sb_tiles(nc, stk, "rstd", [P, 1])
    epsv = sb_tiles(nc, stk, "epsv", [P, 1])
    pt = [stk.enter_context(nc.psum_tensor(f"pt{i}", [P, 1024], BF16))
          for i in range(2)]
    em.xnf = xnf
    em.idn = idn
    em.pt = pt

    dv, ac, pe = em.sems["dv"], em.sems["ac"], em.sems["pe"]

    xw = {}

    def load_x(s, tt):
        em.dma(s, f"x{tt % 2}", xt[tt % 2][:], x[bass.ts(tt, P), :])
        xw[tt] = em.wv(f"x{tt % 2}")

    def sec_init(s):
        em.dma(s, "cst", idn[:], ident)
        load_x(s, 0)
        load_x(s, 1)
    em.sec("sync", sec_init)
    em.cstw = em.wv("cst")

    def init_eps(v):
        v.memset(epsv[:], EPS)
    em.sec("vector", init_eps)

    for tt in range(NT):
        buf = xt[tt % 2]

        def stats(v, tt=tt, buf=buf, w=xw[tt]):
            v.wait_ge(*w)
            for hh in range(8):
                i_ = v.bn_stats(out=bns[:, 6 * hh:6 * hh + 6],
                                in_=buf[:, 512 * hh:512 * hh + 512])
            i_.then_inc(dv, 1)
            v.wait_ge(dv, em.inc("dv"))
            v.bn_aggr(out=bna[:], in_=bns[:]).then_inc(dv, 1)
            v.wait_ge(dv, em.inc("dv"))
            v.tensor_scalar(out=negm[:], in0=bna[:, 0:1], scalar1=-1.0,
                            scalar2=None, op0=ALU.mult).then_inc(dv, 1)
        em.sec("vector", stats)
        sdv = em.inc("dv")

        def rstd_f(sc, sdv=sdv):
            sc.wait_ge(dv, sdv)
            sc.activation(out=lnv[:], in_=bna[:, 1:2], func=AF.Ln,
                          bias=epsv[:, 0:1], scale=1.0).then_inc(ac, 1)
            sc.wait_ge(ac, em.inc("ac"))
            sc.activation(out=rstd[:], in_=lnv[:], func=AF.Exp,
                          bias=0.0, scale=-0.5).then_inc(ac, 1)
        em.sec("scalar", rstd_f)
        sac = em.inc("ac")

        def norm(v, buf=buf, sac=sac):
            v.wait_ge(ac, sac)
            v.tensor_scalar(out=xnt[:], in0=buf[:], scalar1=negm[:, 0:1],
                            scalar2=rstd[:, 0:1], op0=ALU.add,
                            op1=ALU.mult).then_inc(dv, 1)
        em.sec("vector", norm)
        norm_dv = em.inc("dv")

        if tt + 2 < NT:
            def prefetch(s, tt=tt, w=norm_dv):
                s.wait_ge(dv, w)
                load_x(s, tt + 2)
            em.sec("sync", prefetch)

        def trans(t, w=norm_dv, base_dv=em.val("dv"), cst=em.cstw):
            t.wait_ge(*cst)
            t.wait_ge(dv, w)
            for c in range(NCHUNK):
                if c >= 2:
                    t.wait_ge(dv, base_dv + c - 1)
                t.transpose(out=pt[c % 2][:, 0:P], in_=xnt[:, bass.ts(c, P)],
                            identity=idn[:]).then_inc(pe, 1)
        em.sec("tensor", trans)
        tr_base = em.val("pe")
        em.inc("pe", NCHUNK)

        def evac(v, tt=tt, tr_base=tr_base):
            for c in range(NCHUNK):
                v.wait_ge(pe, tr_base + c + 1)
                v.tensor_copy(out=xnf[:, c * NTOK + tt * P:c * NTOK + (tt + 1) * P],
                              in_=pt[c % 2][:, 0:P]).then_inc(dv, 1)
        em.sec("vector", evac)
        em.inc("dv", NCHUNK)
    em.ln_done = em.val("dv")


def _emit_qkv_common(nc, em, stk, wq, wkv, cq, sq, ck, sk, qf, kf, vt):
    xnf, idn, pt = em.xnf, em.idn, em.pt
    pe, dv, ac = em.sems["pe"], em.sems["dv"], em.sems["ac"]
    wb = [sb_tiles(nc, stk, f"wb{i}", [P, D], BF16) for i in range(2)]
    cqs = sb_tiles(nc, stk, "cqs", [64, NTOK])
    sqs = sb_tiles(nc, stk, "sqs", [64, NTOK])
    cks = sb_tiles(nc, stk, "cks", [64, NTOK])
    sks = sb_tiles(nc, stk, "sks", [64, NTOK])
    rt1 = sb_tiles(nc, stk, "rt1", [64, NTOK])
    rt2 = sb_tiles(nc, stk, "rt2", [64, NTOK])
    rout = [sb_tiles(nc, stk, f"rout{i}", [P, NTOK], BF16) for i in range(2)]
    vf = sb_tiles(nc, stk, "vf", [P, 8 * NTOK], BF16)
    vts = sb_tiles(nc, stk, "vts", [P, 8 * KD], BF16)
    pq = [stk.enter_context(nc.psum_tensor(f"pq{i}", [P, 1024])) for i in range(2)]

    def tables(s):
        for t_, d_ in ((cqs, cq), (sqs, sq), (cks, ck), (sks, sk)):
            em.dma(s, "cst", t_[:], d_)
    em.sec("sync", tables)
    tabw = em.wv("cst")

    ww = {}

    def wload_one(g, dram, idx, slot):
        em.dma(g, f"w{slot}",
               wb[slot][:].rearrange("p (c m) -> p c m", m=P),
               dram[idx, :, :].rearrange("(c p) m -> p c m", p=P))
        ww[idx if dram is wq else 32 + idx] = em.wv(f"w{slot}")

    def init_wload(g):
        wload_one(g, wq, 0, 0)
        wload_one(g, wq, 1, 1)
    em.sec("gpsimd", init_wload)

    mm_pe = []
    rope_dv = []
    stw = {}

    for m in range(48):
        slot = m % 2
        is_q = m < 32
        is_k = 32 <= m < 40

        def mm(t, m=m, slot=slot, w=ww[m]):
            t.wait_ge(*w)
            t.wait_ge(dv, em.ln_done)
            if m >= 2:
                t.wait_ge(dv, rope_dv[m - 2])
            for c in range(NCHUNK):
                last = c == NCHUNK - 1
                i1 = t.matmul(out=pq[slot][:, 0:512],
                              lhsT=wb[slot][:, bass.ts(c, P)],
                              rhs=xnf[:, c * NTOK:c * NTOK + 512],
                              start=(c == 0), stop=last)
                i2 = t.matmul(out=pq[slot][:, 512:1024],
                              lhsT=wb[slot][:, bass.ts(c, P)],
                              rhs=xnf[:, c * NTOK + 512:c * NTOK + 1024],
                              start=(c == 0), stop=last)
                if last:
                    i1.then_inc(pe, 1)
                    i2.then_inc(pe, 1)
        em.sec("tensor", mm)
        mm_pe.append(em.inc("pe", 2))

        if m + 2 < 48:
            nm = m + 2

            def pre(g, nm=nm, slot=slot, w=mm_pe[m]):
                g.wait_ge(pe, w)
                if nm < 32:
                    wload_one(g, wq, nm, slot)
                else:
                    wload_one(g, wkv, nm - 32, slot)
            em.sec("gpsimd", pre)

        if is_q or is_k:
            c_, s_ = (cqs, sqs) if is_q else (cks, sks)

            def rope(v, m=m, slot=slot, c_=c_, s_=s_, tw=tabw,
                     sw=stw.get(m - 2)):
                v.wait_ge(pe, mm_pe[m])
                v.wait_ge(*tw)
                if sw:
                    v.wait_ge(*sw)
                p_ = pq[slot]
                ro = rout[m % 2]
                v.tensor_tensor(out=rt1[:], in0=p_[0:64, :], in1=c_[:], op=ALU.mult)
                v.tensor_tensor(out=rt2[:], in0=p_[64:128, :], in1=s_[:], op=ALU.mult)
                v.tensor_tensor(out=ro[0:64, :], in0=rt1[:], in1=rt2[:],
                                op=ALU.subtract)
                v.tensor_tensor(out=rt1[:], in0=p_[0:64, :], in1=s_[:], op=ALU.mult)
                v.tensor_tensor(out=rt2[:], in0=p_[64:128, :], in1=c_[:], op=ALU.mult)
                v.tensor_tensor(out=ro[64:128, :], in0=rt1[:], in1=rt2[:],
                                op=ALU.add).then_inc(dv, 1)
            em.sec("vector", rope)
            rope_dv.append(em.inc("dv"))

            def store(s, m=m):
                s.wait_ge(dv, rope_dv[m])
                dst = qf[bass.ts(m, P), :] if m < 32 else kf[bass.ts(m - 32, P), :]
                em.dma(s, f"st{m % 2}", dst, rout[m % 2][:])
            em.sec("sync", store)
            stw[m] = em.wv(f"st{m % 2}")
        else:
            def vcopy(v, m=m, slot=slot):
                v.wait_ge(pe, mm_pe[m])
                v.tensor_copy(out=vf[:, bass.ts(m - 40, NTOK)],
                              in_=pq[slot][:]).then_inc(dv, 1)
            em.sec("vector", vcopy)
            rope_dv.append(em.inc("dv"))
            stw[m] = stw.get(m - 1)

    v_done = rope_dv[-1]

    for kk in range(8):
        def trans(t, kk=kk, base=em.val("dv")):
            t.wait_ge(dv, v_done)
            t.wait_ge(dv, base)
            for tt in range(NT):
                if tt >= 2:
                    t.wait_ge(dv, base + tt - 1)
                t.transpose(out=pt[tt % 2][:, 0:P],
                            in_=vf[:, kk * NTOK + tt * P:kk * NTOK + (tt + 1) * P],
                            identity=idn[:]).then_inc(pe, 1)
        em.sec("tensor", trans)
        tb = em.val("pe")
        em.inc("pe", NT)

        def evac(v, kk=kk, tb=tb):
            for tt in range(NT):
                v.wait_ge(pe, tb + tt + 1)
                v.tensor_copy(out=vts[:, tt * KD + kk * P:tt * KD + (kk + 1) * P],
                              in_=pt[tt % 2][:, 0:P]).then_inc(dv, 1)
        em.sec("vector", evac)
        em.inc("dv", NT)

    def store_v(s, w=em.val("dv")):
        s.wait_ge(dv, w)
        em.dma(s, "stv", vt.rearrange("(tt p) d -> p tt d", p=P),
               vts[:].rearrange("p (tt d) -> p tt d", d=KD))
    em.sec("sync", store_v)

    def fin(s, w0=em.wv("st0"), w1=em.wv("st1"), wv_=em.wv("stv")):
        s.wait_ge(*w0)
        s.wait_ge(*w1)
        s.wait_ge(*wv_)
    em.sec("sync", fin)



_IDENT = np.eye(P, dtype=BF)


def _prep_qkv_l0(inputs):
    Xf = np.asarray(inputs["L0_attn_X"], np.float32)
    wq_t = _pretile(np.asarray(inputs["L0_attn_Wq"], np.float32))
    wkv = np.concatenate([np.asarray(inputs["L0_attn_Wk"], np.float32),
                          np.asarray(inputs["L0_attn_Wv"], np.float32)], axis=1)
    wkv_t = _pretile(wkv)
    maps = []
    for c in range(8):
        b, j = c // 4, c % 4
        rows = _own_rows(j)
        cqs, sqs = _rope_tables(rows, 1.0 / np.sqrt(DH))
        cks, sks = _rope_tables(rows, 1.0)
        maps.append({
            "x": np.ascontiguousarray(Xf[b][rows]),
            "wq": wq_t, "wkv": wkv_t,
            "cq": cqs, "sq": sqs, "ck": cks, "sk": sks,
            "ident": _IDENT,
        })
    return maps


def kernel(**inputs):
    raise NotImplementedError("built incrementally; see tests")



def build_attn():
    nc = bass.Bass("TRN2", target_bir_lowering=False, debug=False)
    qf = nc.dram_tensor("qf", [D, NTOK], BF16, kind="ExternalInput").ap()
    kf = nc.dram_tensor("kf", [KD, S], BF16, kind="ExternalInput").ap()
    vtok = nc.dram_tensor("vtok", [S, KD], BF16, kind="ExternalInput").ap()
    x = nc.dram_tensor("x", [NTOK, D], F32, kind="ExternalInput").ap()
    wo = nc.dram_tensor("wo", [D, D], F32, kind="ExternalInput").ap()
    masks = nc.dram_tensor("masks", [4, P, P], BF16, kind="ExternalInput").ap()
    ones = nc.dram_tensor("ones", [P, P], BF16, kind="ExternalInput").ap()
    y = nc.dram_tensor("y", [NTOK, D], F32, kind="ExternalOutput").ap()

    with ExitStack() as stk:
        em = Emit(nc, stk)
        ld, lg, pe, dv, ac, so = (em.sems[k] for k in
                                  ("ld", "lg", "pe", "dv", "ac", "so"))

        kbuf = [sb_tiles(nc, stk, f"kbuf{i}", [P, S], BF16) for i in range(2)]
        vbuf = [sb_tiles(nc, stk, f"vbuf{i}", [P, S], BF16) for i in range(2)]
        qbuf = [sb_tiles(nc, stk, f"qbuf{i}", [P, NTOK], BF16) for i in range(2)]
        sqk = sb_tiles(nc, stk, "sqk", [P, S], BF16)
        sqq = sb_tiles(nc, stk, "sqq", [P, NTOK], BF16)
        st = [sb_tiles(nc, stk, f"st{i}", [P, 1024], BF16) for i in range(2)]
        acc = sb_tiles(nc, stk, "acc", [P, 1024])
        accb = sb_tiles(nc, stk, "accb", [P, 1024], BF16)
        attn = sb_tiles(nc, stk, "attn", [P, NQH * NTOK], BF16)
        mko = sb_tiles(nc, stk, "mko", [P, 5 * P], BF16)
        negb = sb_tiles(nc, stk, "negb", [P, NQH])
        nrm = sb_tiles(nc, stk, "nrm", [1, 16])
        nbf = sb_tiles(nc, stk, "nbf", [1, 16], BF16)
        rsb = sb_tiles(nc, stk, "rsb", [1, 1024], BF16)
        rb = sb_tiles(nc, stk, "rb", [P, 1024])
        xb = [sb_tiles(nc, stk, f"xb{i}", [P, 1024]) for i in range(2)]
        yb = [sb_tiles(nc, stk, f"yb{i}", [P, 1024]) for i in range(2)]
        wob = sb_tiles(nc, stk, "wob", [P, NCHUNK * 1024], BF16)

        py0 = stk.enter_context(nc.psum_tensor("py0", [P, 1024]))
        py1 = stk.enter_context(nc.psum_tensor("py1", [P, 1024]))
        pbc = stk.enter_context(nc.psum_tensor("pbc", [P, 1024]))
        pnq = stk.enter_context(nc.psum_tensor("pnq", [1, 1024]))

        def c_loads(s):
            em.dma(s, "cst", mko[:, 0:4 * P].rearrange("p (k q) -> p k q", q=P),
                   masks.rearrange("k p q -> p k q"))
            em.dma(s, "cst", mko[:, 4 * P:5 * P], ones)
        em.sec("sync", c_loads)
        cstw = em.wv("cst")

        kload, qload = {}, {}
        kfree_pe = {}
        qfree_pe = {}

        def load_kv(g, h):
            em.dma(g, f"kv{h % 2}", kbuf[h % 2][:], kf[bass.ts(h, P), :])
            em.dma(g, f"kv{h % 2}",
                   vbuf[h % 2][:].rearrange("p (t d) -> p t d", d=P),
                   vtok.rearrange("(t p) d -> p t d", p=P)[:, :, bass.ts(h, P)])
            kload[h] = em.wv(f"kv{h % 2}")

        def load_q(s, qg):
            em.dma(s, f"q{qg % 2}", qbuf[qg % 2][:], qf[bass.ts(qg, P), :])
            qload[qg] = em.wv(f"q{qg % 2}")

        em.sec("gpsimd", lambda g: (load_kv(g, 0), load_kv(g, 1)))
        em.sec("sync", lambda s: (load_q(s, 0), load_q(s, 1)))

        exp_ac = {}
        mask_dv = {}
        add_dv = {}
        pv_pe = {}
        evac_dv = {}
        rb_ac = {}
        negb_ac = {}
        nb_dv = {}

        def kcol(h):
            return 1 + (h % 2)

        def emit_norm_k(h):
            def sq(sc, h=h, pw=em.val("pe"), w=kload[h]):
                sc.wait_ge(*w)
                sc.wait_ge(pe, pw)
                sc.activation(out=sqk[:], in_=kbuf[h % 2][:],
                              func=AF.Square).then_inc(ac, 1)
            em.sec("scalar", sq)
            sqa = em.inc("ac")
            for it in range(8):
                def mmk(t, it=it, sqa=sqa, w=em.val("dv"), cw=cstw):
                    t.wait_ge(ac, sqa)
                    t.wait_ge(*cw)
                    t.wait_ge(dv, w)
                    t.matmul(out=pnq[0:1, 0:512],
                             lhsT=mko[:, 4 * P:4 * P + 1],
                             rhs=sqk[:, it * 512:(it + 1) * 512],
                             start=True, stop=True).then_inc(pe, 1)
                em.sec("tensor", mmk)
                mk = em.inc("pe")

                def mx(v, it=it, mk=mk, h=h):
                    v.wait_ge(pe, mk)
                    v.reduce_max(out=nrm[0:1, 4:5], in_=pnq[0:1, 0:512],
                                 axis=AX.X).then_inc(dv, 1)
                    v.wait_ge(dv, em.inc("dv"))
                    kc = kcol(h)
                    if it == 0:
                        v.tensor_copy(out=nrm[0:1, kc:kc + 1],
                                      in_=nrm[0:1, 4:5]).then_inc(dv, 1)
                    else:
                        v.tensor_tensor(out=nrm[0:1, kc:kc + 1],
                                        in0=nrm[0:1, kc:kc + 1],
                                        in1=nrm[0:1, 4:5],
                                        op=ALU.max).then_inc(dv, 1)
                    em.inc("dv")
                    v.wait_ge(dv, em.val("dv"))
                em.sec("vector", mx)

        def emit_norm_q(qg):
            h = qg // 4

            def sq(sc, qg=qg, pw=em.val("pe"), w=qload[qg]):
                sc.wait_ge(*w)
                sc.wait_ge(pe, pw)
                sc.activation(out=sqq[:], in_=qbuf[qg % 2][:],
                              func=AF.Square).then_inc(ac, 1)
            em.sec("scalar", sq)
            sqa = em.inc("ac")

            def mmq(t, sqa=sqa, w=em.val("dv"), cw=cstw):
                t.wait_ge(ac, sqa)
                t.wait_ge(*cw)
                t.wait_ge(dv, w)
                t.matmul(out=pnq[0:1, 512:1024],
                         lhsT=mko[:, 4 * P:4 * P + 1],
                         rhs=sqq[:, 0:512], start=True, stop=False)
                t.matmul(out=pnq[0:1, 512:1024],
                         lhsT=mko[:, 4 * P:4 * P + 1],
                         rhs=sqq[:, 512:1024], start=False,
                         stop=True).then_inc(pe, 1)
            em.sec("tensor", mmq)
            mq = em.inc("pe")

            def chain(v, mq=mq, h=h):
                v.wait_ge(pe, mq)
                v.reduce_max(out=nrm[0:1, 5:6], in_=pnq[0:1, 512:1024],
                             axis=AX.X).then_inc(dv, 1)
                v.wait_ge(dv, em.inc("dv"))
                kc = kcol(h)
                v.tensor_tensor(out=nrm[0:1, 6:7], in0=nrm[0:1, 5:6],
                                in1=nrm[0:1, kc:kc + 1],
                                op=ALU.mult).then_inc(dv, 1)
                em.inc("dv")
                v.wait_ge(dv, em.val("dv"))
            em.sec("vector", chain)
            w2 = em.val("dv")

            def lnexp(sc, w2=w2):
                sc.wait_ge(dv, w2)
                sc.activation(out=nrm[0:1, 7:8], in_=nrm[0:1, 6:7],
                              func=AF.Ln, bias=0.0, scale=1.0).then_inc(ac, 1)
                sc.wait_ge(ac, em.inc("ac"))
                sc.activation(out=nrm[0:1, 8:9], in_=nrm[0:1, 7:8],
                              func=AF.Exp, bias=0.0, scale=0.5).then_inc(ac, 1)
            em.sec("scalar", lnexp)
            w3 = em.inc("ac")

            def neg(v, w3=w3, pw=em.val("pe")):
                v.wait_ge(ac, w3)
                v.wait_ge(pe, pw)
                v.tensor_scalar(out=nbf[0:1, 0:1], in0=nrm[0:1, 8:9],
                                scalar1=-1.0, scalar2=None,
                                op0=ALU.mult).then_inc(dv, 1)
            em.sec("vector", neg)
            nb_dv[qg] = em.inc("dv")

        def emit_negb_bcast(qg, prev_ac):
            def bcmm(t, qg=qg, prev_ac=prev_ac):
                t.wait_ge(dv, nb_dv[qg])
                if prev_ac:
                    t.wait_ge(ac, prev_ac)
                t.matmul(out=pbc[:, 0:1],
                         lhsT=mko[0:1, 4 * P:5 * P],
                         rhs=nbf[0:1, 0:1],
                         start=True, stop=True).then_inc(pe, 1)
            em.sec("tensor", bcmm)
            bm = em.inc("pe")

            def cpy(sc, qg=qg, bm=bm):
                sc.wait_ge(pe, bm)
                sc.activation(out=negb[:, qg:qg + 1],
                              in_=pbc[:, 0:1],
                              func=AF.Copy, bias=0.0, scale=1.0).then_inc(ac, 1)
            em.sec("scalar", cpy)
            negb_ac[qg] = em.inc("ac")

        emit_norm_k(0)
        emit_norm_q(0)
        emit_negb_bcast(0, 0)
        for h in range(NKVH):
            for qh in range(4):
                qg = 4 * h + qh
                for kt in range(32):
                    g, m = kt // 4, kt % 4
                    a0 = g * P
                    segs = ([(a0, 512)] if g < 4 else []) + \
                           [(max(512, a0), 1024)]
                    sp = kt % 2

                    def score(t, h=h, qg=qg, kt=kt, segs=segs,
                              kw=kload[h], qw=qload[qg],
                              bw=exp_ac.get((qg, kt - 1),
                                            exp_ac.get((qg - 1, 31), 0))):
                        t.wait_ge(*kw)
                        t.wait_ge(*qw)
                        if bw:
                            t.wait_ge(ac, bw)
                        for si, (a, b) in enumerate(segs):
                            t.matmul(out=py0[:, a:b],
                                     lhsT=kbuf[h % 2][:, bass.ts(kt, P)],
                                     rhs=qbuf[qg % 2][:, a:b],
                                     start=True, stop=True).then_inc(pe, 1)
                    em.sec("tensor", score)
                    sc_pe = em.inc("pe", len(segs))
                    qfree_pe[qg] = sc_pe

                    prv = (qg, kt - 2) if kt >= 2 else (qg - 1, 30 + kt)

                    def expf(sc, qg=qg, kt=kt, segs=segs, a0=a0, sp=sp,
                             sc_pe=sc_pe, pvw=pv_pe.get(prv, 0),
                             adw=add_dv.get(prv, 0)):
                        sc.wait_ge(pe, sc_pe)
                        sc.wait_ge(ac, negb_ac[qg])
                        if pvw:
                            sc.wait_ge(pe, pvw)
                        if adw:
                            sc.wait_ge(dv, adw)
                        for (a, b) in segs:
                            sc.activation(out=st[sp][:, a - a0:b - a0],
                                          in_=py0[:, a:b], func=AF.Exp,
                                          bias=negb[:, qg:qg + 1],
                                          scale=1.0).then_inc(ac, 1)
                    em.sec("scalar", expf)
                    exp_ac[(qg, kt)] = em.inc("ac", len(segs))

                    def maskf(v, qg=qg, kt=kt, m=m, sp=sp):
                        v.wait_ge(ac, exp_ac[(qg, kt)])
                        v.tensor_tensor(out=st[sp][:, 0:P],
                                        in0=st[sp][:, 0:P],
                                        in1=mko[:, m * P:(m + 1) * P],
                                        op=ALU.mult).then_inc(dv, 1)
                    em.sec("vector", maskf)
                    mask_dv[(qg, kt)] = em.inc("dv")

                    def adds(v, qg=qg, kt=kt, a0=a0, sp=sp):
                        if kt == 0:
                            v.tensor_copy(out=acc[:, 0:1024],
                                          in_=st[0][:, 0:1024]).then_inc(dv, 1)
                        else:
                            v.tensor_tensor(out=acc[:, a0:1024],
                                            in0=acc[:, a0:1024],
                                            in1=st[sp][:, 0:1024 - a0],
                                            op=ALU.add).then_inc(dv, 1)
                    em.sec("vector", adds)
                    add_dv[(qg, kt)] = em.inc("dv")

                    def pvf(t, h=h, qg=qg, kt=kt, segs=segs, a0=a0, sp=sp,
                            evw=evac_dv.get(qg - 1, 0)):
                        t.wait_ge(dv, mask_dv[(qg, kt)])
                        if kt == 0 and evw:
                            t.wait_ge(dv, evw)
                        for (a, b) in segs:
                            stop = (kt == 15 and a < 512) or \
                                   (kt == 31 and a >= 512)
                            mm = t.matmul(out=py1[:, a:b],
                                          lhsT=vbuf[h % 2][:, bass.ts(kt, P)],
                                          rhs=st[sp][:, a - a0:b - a0],
                                          start=(kt == 0), stop=stop)
                            if stop or (a, b) == segs[-1]:
                                mm.then_inc(pe, 1)
                                em.inc("pe")
                    em.sec("tensor", pvf)
                    pv_pe[(qg, kt)] = em.val("pe")
                    kfree_pe[h] = pv_pe[(qg, kt)]

                def accbf(v):
                    v.tensor_copy(out=accb[:], in_=acc[:]).then_inc(dv, 1)
                em.sec("vector", accbf)
                accb_dv = em.inc("dv")

                def denf(t, qg=qg, accb_dv=accb_dv):
                    t.wait_ge(dv, accb_dv)
                    t.wait_ge(ac, negb_ac[qg])
                    t.matmul(out=pbc[0:1, 0:512],
                             lhsT=mko[:, 4 * P:4 * P + 1],
                             rhs=accb[:, 0:512],
                             start=True, stop=True).then_inc(pe, 1)
                    t.matmul(out=pbc[0:1, 512:1024],
                             lhsT=mko[:, 4 * P:4 * P + 1],
                             rhs=accb[:, 512:1024],
                             start=True, stop=True).then_inc(pe, 1)
                em.sec("tensor", denf)
                den_pe = em.inc("pe", 2)

                def recipf(v, den_pe=den_pe):
                    v.wait_ge(pe, den_pe)
                    with nc.allow_low_precision(reason="bf16 1/den as before"):
                        v.reciprocal(out=rsb[0:1, 0:512],
                                     in_=pbc[0:1, 0:512]).then_inc(dv, 1)
                        v.reciprocal(out=rsb[0:1, 512:1024],
                                     in_=pbc[0:1, 512:1024]).then_inc(dv, 1)
                em.sec("vector", recipf)
                rsb_dv = em.inc("dv", 2)

                def bcf(t, rsb_dv=rsb_dv):
                    t.wait_ge(dv, rsb_dv)
                    t.matmul(out=pbc[:, 0:512],
                             lhsT=mko[0:1, 4 * P:5 * P],
                             rhs=rsb[0:1, 0:512],
                             start=True, stop=True).then_inc(pe, 1)
                    t.matmul(out=pbc[:, 512:1024],
                             lhsT=mko[0:1, 4 * P:5 * P],
                             rhs=rsb[0:1, 512:1024],
                             start=True, stop=True).then_inc(pe, 1)
                em.sec("tensor", bcf)
                bc_pe = em.inc("pe", 2)

                def rbf(sc, qg=qg, bc_pe=bc_pe,
                        rw=evac_dv.get(qg - 1, 0)):
                    sc.wait_ge(pe, bc_pe)
                    if rw:
                        sc.wait_ge(dv, rw)
                    sc.activation(out=rb[:, 0:512], in_=pbc[:, 0:512],
                                  func=AF.Copy, bias=0.0,
                                  scale=1.0).then_inc(ac, 1)
                    sc.activation(out=rb[:, 512:1024], in_=pbc[:, 512:1024],
                                  func=AF.Copy, bias=0.0,
                                  scale=1.0).then_inc(ac, 1)
                em.sec("scalar", rbf)
                rb_ac[qg] = em.inc("ac", 2)

                def evf(v, qg=qg, pvw=pv_pe[(qg, 31)]):
                    v.wait_ge(ac, rb_ac[qg])
                    v.wait_ge(pe, pvw)
                    v.tensor_tensor(out=attn[:, qg * NTOK:(qg + 1) * NTOK],
                                    in0=py1[:, 0:1024], in1=rb[:],
                                    op=ALU.mult).then_inc(dv, 1)
                em.sec("vector", evf)
                evac_dv[qg] = em.inc("dv")

                if qg + 1 < NQH:
                    if (qg + 1) % 4 == 0:
                        emit_norm_k(h + 1)
                    emit_norm_q(qg + 1)
                    emit_negb_bcast(qg + 1, rb_ac[qg])

                if qg + 2 < NQH:
                    def pre_q(s, qg=qg, w=qfree_pe[qg]):
                        s.wait_ge(pe, w)
                        load_q(s, qg + 2)
                    em.sec("sync", pre_q)
            if h + 2 < NKVH:
                def pre_kv(g, h=h, w=kfree_pe[h]):
                    g.wait_ge(pe, w)
                    load_kv(g, h + 2)
                em.sec("gpsimd", pre_kv)

        attn_done = em.val("dv")

        woload = {}
        xload2 = {}
        ydone_dv = {}
        ystore_so = {}

        for oq in range(4):
            def wol(g, oq=oq, w=em.val("pe")):
                if oq == 0:
                    g.wait_ge(dv, attn_done)
                else:
                    g.wait_ge(pe, w)
                em.dma(g, "wol", wob[:].rearrange("p (c o) -> p c o", o=1024),
                       wo[:, oq * 1024:(oq + 1) * 1024].rearrange(
                           "(c p) o -> p c o", p=P))
            em.sec("gpsimd", wol)
            woload[oq] = em.wv("wol")

            for sl in range(NT):
                def xslice(s, oq=oq, sl=sl,
                           w=ydone_dv.get((oq, sl - 2),
                                          ydone_dv.get((oq - 1, sl + NT - 2), 0))):
                    if w:
                        s.wait_ge(dv, w)
                    em.dma(s, f"xs{sl % 2}", xb[sl % 2][:],
                           x[bass.ts(sl, P), oq * 1024:(oq + 1) * 1024])
                em.sec("sync", xslice)
                xload2[(oq, sl)] = em.wv(f"xs{sl % 2}")

                pyx = py0 if sl % 2 == 0 else py1

                def womm(t, oq=oq, sl=sl, pyx=pyx,
                         w=ydone_dv.get((oq, sl - 2),
                                        ydone_dv.get((oq - 1, sl + NT - 2), 0)),
                         ww_=woload[oq]):
                    t.wait_ge(*ww_)
                    if w:
                        t.wait_ge(dv, w)
                    for c in range(NCHUNK):
                        for half in range(2):
                            mm = t.matmul(
                                out=pyx[:, half * 512:half * 512 + 512],
                                lhsT=attn[:, c * NTOK + sl * P:
                                          c * NTOK + (sl + 1) * P],
                                rhs=wob[:, c * 1024 + half * 512:
                                        c * 1024 + (half + 1) * 512],
                                start=(c == 0), stop=(c == NCHUNK - 1))
                            if c == NCHUNK - 1:
                                mm.then_inc(pe, 1)
                em.sec("tensor", womm)
                wmm = em.inc("pe", 2)

                def yev(v, oq=oq, sl=sl, pyx=pyx, wmm=wmm,
                        sw=ystore_so.get((oq, sl - 2),
                                         ystore_so.get((oq - 1, sl + NT - 2))),
                        xw_=xload2[(oq, sl)]):
                    v.wait_ge(pe, wmm)
                    v.wait_ge(*xw_)
                    if sw:
                        v.wait_ge(*sw)
                    v.tensor_tensor(out=yb[sl % 2][:], in0=pyx[:],
                                    in1=xb[sl % 2][:],
                                    op=ALU.add).then_inc(dv, 1)
                em.sec("vector", yev)
                ydone_dv[(oq, sl)] = em.inc("dv")

                def yst(s, oq=oq, sl=sl, w=ydone_dv[(oq, sl)]):
                    s.wait_ge(dv, w)
                    em.dma(s, f"y{sl % 2}",
                           y[bass.ts(sl, P), oq * 1024:(oq + 1) * 1024],
                           yb[sl % 2][:])
                em.sec("sync", yst)
                ystore_so[(oq, sl)] = em.wv(f"y{sl % 2}")

        def fin(s, w0=em.wv("y0"), w1=em.wv("y1")):
            s.wait_ge(*w0)
            s.wait_ge(*w1)
        em.sec("sync", fin)
    return nc


def _prep_attn(inputs_x_own, qf, kf_all, vt_all, wo_natural, j):
    return {
        "qf": qf, "kf": kf_all, "vtok": vt_all, "x": inputs_x_own,
        "wo": wo_natural, "masks": _masks(j), "ones": np.ones((P, P), BF),
    }



def build_ffn0():
    nc = bass.Bass("TRN2", target_bir_lowering=False, debug=False)
    x = nc.dram_tensor("x", [NTOK, D], F32, kind="ExternalInput").ap()
    gate_p = nc.dram_tensor("gate_p", [2048, 11008], I32, kind="ExternalInput").ap()
    up_p = nc.dram_tensor("up_p", [2048, 11008], I32, kind="ExternalInput").ap()
    down_p = nc.dram_tensor("down_p", [5504, D], I32, kind="ExternalInput").ap()
    sg = nc.dram_tensor("sg", [64, 11008], F32, kind="ExternalInput").ap()
    zg = nc.dram_tensor("zg", [64, 11008], F32, kind="ExternalInput").ap()
    su = nc.dram_tensor("su", [64, 11008], F32, kind="ExternalInput").ap()
    zu = nc.dram_tensor("zu", [64, 11008], F32, kind="ExternalInput").ap()
    sd = nc.dram_tensor("sd", [43, 4, D], F32, kind="ExternalInput").ap()
    zd = nc.dram_tensor("zd", [43, 4, D], F32, kind="ExternalInput").ap()
    ind4 = nc.dram_tensor("ind4", [P, 4], BF16, kind="ExternalInput").ap()
    ind4t = nc.dram_tensor("ind4t", [4, P], BF16, kind="ExternalInput").ap()
    ident = nc.dram_tensor("ident", [P, P], BF16, kind="ExternalInput").ap()
    y = nc.dram_tensor("y", [D, NTOK], BF16, kind="ExternalOutput").ap()

    h_dram = nc.dram_tensor("h_dram", [2, 43, P, NTOK], BF16, kind="Internal").ap()
    wd_dram = nc.dram_tensor("wd_dram", [2, 43, P, D], BF16, kind="Internal").ap()
    hs_dram = nc.dram_tensor("hs_dram", [176, NTOK], F32, kind="Internal").ap()
    zsd_dram = nc.dram_tensor("zsd_dram", [176, D], F32, kind="Internal").ap()

    NIB = 43

    with ExitStack() as stk:
        em = Emit(nc, stk)
        ld, lg, pe, dv, ac, so = (em.sems[k] for k in
                                  ("ld", "lg", "pe", "dv", "ac", "so"))
        pg = stk.enter_context(nc.psum_tensor("pg", [P, 1024]))
        pu = stk.enter_context(nc.psum_tensor("pu", [P, 1024]))
        pH = stk.enter_context(nc.psum_tensor("pH", [4, 1024]))
        ptp = stk.enter_context(nc.psum_tensor("ptp", [P, 1024], BF16))
        pse = stk.enter_context(nc.psum_tensor("pse", [P, 512]))
        idn = sb_tiles(nc, stk, "idn", [P, P], BF16)
        i4 = sb_tiles(nc, stk, "i4", [P, 4], BF16)
        i4t = sb_tiles(nc, stk, "i4t", [4, P], BF16)
        epsv = sb_tiles(nc, stk, "epsv", [P, 1])

        def c_loads(s):
            em.dma(s, "cst", idn[:], ident)
            em.dma(s, "cst", i4[:], ind4)
            em.dma(s, "cst", i4t[:], ind4t)
        em.sec("sync", c_loads)
        cstw = em.wv("cst")

        mm_pe = {}
        hst_so = {}
        Hst_so = {}

        with ExitStack() as ska:
            xt = [sb_tiles(nc, ska, f"xt{i}", [P, D]) for i in range(2)]
            xnt = sb_tiles(nc, ska, "xnt", [P, D], BF16)
            xnE = sb_tiles(nc, ska, "xnE", [P, 16 * NTOK], BF16)
            xnO = sb_tiles(nc, ska, "xnO", [P, 16 * NTOK], BF16)
            bns = sb_tiles(nc, ska, "bns", [P, 48])
            bna = sb_tiles(nc, ska, "bna", [P, 2])
            lnv = sb_tiles(nc, ska, "lnv", [P, 1])
            negm = sb_tiles(nc, ska, "negm", [P, 1])
            rstd = sb_tiles(nc, ska, "rstd", [P, 1])
            xg = sb_tiles(nc, ska, "xg", [P, 64], BF16)
            Gf = sb_tiles(nc, ska, "Gf", [64, NTOK], BF16)
            sgl = [sb_tiles(nc, ska, f"sgl{i}", [64, 512], BF16) for i in range(2)]
            zgl = [sb_tiles(nc, ska, f"zgl{i}", [64, 512], BF16) for i in range(2)]
            zsl = [sb_tiles(nc, ska, f"zsl{i}", [64, 512], BF16) for i in range(2)]
            sesl = sb_tiles(nc, ska, "sesl", [P, 512], BF16)
            gwb = [sb_tiles(nc, ska, f"gwb{i}", [P, 4096], BF16) for i in range(2)]
            uwb = [sb_tiles(nc, ska, f"uwb{i}", [P, 4096], BF16) for i in range(2)]
            t1 = sb_tiles(nc, ska, "t1", [P, 2048])
            him = sb_tiles(nc, ska, "him", [P, 2048], BF16)
            Hg = sb_tiles(nc, ska, "Hg", [P, 4096], BF16)
            Lg = sb_tiles(nc, ska, "Lg", [P, 4096], BF16)
            Hu = sb_tiles(nc, ska, "Hu", [P, 4096], BF16)
            Lu = sb_tiles(nc, ska, "Lu", [P, 4096], BF16)
            sgb = sb_tiles(nc, ska, "sgb", [P, NTOK], BF16)
            hb = sb_tiles(nc, ska, "hb", [P, NTOK], BF16)
            htmpb = sb_tiles(nc, ska, "htmpb", [4, NTOK])

            def zp0(v):
                v.memset(epsv[:], EPS)
                v.memset(htmpb[:], 0.0)
                v.memset(t1[0:4, :], 0.0).then_inc(dv, 1)
            em.sec("vector", zp0)
            zp_dv = em.inc("dv")

            def zpad_st(s, w=zp_dv):
                s.wait_ge(dv, w)
                em.dma(s, "zp", hs_dram[172:176, :], htmpb[:])
                em.dma(s, "zp", zsd_dram[172:176, 0:2048], t1[0:4, :])
                em.dma(s, "zp", zsd_dram[172:176, 2048:4096], t1[0:4, :])
            em.sec("sync", zpad_st)
            zpad_so = em.wv("zp")

            xload = []

            def load_x(s, tt):
                em.dma(s, f"x{tt % 2}", xt[tt % 2][:], x[bass.ts(tt, P), :])
                xload.append(em.wv(f"x{tt % 2}"))

            em.sec("sync", lambda s: (load_x(s, 0), load_x(s, 1)))

            for tt in range(NT):
                buf = xt[tt % 2]

                def stats(v, tt=tt, buf=buf, w=xload[tt]):
                    v.wait_ge(*w)
                    for hh in range(8):
                        i_ = v.bn_stats(out=bns[:, 6 * hh:6 * hh + 6],
                                        in_=buf[:, 512 * hh:512 * hh + 512])
                    i_.then_inc(dv, 1)
                    v.wait_ge(dv, em.inc("dv"))
                    v.bn_aggr(out=bna[:], in_=bns[:]).then_inc(dv, 1)
                    v.wait_ge(dv, em.inc("dv"))
                    v.tensor_scalar(out=negm[:], in0=bna[:, 0:1], scalar1=-1.0,
                                    scalar2=None, op0=ALU.mult).then_inc(dv, 1)
                em.sec("vector", stats)
                sdv = em.inc("dv")

                def rstd_f(sc, sdv=sdv):
                    sc.wait_ge(dv, sdv)
                    sc.activation(out=lnv[:], in_=bna[:, 1:2], func=AF.Ln,
                                  bias=epsv[:, 0:1], scale=1.0).then_inc(ac, 1)
                    sc.wait_ge(ac, em.inc("ac"))
                    sc.activation(out=rstd[:], in_=lnv[:], func=AF.Exp,
                                  bias=0.0, scale=-0.5).then_inc(ac, 1)
                em.sec("scalar", rstd_f)
                sac = em.inc("ac")

                def norm(v, buf=buf, sac=sac):
                    v.wait_ge(ac, sac)
                    v.tensor_scalar(out=xnt[:], in0=buf[:], scalar1=negm[:, 0:1],
                                    scalar2=rstd[:, 0:1], op0=ALU.add,
                                    op1=ALU.mult).then_inc(dv, 1)
                    v.wait_ge(dv, em.inc("dv"))
                    v.reduce_sum(out=t1[:, 0:64],
                                 in_=xnt[:].rearrange("p (g k) -> p g k", g=64),
                                 axis=AX.X).then_inc(dv, 1)
                    v.wait_ge(dv, em.inc("dv"))
                    v.tensor_copy(out=xg[:], in_=t1[:, 0:64]).then_inc(dv, 1)
                em.sec("vector", norm)
                norm_dv = em.inc("dv")

                if tt + 2 < NT:
                    def prefetch(s, tt=tt, w=norm_dv):
                        s.wait_ge(dv, w)
                        load_x(s, tt + 2)
                    em.sec("sync", prefetch)

                def gtr(t, w=norm_dv, cw=cstw):
                    t.wait_ge(*cw)
                    t.wait_ge(dv, w)
                    t.transpose(out=ptp[0:64, 0:P], in_=xg[:],
                                identity=idn[:]).then_inc(pe, 1)
                em.sec("tensor", gtr)
                gpe = em.inc("pe")

                def gcp(v, tt=tt, gpe=gpe):
                    v.wait_ge(pe, gpe)
                    v.tensor_copy(out=Gf[:, bass.ts(tt, P)],
                                  in_=ptp[0:64, 0:P]).then_inc(dv, 1)
                em.sec("vector", gcp)
                gcp_dv = em.inc("dv")

                for c in range(16):
                    def ptr(t, c=c, w=gcp_dv, base=em.val("dv")):
                        t.wait_ge(dv, w if c == 0 else base)
                        for hh in range(2):
                            for par in range(2):
                                off = 4 * c + 2 * hh + par
                                t.transpose(
                                    out=ptp[0:64, (2 * hh + par) * P:
                                            (2 * hh + par + 1) * P],
                                    in_=xnt[:, off:D:64],
                                    identity=idn[:]).then_inc(pe, 1)
                    em.sec("tensor", ptr)
                    tb = em.val("pe")
                    em.inc("pe", 4)

                    def pcp(v, tt=tt, c=c, tb=tb):
                        v.wait_ge(pe, tb + 4)
                        for hh in range(2):
                            for par in range(2):
                                dst = xnE if par == 0 else xnO
                                i_ = v.tensor_copy(
                                    out=dst[64 * hh:64 * hh + 64,
                                            c * NTOK + tt * P:
                                            c * NTOK + (tt + 1) * P],
                                    in_=ptp[0:64, (2 * hh + par) * P:
                                            (2 * hh + par + 1) * P])
                        i_.then_inc(dv, 1)
                    em.sec("vector", pcp)
                    em.inc("dv")
            ln_done = em.val("dv")

            wload = {}
            sload = {}

            def wl(g, ib):
                sn = f"w{ib % 2}"
                em.dma(g, sn, gwb[ib % 2][:].rearrange("p (c i) -> p c i", i=256),
                       gate_p[:, 256 * ib:256 * (ib + 1)].rearrange(
                           "(c p) i -> p c i", p=P))
                em.dma(g, sn, uwb[ib % 2][:].rearrange("p (c i) -> p c i", i=256),
                       up_p[:, 256 * ib:256 * (ib + 1)].rearrange(
                           "(c p) i -> p c i", p=P))
                sl = sgl[ib % 2]
                zl_ = zgl[ib % 2]
                em.dma(g, sn, sl[:, 0:256], sg[:, 256 * ib:256 * (ib + 1)])
                em.dma(g, sn, sl[:, 256:512], su[:, 256 * ib:256 * (ib + 1)])
                em.dma(g, sn, zl_[:, 0:256], zg[:, 256 * ib:256 * (ib + 1)])
                em.dma(g, sn, zl_[:, 256:512], zu[:, 256 * ib:256 * (ib + 1)])
                wload[ib] = em.wv(sn)

            em.sec("gpsimd", lambda g: (wl(g, 0), wl(g, 1)))

            deq_dv = {}
            zsl_dv = {}
            ev_dv = {}
            ev_ac = {}

            for ib in range(NIB):
                def zslf(v, ib=ib, w=wload[ib]):
                    v.wait_ge(*w)
                    v.scalar_tensor_tensor(out=zsl[ib % 2][:],
                                           in0=zgl[ib % 2][:], scalar=-1.0,
                                           in1=sgl[ib % 2][:], op0=ALU.mult,
                                           op1=ALU.mult).then_inc(dv, 1)
                em.sec("vector", zslf)
                zsl_dv[ib] = em.inc("dv")

                for wi in range(2):
                    wb_ = gwb if wi == 0 else uwb
                    Hd = Hg if wi == 0 else Hu
                    Ld = Lg if wi == 0 else Lu

                    def sesl_f(v, ib=ib, wi=wi, w=wload[ib],
                               mmw=mm_pe.get(ib - 1, 0)):
                        v.wait_ge(*w)
                        if mmw:
                            v.wait_ge(pe, mmw)
                        src = sgl[ib % 2][:, wi * 256:wi * 256 + 256]
                        v.tensor_copy(out=sesl[0:64, 0:256], in_=src)
                        v.tensor_copy(out=sesl[64:128, 0:256],
                                      in_=src).then_inc(dv, 1)
                    em.sec("vector", sesl_f)
                    sw = em.inc("dv")

                    for hv in range(2):
                        HV = slice(hv * 2048, hv * 2048 + 2048)

                        def deq(v, ib=ib, wb_=wb_, sw=sw, HV=HV):
                            v.wait_ge(dv, sw)
                            v.tensor_scalar(out=t1[:], in0=wb_[ib % 2][:, HV],
                                            scalar1=1.0 / 16.0,
                                            scalar2=-0.46875, op0=ALU.mult,
                                            op1=ALU.add).then_inc(dv, 1)
                        em.sec("vector", deq)
                        d1 = em.inc("dv")

                        def deqa(sc, d1=d1):
                            sc.wait_ge(dv, d1)
                            sc.activation(out=him[:], in_=t1[:], func=AF.Copy,
                                          bias=129.0, scale=1.0).then_inc(ac, 1)
                        em.sec("scalar", deqa)
                        d2 = em.inc("ac")

                        def deq2(v, ib=ib, wb_=wb_, Hd=Hd, Ld=Ld, d2=d2,
                                 hv=hv, HV=HV):
                            v.wait_ge(ac, d2)
                            if ib == 0:
                                v.wait_ge(*zpad_so)
                            se3 = sesl[:, 0:256].unsqueeze(1).broadcast_to(
                                (P, 8, 256))
                            v.scalar_tensor_tensor(
                                out=Hd[:, HV].rearrange("p (c i) -> p c i", i=256),
                                in0=him[:].rearrange("p (c i) -> p c i", i=256),
                                scalar=-129.0, in1=se3, op0=ALU.add,
                                op1=ALU.mult)
                            v.scalar_tensor_tensor(out=t1[:], in0=him[:],
                                                   scalar=-16.0,
                                                   in1=wb_[ib % 2][:, HV],
                                                   op0=ALU.mult, op1=ALU.add)
                            v.scalar_tensor_tensor(
                                out=Ld[:, HV].rearrange("p (c i) -> p c i", i=256),
                                in0=t1[:].rearrange("p (c i) -> p c i", i=256),
                                scalar=2064.0, in1=se3, op0=ALU.add,
                                op1=ALU.mult).then_inc(dv, 1)
                        em.sec("vector", deq2)
                        deq_dv[(ib, wi)] = em.inc("dv")

                if ib + 2 < NIB:
                    def wpre(g, w=deq_dv[(ib, 1)]):
                        g.wait_ge(dv, w)
                        wl(g, ib + 2)
                    em.sec("gpsimd", wpre)

                for par in range(2):
                    for wi in range(2):
                        Hd = Hg if wi == 0 else Hu
                        Ld = Lg if wi == 0 else Lu
                        pdst = pg if wi == 0 else pu

                        def mm(t, ib=ib, par=par, wi=wi, Hd=Hd, Ld=Ld,
                               pdst=pdst, dw=deq_dv[(ib, wi)], lnw=ln_done,
                               zw=zsl_dv[ib],
                               evw=ev_dv.get((ib, par - 1),
                                             ev_dv.get((ib - 1, 1), 0)),
                               eaw=ev_ac.get((ib, par - 1),
                                             ev_ac.get((ib - 1, 1), 0))):
                            t.wait_ge(dv, dw)
                            t.wait_ge(dv, lnw)
                            t.wait_ge(dv, zw)
                            if evw:
                                t.wait_ge(dv, evw)
                            if eaw:
                                t.wait_ge(ac, eaw)
                            for c in range(16):
                                for HL, xn_ in ((Hd, xnE), (Ld, xnO)):
                                    lhsT = HL[:, c * 256 + par:(c + 1) * 256:2]
                                    st_ = (c == 0 and HL is Hd)
                                    t.matmul(out=pdst[:, 0:512], lhsT=lhsT,
                                             rhs=xn_[:, c * NTOK:c * NTOK + 512],
                                             start=st_, stop=False)
                                    t.matmul(out=pdst[:, 512:1024], lhsT=lhsT,
                                             rhs=xn_[:, c * NTOK + 512:
                                                     c * NTOK + 1024],
                                             start=st_, stop=False)
                            zl2 = zsl[ib % 2][:, wi * 256 + par:
                                              wi * 256 + 256:2]
                            if True:
                                t.matmul(out=pdst[:, 0:512], lhsT=zl2,
                                         rhs=Gf[:, 0:512], start=False,
                                         stop=True).then_inc(pe, 1)
                                t.matmul(out=pdst[:, 512:1024], lhsT=zl2,
                                         rhs=Gf[:, 512:1024], start=False,
                                         stop=True).then_inc(pe, 1)
                        em.sec("tensor", mm)
                        em.inc("pe", 2)
                    mm_pe[ib] = em.val("pe")
                    mmw = mm_pe[ib]

                    def silu_f(sc, w=mmw):
                        sc.wait_ge(pe, w)
                        sc.activation(out=sgb[:], in_=pg[:],
                                      func=AF.Silu).then_inc(ac, 1)
                    em.sec("scalar", silu_f)
                    sia = em.inc("ac")

                    def hmul(v, sia=sia, sw=hst_so.get((ib, par - 1),
                                                       hst_so.get((ib - 1, 1)))):
                        v.wait_ge(ac, sia)
                        if sw:
                            v.wait_ge(*sw)
                        v.tensor_tensor(out=hb[:], in0=sgb[:], in1=pu[:],
                                        op=ALU.mult).then_inc(dv, 1)
                    em.sec("vector", hmul)
                    ev_dv[(ib, par)] = em.inc("dv")
                    ev_ac[(ib, par)] = sia

                    if DBG_NO_HSUM:
                        hst2 = None

                    def hsum(t, ib=ib, par=par, w=ev_dv[(ib, par)],
                             hw=Hst_so.get(ib - 1)):
                        if DBG_NO_HSUM:
                            return
                        t.wait_ge(dv, w)
                        if par == 0 and hw:
                            t.wait_ge(*hw)
                        t.matmul(out=pH[:, 0:512], lhsT=i4[:],
                                 rhs=hb[:, 0:512], start=(par == 0),
                                 stop=(par == 1)).then_inc(pe, 1)
                        t.matmul(out=pH[:, 512:1024], lhsT=i4[:],
                                 rhs=hb[:, 512:1024], start=(par == 0),
                                 stop=(par == 1)).then_inc(pe, 1)
                    em.sec("tensor", hsum)
                    hsw = em.inc("pe", 2)

                    def hstore(s, ib=ib, par=par, w=ev_dv[(ib, par)]):
                        s.wait_ge(dv, w)
                        em.dma(s, "hst", h_dram[par, ib], hb[:])
                    em.sec("sync", hstore)
                    hst_so[(ib, par)] = em.wv("hst")

                    if par == 1:
                        def Hev(sc, w=hsw, zw=zpad_so):
                            if DBG_NO_HSUM:
                                return
                            sc.wait_ge(pe, w)
                            if zw:
                                sc.wait_ge(*zw)
                            sc.activation(out=htmpb[:], in_=pH[:],
                                          func=AF.Copy, bias=0.0,
                                          scale=1.0).then_inc(ac, 1)
                        em.sec("scalar", Hev)
                        hea = em.inc("ac")

                        def Hst(s, ib=ib, w=hea):
                            s.wait_ge(ac, w)
                            em.dma(s, "Hs", hs_dram[4 * ib:4 * ib + 4, :],
                                   htmpb[:])
                        em.sec("sync", Hst)
                        Hst_so[ib] = em.wv("Hs")

            gate_done_pe = em.val("pe")

        with ExitStack() as skb:
            dwb = [sb_tiles(nc, skb, f"dwb{i}", [P, D], BF16) for i in range(2)]
            sdsl = [sb_tiles(nc, skb, f"sdsl{i}", [4, D], BF16) for i in range(2)]
            zdsl = sb_tiles(nc, skb, "zdsl", [4, D])
            zdsl2 = sb_tiles(nc, skb, "zdsl2", [4, D])
            zsdsl = sb_tiles(nc, skb, "zsdsl", [4, D])
            sed = sb_tiles(nc, skb, "sed", [P, D], BF16)
            t1b = sb_tiles(nc, skb, "t1b", [P, D])
            himb = sb_tiles(nc, skb, "himb", [P, D], BF16)
            HdB = sb_tiles(nc, skb, "HdB", [P, D], BF16)
            LdB = sb_tiles(nc, skb, "LdB", [P, D], BF16)

            dload = {}
            zload = {}

            def dl(g, cd):
                sn = f"d{cd % 2}"
                em.dma(g, sn, dwb[cd % 2][:], down_p[bass.ts(cd, P), :])
                em.dma(g, sn, sdsl[cd % 2][:], sd[cd])
                dload[cd] = em.wv(sn)

            def dzl(g, cd):
                em.dma(g, "zl", zdsl[:], zd[cd])
                zload[cd] = em.wv("zl")

            def init_d(g, w=gate_done_pe, dw=em.val("dv"), aw=em.val("ac"),
                       h1=em.wv("hst"), h2=em.wv("Hs"), zp_=em.wv("zp")):
                g.wait_ge(pe, w)
                g.wait_ge(dv, dw)
                g.wait_ge(ac, aw)
                g.wait_ge(*h1)
                g.wait_ge(*h2)
                g.wait_ge(*zp_)
                dl(g, 0)
                dzl(g, 0)
                dl(g, 1)
            em.sec("gpsimd", init_d)

            dst_so = {}
            sed_ac = {}
            ddq = {}

            for cd in range(43):
                def zss(v, cd=cd, w=zload[cd], pw=dload[cd]):
                    v.wait_ge(*w)
                    v.wait_ge(*pw)
                    v.tensor_scalar(out=zdsl2[:], in0=zdsl[:], scalar1=-1.0,
                                    scalar2=8.0, op0=ALU.mult,
                                    op1=ALU.add).then_inc(dv, 1)
                    v.wait_ge(dv, em.inc("dv"))
                    v.tensor_tensor(out=zsdsl[:], in0=zdsl2[:],
                                    in1=sdsl[cd % 2][:],
                                    op=ALU.mult).then_inc(dv, 1)
                em.sec("vector", zss)
                zw = em.inc("dv")

                def zst(s, cd=cd, w=zw):
                    s.wait_ge(dv, w)
                    em.dma(s, "zs", zsd_dram[4 * cd:4 * cd + 4, :], zsdsl[:])
                em.sec("sync", zst)
                zso = em.wv("zs")

                if cd + 1 < 43:
                    def dz2(g, cd=cd, w=zso):
                        g.wait_ge(*w)
                        dzl(g, cd + 1)
                    em.sec("gpsimd", dz2)

                for hf in range(8):
                    def bch(t, cd=cd, hf=hf, w=dload[cd],
                            aw=sed_ac.get((cd, hf - 1),
                                          sed_ac.get((cd - 1, 7), 0)),
                            dq=ddq.get(cd - 1, 0), cw=cstw):
                        t.wait_ge(*w)
                        t.wait_ge(*cw)
                        if aw:
                            t.wait_ge(ac, aw)
                        if hf == 0 and dq:
                            t.wait_ge(dv, dq)
                        t.matmul(out=pse[:, 0:512],
                                 lhsT=i4t[:],
                                 rhs=sdsl[cd % 2][0:4,
                                                  hf * 512:hf * 512 + 512],
                                 start=True, stop=True).then_inc(pe, 1)
                    em.sec("tensor", bch)
                    bw = em.inc("pe")

                    def sevac(sc, cd=cd, hf=hf, w=bw):
                        sc.wait_ge(pe, w)
                        sc.activation(out=sed[:, hf * 512:hf * 512 + 512],
                                      in_=pse[:, 0:512],
                                      func=AF.Copy, bias=0.0,
                                      scale=1.0).then_inc(ac, 1)
                    em.sec("scalar", sevac)
                    sed_ac[(cd, hf)] = em.inc("ac")

                def ddeq(v, cd=cd, w=dload[cd], stw=dst_so.get(cd - 1)):
                    v.wait_ge(*w)
                    if stw:
                        v.wait_ge(*stw)
                    v.tensor_scalar(out=t1b[:], in0=dwb[cd % 2][:],
                                    scalar1=1.0 / 16.0, scalar2=-0.46875,
                                    op0=ALU.mult, op1=ALU.add).then_inc(dv, 1)
                em.sec("vector", ddeq)
                dd1 = em.inc("dv")

                def ddeqa(sc, dd1=dd1):
                    sc.wait_ge(dv, dd1)
                    sc.activation(out=himb[:], in_=t1b[:], func=AF.Copy,
                                  bias=129.0, scale=1.0).then_inc(ac, 1)
                em.sec("scalar", ddeqa)
                dd2 = em.inc("ac")

                def ddeq2(v, cd=cd, dd2=dd2, saw=sed_ac[(cd, 7)]):
                    v.wait_ge(ac, dd2)
                    v.wait_ge(ac, saw)
                    v.scalar_tensor_tensor(out=HdB[:], in0=himb[:],
                                           scalar=-137.0, in1=sed[:],
                                           op0=ALU.add, op1=ALU.mult)
                    v.scalar_tensor_tensor(out=t1b[:], in0=himb[:],
                                           scalar=-16.0, in1=dwb[cd % 2][:],
                                           op0=ALU.mult, op1=ALU.add)
                    v.scalar_tensor_tensor(out=LdB[:], in0=t1b[:],
                                           scalar=2056.0, in1=sed[:],
                                           op0=ALU.add,
                                           op1=ALU.mult).then_inc(dv, 1)
                em.sec("vector", ddeq2)
                dqw = em.inc("dv")
                ddq[cd] = dqw

                def dstore(s, cd=cd, w=dqw):
                    s.wait_ge(dv, w)
                    em.dma(s, "ws", wd_dram[0, cd], HdB[:])
                    em.dma(s, "ws", wd_dram[1, cd], LdB[:])
                em.sec("sync", dstore)
                dst_so[cd] = em.wv("ws")

                if cd + 2 < 43:
                    def dpre(g, cd=cd, w=dqw):
                        g.wait_ge(dv, w)
                        dl(g, cd + 2)
                    em.sec("gpsimd", dpre)
            down_deq_ws = em.wv("ws")
            down_deq_zs = em.wv("zs")

        with ExitStack() as skc:
            hres = sb_tiles(nc, skc, "hres", [P, 86 * 512], BF16)
            Hh = sb_tiles(nc, skc, "Hh", [88, 2 * 512])
            zsdh = sb_tiles(nc, skc, "zsdh", [88, 2 * D])
            dwc = [sb_tiles(nc, skc, f"dwc{i}", [P, 86 * P], BF16)
                   for i in range(2)]
            yo = [sb_tiles(nc, skc, f"yo{i}", [P, 512], BF16) for i in range(2)]

            def zsdh_load(g, w1=down_deq_ws, w2=down_deq_zs):
                g.wait_ge(*w1)
                g.wait_ge(*w2)
                em.dma(g, "zh", zsdh[:].rearrange("p (a n) -> p a n", n=D),
                       zsd_dram.rearrange("(a p) n -> p a n", a=2))
            em.sec("gpsimd", zsdh_load)
            zh_ld = em.wv("zh")

            hload = {}

            def h_load(g, nh):
                for a_ in range(2):
                    em.dma(g, "hl",
                           hres[:].rearrange("p (c n) -> p c n", n=512)[:, a_::2, :],
                           h_dram[a_].rearrange("c p n -> p c n")
                           [:, :, nh * 512:nh * 512 + 512])
                em.dma(g, "hl", Hh[:].rearrange("p (a n) -> p a n", n=512),
                       hs_dram.rearrange("(a p) n -> p a n", a=2)
                       [:, :, nh * 512:nh * 512 + 512])
                hload[nh] = em.wv("hl")

            wcl = {}

            def wc_load(g, ot):
                sn = f"wc{ot % 2}"
                for a_ in range(2):
                    em.dma(g, sn,
                           dwc[ot % 2][:].rearrange("p (c o) -> p c o",
                                                    o=P)[:, a_::2, :],
                           wd_dram[a_].rearrange("c p o -> p c o")
                           [:, :, bass.ts(ot, P)])
                wcl[ot] = em.wv(sn)

            def wc0(g, w1=down_deq_ws, hw=em.wv("hst"), Hw=em.wv("Hs")):
                g.wait_ge(*w1)
                g.wait_ge(*hw)
                g.wait_ge(*Hw)
                h_load(g, 0)
                wc_load(g, 0)
                wc_load(g, 1)
            em.sec("gpsimd", wc0)

            yev_dv = {}
            ymm_pe = {}
            yst_so = {}
            for nh in range(2):
                for oti in range(32):
                    ot = oti if nh == 0 else 31 - oti
                    key = (nh, ot)
                    pdst = pg if oti % 2 == 0 else pu

                    def ymm(t, nh=nh, ot=ot, oti=oti, pdst=pdst,
                            w=wcl[ot], bw=zh_ld, hw2=hload[nh]):
                        t.wait_ge(*w)
                        t.wait_ge(*bw)
                        t.wait_ge(*hw2)
                        kprev = (nh, ot - 2) if nh == 0 else (nh, ot + 2)
                        if kprev in yev_dv:
                            t.wait_ge(dv, yev_dv[kprev])
                        elif nh == 1 and (0, ot) in yev_dv and oti < 2:
                            t.wait_ge(dv, yev_dv[(0, ot)])
                        for cc in range(86):
                            t.matmul(out=pdst[:, 0:512],
                                     lhsT=dwc[ot % 2][:, bass.ts(cc, P)],
                                     rhs=hres[:, cc * 512:(cc + 1) * 512],
                                     start=(cc == 0), stop=False)
                        for a in range(2):
                            mmc = t.matmul(out=pdst[:, 0:512],
                                           lhsT=zsdh[:, a * D + ot * P:
                                                     a * D + (ot + 1) * P],
                                           rhs=Hh[:, a * 512:(a + 1) * 512],
                                           start=False, stop=(a == 1))
                            if a == 1:
                                mmc.then_inc(pe, 1)
                    em.sec("tensor", ymm)
                    ymm_pe[key] = em.inc("pe")

                    nxt = ot + 2 if nh == 0 else ot - 2
                    if 0 <= nxt < 32 and (nh == 0 or oti < 30):
                        def wcp(g, nxt=nxt, w=ymm_pe[key]):
                            g.wait_ge(pe, w)
                            wc_load(g, nxt)
                        em.sec("gpsimd", wcp)
                    if nh == 0 and oti == 30:
                        def hl2(g, w=ymm_pe[key]):
                            g.wait_ge(pe, w)
                        em.sec("gpsimd", hl2)

                    def yev(v, key=key, pdst=pdst, w=ymm_pe[key],
                            sw=yst_so.get((key[0], key[1] - 2 if nh == 0
                                           else key[1] + 2))):
                        v.wait_ge(pe, w)
                        if sw:
                            v.wait_ge(*sw)
                        v.tensor_copy(out=yo[key[1] % 2][:],
                                      in_=pdst[:, 0:512]).then_inc(dv, 1)
                    em.sec("vector", yev)
                    yev_dv[key] = em.inc("dv")

                    def yst(s, nh=nh, ot=ot, w=yev_dv[key]):
                        s.wait_ge(dv, w)
                        em.dma(s, f"ys{ot % 2}",
                               y[bass.ts(ot, P), nh * 512:nh * 512 + 512],
                               yo[ot % 2][:])
                    em.sec("sync", yst)
                    yst_so[key] = em.wv(f"ys{ot % 2}")

                if nh == 0:
                    def hswap(g, w=em.val("pe")):
                        g.wait_ge(pe, w)
                        h_load(g, 1)
                    em.sec("gpsimd", hswap)

            def fin(s, w0=em.wv("ys0"), w1=em.wv("ys1")):
                s.wait_ge(*w0)
                s.wait_ge(*w1)
            em.sec("sync", fin)
    return nc



def build_qkv_l1():
    nc = bass.Bass("TRN2", target_bir_lowering=False, debug=False)
    x = nc.dram_tensor("x", [NTOK, D], F32, kind="ExternalInput").ap()
    f0 = nc.dram_tensor("f0", [NTOK, D], BF16, kind="ExternalInput").ap()
    wq1 = nc.dram_tensor("wq1", [32, D, P], F32, kind="ExternalInput").ap()
    wkr = nc.dram_tensor("wkr", [4, D, P], F32, kind="ExternalInput").ap()
    wdk = nc.dram_tensor("wdk", [4, D, P], F32, kind="ExternalInput").ap()
    wupk = nc.dram_tensor("wupk", [4, 512, P], F32, kind="ExternalInput").ap()
    wupv = nc.dram_tensor("wupv", [8, 512, P], F32, kind="ExternalInput").ap()
    cq = nc.dram_tensor("cq", [64, NTOK], F32, kind="ExternalInput").ap()
    sq = nc.dram_tensor("sq", [64, NTOK], F32, kind="ExternalInput").ap()
    ck = nc.dram_tensor("ck", [64, NTOK], F32, kind="ExternalInput").ap()
    sk = nc.dram_tensor("sk", [64, NTOK], F32, kind="ExternalInput").ap()
    ident = nc.dram_tensor("ident", [P, P], BF16, kind="ExternalInput").ap()
    x2 = nc.dram_tensor("x2", [NTOK, D], F32, kind="ExternalOutput").ap()
    qf = nc.dram_tensor("qf", [D, NTOK], BF16, kind="ExternalOutput").ap()
    kf = nc.dram_tensor("kf", [KD, NTOK], BF16, kind="ExternalOutput").ap()
    vt = nc.dram_tensor("vt", [NTOK, KD], BF16, kind="ExternalOutput").ap()

    with ExitStack() as stk:
        em = Emit(nc, stk)
        pe, dv, ac = em.sems["pe"], em.sems["dv"], em.sems["ac"]

        xt = [sb_tiles(nc, stk, f"xt{i}", [P, D]) for i in range(2)]
        ft = [sb_tiles(nc, stk, f"ft{i}", [P, D], BF16) for i in range(2)]
        xnf = sb_tiles(nc, stk, "xnf", [P, NCHUNK * NTOK], BF16)
        xnt = sb_tiles(nc, stk, "xnt", [P, D], BF16)
        idn = sb_tiles(nc, stk, "idn", [P, P], BF16)
        bns = sb_tiles(nc, stk, "bns", [P, 48])
        bna = sb_tiles(nc, stk, "bna", [P, 2])
        lnv = sb_tiles(nc, stk, "lnv", [P, 1])
        negm = sb_tiles(nc, stk, "negm", [P, 1])
        rstd = sb_tiles(nc, stk, "rstd", [P, 1])
        epsv = sb_tiles(nc, stk, "epsv", [P, 1])
        wb = [sb_tiles(nc, stk, f"wb{i}", [P, D], BF16) for i in range(2)]
        cqs = sb_tiles(nc, stk, "cqs", [64, NTOK])
        sqs = sb_tiles(nc, stk, "sqs", [64, NTOK])
        cks = sb_tiles(nc, stk, "cks", [64, NTOK])
        sks = sb_tiles(nc, stk, "sks", [64, NTOK])
        rt1 = sb_tiles(nc, stk, "rt1", [64, NTOK])
        rt2 = sb_tiles(nc, stk, "rt2", [64, NTOK])
        rout = [sb_tiles(nc, stk, f"rout{i}", [P, NTOK], BF16) for i in range(2)]
        kfb = sb_tiles(nc, stk, "kfb", [P, 8 * NTOK], BF16)
        cbuf = sb_tiles(nc, stk, "cbuf", [P, 4 * NTOK], BF16)
        vf = xnf[:, 0:8 * NTOK]
        vts = xnf[:, 8 * NTOK:16 * NTOK]
        pt = [stk.enter_context(nc.psum_tensor(f"pt{i}", [P, 1024], BF16))
              for i in range(2)]
        pq = [stk.enter_context(nc.psum_tensor(f"pq{i}", [P, 1024]))
              for i in range(2)]

        xw, fw = {}, {}

        def load_x(s, tt):
            em.dma(s, f"x{tt % 2}", xt[tt % 2][:], x[bass.ts(tt, P), :])
            xw[tt] = em.wv(f"x{tt % 2}")

        def load_f(g, tt):
            em.dma(g, f"f{tt % 2}", ft[tt % 2][:], f0[bass.ts(tt, P), :])
            fw[tt] = em.wv(f"f{tt % 2}")

        def c_loads(s):
            em.dma(s, "cst", idn[:], ident)
            for t_, d_ in ((cqs, cq), (sqs, sq), (cks, ck), (sks, sk)):
                em.dma(s, "cst", t_[:], d_)
            load_x(s, 0)
            load_x(s, 1)
        em.sec("sync", c_loads)
        cstw = em.wv("cst")
        em.sec("gpsimd", lambda g: (load_f(g, 0), load_f(g, 1)))

        def init_eps(v):
            v.memset(epsv[:], EPS)
        em.sec("vector", init_eps)

        x2st = {}
        for tt in range(NT):
            buf = xt[tt % 2]

            def addres(v, tt=tt, buf=buf, w=xw[tt], w2=fw[tt]):
                v.wait_ge(*w)
                v.wait_ge(*w2)
                v.tensor_tensor(out=buf[:], in0=buf[:], in1=ft[tt % 2][:],
                                op=ALU.add).then_inc(dv, 1)
            em.sec("vector", addres)
            adv = em.inc("dv")

            def x2store(s, tt=tt, w=adv):
                s.wait_ge(dv, w)
                em.dma(s, f"sx{tt % 2}", x2[bass.ts(tt, P), :], xt[tt % 2][:])
            em.sec("sync", x2store)
            x2st[tt] = em.wv(f"sx{tt % 2}")

            def stats(v, tt=tt, buf=buf, w=adv):
                v.wait_ge(dv, w)
                for hh in range(8):
                    i_ = v.bn_stats(out=bns[:, 6 * hh:6 * hh + 6],
                                    in_=buf[:, 512 * hh:512 * hh + 512])
                i_.then_inc(dv, 1)
                v.wait_ge(dv, em.inc("dv"))
                v.bn_aggr(out=bna[:], in_=bns[:]).then_inc(dv, 1)
                v.wait_ge(dv, em.inc("dv"))
                v.tensor_scalar(out=negm[:], in0=bna[:, 0:1], scalar1=-1.0,
                                scalar2=None, op0=ALU.mult).then_inc(dv, 1)
            em.sec("vector", stats)
            sdv = em.inc("dv")

            def rstd_f(sc, sdv=sdv):
                sc.wait_ge(dv, sdv)
                sc.activation(out=lnv[:], in_=bna[:, 1:2], func=AF.Ln,
                              bias=epsv[:, 0:1], scale=1.0).then_inc(ac, 1)
                sc.wait_ge(ac, em.inc("ac"))
                sc.activation(out=rstd[:], in_=lnv[:], func=AF.Exp,
                              bias=0.0, scale=-0.5).then_inc(ac, 1)
            em.sec("scalar", rstd_f)
            sac = em.inc("ac")

            def norm(v, buf=buf, sac=sac, sw=x2st[tt]):
                v.wait_ge(ac, sac)
                v.wait_ge(*sw)
                v.tensor_scalar(out=xnt[:], in0=buf[:], scalar1=negm[:, 0:1],
                                scalar2=rstd[:, 0:1], op0=ALU.add,
                                op1=ALU.mult).then_inc(dv, 1)
            em.sec("vector", norm)
            norm_dv = em.inc("dv")

            if tt + 2 < NT:
                def prefetch(s, tt=tt, w=norm_dv):
                    s.wait_ge(dv, w)
                    load_x(s, tt + 2)
                em.sec("sync", prefetch)

                def prefetch2(g, tt=tt, w=norm_dv):
                    g.wait_ge(dv, w)
                    load_f(g, tt + 2)
                em.sec("gpsimd", prefetch2)

            def trans(t, w=norm_dv, base_dv=em.val("dv"), cw=cstw):
                t.wait_ge(*cw)
                t.wait_ge(dv, w)
                for c in range(NCHUNK):
                    if c >= 2:
                        t.wait_ge(dv, base_dv + c - 1)
                    t.transpose(out=pt[c % 2][:, 0:P], in_=xnt[:, bass.ts(c, P)],
                                identity=idn[:]).then_inc(pe, 1)
            em.sec("tensor", trans)
            tr_base = em.val("pe")
            em.inc("pe", NCHUNK)

            def evac(v, tt=tt, tr_base=tr_base):
                for c in range(NCHUNK):
                    v.wait_ge(pe, tr_base + c + 1)
                    v.tensor_copy(out=xnf[:, c * NTOK + tt * P:
                                          c * NTOK + (tt + 1) * P],
                                  in_=pt[c % 2][:, 0:P]).then_inc(dv, 1)
            em.sec("vector", evac)
            em.inc("dv", NCHUNK)
        ln_done = em.val("dv")

        ww = {}

        def wload_one(g, dram, idx, slot, key):
            em.dma(g, f"w{slot}",
                   wb[slot][:, 0:dram.shape[1] // P * P * (P // P)].rearrange(
                       "p (c m) -> p c m", m=P)[:, 0:dram.shape[1] // P, :],
                   dram[idx, :, :].rearrange("(c p) m -> p c m", p=P))
            ww[key] = em.wv(f"w{slot}")

        seq = []
        for mm_ in range(32):
            seq.append(("wq1", mm_, "q", mm_))
        for mm_ in range(4):
            seq.append(("wkr", mm_, "kr", mm_))
        for mm_ in range(4):
            seq.append(("wdk", mm_, "dk", mm_))
        for mm_ in range(4):
            seq.append(("wupk", mm_, "upk", mm_))
        for mm_ in range(8):
            seq.append(("wupv", mm_, "upv", mm_))
        drams = {"wq1": wq1, "wkr": wkr, "wdk": wdk, "wupk": wupk, "wupv": wupv}

        def emit_load(g, n, slot):
            nm, idx, kind, _ = seq[n]
            em.dma(g, f"w{slot}",
                   wb[slot][:].rearrange("p (c m) -> p c m", m=P)
                   [:, 0:drams[nm].shape[1] // P, :],
                   drams[nm][idx, :, :].rearrange("(c p) m -> p c m", p=P))
            ww[n] = em.wv(f"w{slot}")

        em.sec("gpsimd", lambda g: (emit_load(g, 0, 0), emit_load(g, 1, 1)))

        mm_pe = {}
        done_dv = {}
        stw = {}
        cb_dv = {}

        for n, (nm, idx, kind, oi) in enumerate(seq):
            slot = n % 2
            nchk = drams[nm].shape[1] // P

            def mm(t, n=n, slot=slot, kind=kind, nchk=nchk, w=ww[n],
                   pw=done_dv.get(n - 2, 0)):
                t.wait_ge(*w)
                t.wait_ge(dv, em.ln_done if False else ln_done)
                if pw:
                    t.wait_ge(dv, pw)
                for c in range(nchk):
                    last = c == nchk - 1
                    if kind in ("upk", "upv"):
                        rhs0 = cbuf[:, c * NTOK:c * NTOK + 512]
                        rhs1 = cbuf[:, c * NTOK + 512:c * NTOK + 1024]
                        if c == 0:
                            t.wait_ge(dv, cb_dv[3])
                    else:
                        rhs0 = xnf[:, c * NTOK:c * NTOK + 512]
                        rhs1 = xnf[:, c * NTOK + 512:c * NTOK + 1024]
                    i1 = t.matmul(out=pq[slot][:, 0:512],
                                  lhsT=wb[slot][:, bass.ts(c, P)],
                                  rhs=rhs0, start=(c == 0), stop=last)
                    i2 = t.matmul(out=pq[slot][:, 512:1024],
                                  lhsT=wb[slot][:, bass.ts(c, P)],
                                  rhs=rhs1, start=(c == 0), stop=last)
                    if last:
                        i1.then_inc(pe, 1)
                        i2.then_inc(pe, 1)
            em.sec("tensor", mm)
            mm_pe[n] = em.inc("pe", 2)

            if n + 2 < len(seq):
                def pre(g, n=n, slot=slot, w=mm_pe[n]):
                    g.wait_ge(pe, w)
                    emit_load(g, n + 2, slot)
                em.sec("gpsimd", pre)

            if kind == "q":
                def rope(v, n=n, slot=slot, oi=oi, tw=cstw,
                         sw=stw.get(n - 2)):
                    v.wait_ge(pe, mm_pe[n])
                    v.wait_ge(*tw)
                    if sw:
                        v.wait_ge(*sw)
                    p_ = pq[slot]
                    ro = rout[n % 2]
                    v.tensor_tensor(out=rt1[:], in0=p_[0:64, :], in1=cqs[:],
                                    op=ALU.mult)
                    v.tensor_tensor(out=rt2[:], in0=p_[64:128, :], in1=sqs[:],
                                    op=ALU.mult)
                    v.tensor_tensor(out=ro[0:64, :], in0=rt1[:], in1=rt2[:],
                                    op=ALU.subtract)
                    v.tensor_tensor(out=rt1[:], in0=p_[0:64, :], in1=sqs[:],
                                    op=ALU.mult)
                    v.tensor_tensor(out=rt2[:], in0=p_[64:128, :], in1=cqs[:],
                                    op=ALU.mult)
                    v.tensor_tensor(out=ro[64:128, :], in0=rt1[:], in1=rt2[:],
                                    op=ALU.add).then_inc(dv, 1)
                em.sec("vector", rope)
                done_dv[n] = em.inc("dv")

                def store(s, n=n, oi=oi):
                    s.wait_ge(dv, done_dv[n])
                    em.dma(s, f"st{n % 2}", qf[bass.ts(oi, P), :], rout[n % 2][:])
                em.sec("sync", store)
                stw[n] = em.wv(f"st{n % 2}")
            elif kind == "kr":
                def roper(v, n=n, slot=slot, oi=oi, tw=cstw):
                    v.wait_ge(pe, mm_pe[n])
                    v.wait_ge(*tw)
                    p_ = pq[slot]
                    for blk in range(2):
                        h = 2 * oi + blk
                        b0 = 64 * blk
                        dst = kfb[:, h * NTOK:(h + 1) * NTOK]
                        v.tensor_tensor(out=rt1[0:32, :], in0=p_[b0:b0 + 32, :],
                                        in1=cks[0:32, :], op=ALU.mult)
                        v.tensor_tensor(out=rt2[0:32, :],
                                        in0=p_[b0 + 32:b0 + 64, :],
                                        in1=sks[0:32, :], op=ALU.mult)
                        v.tensor_tensor(out=rt1[32:64, :], in0=p_[b0:b0 + 32, :],
                                        in1=sks[0:32, :], op=ALU.mult)
                        v.tensor_tensor(out=rt2[32:64, :],
                                        in0=p_[b0 + 32:b0 + 64, :],
                                        in1=cks[0:32, :], op=ALU.mult)
                        i_ = v.tensor_tensor(out=rt1[0:64, :].bitcast(F32),
                                             in0=rt1[0:64, :].bitcast(F32),
                                             in1=rt1[0:64, :].bitcast(F32),
                                             op=ALU.bypass) if False else None
                        v.tensor_sub(out=dst.bitcast(BF16)[0:32, :]
                                     if False else kfb[0:32, h * NTOK:(h + 1) * NTOK],
                                     in0=rt1[0:32, :], in1=rt2[0:32, :])
                        v.tensor_add(out=kfb[64:96, h * NTOK:(h + 1) * NTOK],
                                     in0=rt1[32:64, :], in1=rt2[32:64, :])
                    v.engine_nop().then_inc(dv, 1)
                em.sec("vector", roper)
                done_dv[n] = em.inc("dv")
            elif kind == "dk":
                def cevac(v, n=n, slot=slot, oi=oi):
                    v.wait_ge(pe, mm_pe[n])
                    v.tensor_copy(out=cbuf[:, bass.ts(oi, NTOK)],
                                  in_=pq[slot][:]).then_inc(dv, 1)
                em.sec("vector", cevac)
                done_dv[n] = em.inc("dv")
                cb_dv[oi] = done_dv[n]
            elif kind == "upk":
                def kevac(v, n=n, slot=slot, oi=oi):
                    v.wait_ge(pe, mm_pe[n])
                    p_ = pq[slot]
                    for blk in range(2):
                        h = 2 * oi + blk
                        v.tensor_copy(out=kfb[32:64, h * NTOK:(h + 1) * NTOK],
                                      in_=p_[64 * blk:64 * blk + 32, :])
                        i_ = v.tensor_copy(
                            out=kfb[96:128, h * NTOK:(h + 1) * NTOK],
                            in_=p_[64 * blk + 32:64 * blk + 64, :])
                    i_.then_inc(dv, 1)
                em.sec("vector", kevac)
                done_dv[n] = em.inc("dv")
            else:
                def vevac(v, n=n, slot=slot, oi=oi):
                    v.wait_ge(pe, mm_pe[n])
                    v.tensor_copy(out=vf[:, bass.ts(oi, NTOK)],
                                  in_=pq[slot][:]).then_inc(dv, 1)
                em.sec("vector", vevac)
                done_dv[n] = em.inc("dv")

        kv_done = em.val("dv")

        def kstore(s, w=kv_done):
            s.wait_ge(dv, w)
            em.dma(s, "stk", kf.rearrange("(h p) n -> p h n", p=P),
                   kfb[:].rearrange("p (h n) -> p h n", n=NTOK))
        em.sec("sync", kstore)

        for kk in range(8):
            def trans(t, kk=kk, base=em.val("dv"), w=kv_done):
                t.wait_ge(dv, w)
                t.wait_ge(dv, base)
                for tt in range(NT):
                    if tt >= 2:
                        t.wait_ge(dv, base + tt - 1)
                    t.transpose(out=pt[tt % 2][:, 0:P],
                                in_=vf[:, kk * NTOK + tt * P:
                                       kk * NTOK + (tt + 1) * P],
                                identity=idn[:]).then_inc(pe, 1)
            em.sec("tensor", trans)
            tb = em.val("pe")
            em.inc("pe", NT)

            def evacv(v, kk=kk, tb=tb):
                for tt in range(NT):
                    v.wait_ge(pe, tb + tt + 1)
                    v.tensor_copy(out=vts[:, tt * KD + kk * P:
                                          tt * KD + (kk + 1) * P],
                                  in_=pt[tt % 2][:, 0:P]).then_inc(dv, 1)
            em.sec("vector", evacv)
            em.inc("dv", NT)

        def store_v(s, w=em.val("dv")):
            s.wait_ge(dv, w)
            em.dma(s, "stv", vt.rearrange("(tt p) d -> p tt d", p=P),
                   vts[:].rearrange("p (tt d) -> p tt d", d=KD))
        em.sec("sync", store_v)

        def fin(s, ws=[em.wv(k) for k in
                       ("st0", "st1", "stk", "stv", "sx0", "sx1")]):
            for w in ws:
                s.wait_ge(*w)
        em.sec("sync", fin)
    return nc



def build_ffn1():
    nc = bass.Bass("TRN2", target_bir_lowering=False, debug=False)
    x = nc.dram_tensor("x", [NTOK, D], F32, kind="ExternalInput").ap()
    w_p = nc.dram_tensor("w_p", [2048, D], I32, kind="ExternalInput").ap()
    sc_in = nc.dram_tensor("sc_in", [32, D], F32, kind="ExternalInput").ap()
    ident = nc.dram_tensor("ident", [P, P], BF16, kind="ExternalInput").ap()
    y = nc.dram_tensor("y", [NTOK, D], F32, kind="ExternalOutput").ap()
    wl_dram = nc.dram_tensor("wl_dram", [2, 16, P, D], BF16, kind="Internal").ap()

    with ExitStack() as stk:
        em = Emit(nc, stk)
        pe, dv, ac = em.sems["pe"], em.sems["dv"], em.sems["ac"]

        xt = [sb_tiles(nc, stk, "xtS", [P, D])] * 2
        xnt = sb_tiles(nc, stk, "xnt", [P, D], BF16)
        xnE = sb_tiles(nc, stk, "xnE", [P, 16 * NTOK], BF16)
        xnO = sb_tiles(nc, stk, "xnO", [P, 16 * NTOK], BF16)
        idn = sb_tiles(nc, stk, "idn", [P, P], BF16)
        bns = sb_tiles(nc, stk, "bns", [P, 48])
        bna = sb_tiles(nc, stk, "bna", [P, 2])
        lnv = sb_tiles(nc, stk, "lnv", [P, 1])
        negm = sb_tiles(nc, stk, "negm", [P, 1])
        rstd = sb_tiles(nc, stk, "rstd", [P, 1])
        epsv = sb_tiles(nc, stk, "epsv", [P, 1])
        se1 = sb_tiles(nc, stk, "se1", [P, D], BF16)
        nsc = sb_tiles(nc, stk, "nsc", [32, D], BF16)
        xgs = sb_tiles(nc, stk, "xgs", [P, 32], BF16)
        XB = sb_tiles(nc, stk, "XB", [32, NTOK], BF16)
        wwb = [sb_tiles(nc, stk, "wwbS", [P, D], BF16)] * 2
        t1 = sb_tiles(nc, stk, "t1", [P, 2048])
        him = sb_tiles(nc, stk, "him", [P, 2048], BF16)
        wsl = [sb_tiles(nc, stk, f"wsl{i}", [P, 16 * 1024], BF16)
               for i in range(2)]
        xb = [sb_tiles(nc, stk, f"xbf{i}", [P, 1024]) for i in range(2)]
        yb = [sb_tiles(nc, stk, f"yb{i}", [P, 1024]) for i in range(2)]
        H1 = wsl[0][:][:, 0:D]
        L1b = wsl[1][:][:, 0:D]
        scb = xnE[:][0:32, 0:D]
        pg = stk.enter_context(nc.psum_tensor("pg", [P, 1024]))
        pu = stk.enter_context(nc.psum_tensor("pu", [P, 1024]))
        ptp = stk.enter_context(nc.psum_tensor("ptp", [P, 1024], BF16))

        def c_loads(s):
            em.dma(s, "cst", idn[:], ident)
        em.sec("sync", c_loads)
        cstw = em.wv("cst")

        def sc_load(g):
            em.dma(g, "csg", scb[:], sc_in)
        em.sec("gpsimd", sc_load)
        scw = em.wv("csg")

        def se_build(v, w=scw):
            v.memset(epsv[:], EPS)
            v.wait_ge(*w)
            for b4 in range(4):
                i_ = v.tensor_copy(out=se1[32 * b4:32 * b4 + 32, :], in_=scb[:])
            i_.then_inc(dv, 1)
            v.wait_ge(dv, em.inc("dv"))
            v.tensor_scalar(out=nsc[:], in0=scb[:], scalar1=-8.0,
                            scalar2=None, op0=ALU.mult).then_inc(dv, 1)
        em.sec("vector", se_build)
        se_dv = em.inc("dv")

        xload = []

        def load_x(s, tt):
            em.dma(s, "x0", xt[0][:], x[bass.ts(tt, P), :])
            xload.append(em.wv("x0"))
        em.sec("sync", lambda s: load_x(s, 0))

        xb_dv = {}
        for tt in range(NT):
            buf = xt[tt % 2]

            def stats(v, tt=tt, buf=buf, w=xload[tt]):
                v.wait_ge(*w)
                for hh in range(8):
                    i_ = v.bn_stats(out=bns[:, 6 * hh:6 * hh + 6],
                                    in_=buf[:, 512 * hh:512 * hh + 512])
                i_.then_inc(dv, 1)
                v.wait_ge(dv, em.inc("dv"))
                v.bn_aggr(out=bna[:], in_=bns[:]).then_inc(dv, 1)
                v.wait_ge(dv, em.inc("dv"))
                v.tensor_scalar(out=negm[:], in0=bna[:, 0:1], scalar1=-1.0,
                                scalar2=None, op0=ALU.mult).then_inc(dv, 1)
            em.sec("vector", stats)
            sdv = em.inc("dv")

            def rstd_f(sc2, sdv=sdv):
                sc2.wait_ge(dv, sdv)
                sc2.activation(out=lnv[:], in_=bna[:, 1:2], func=AF.Ln,
                               bias=epsv[:, 0:1], scale=1.0).then_inc(ac, 1)
                sc2.wait_ge(ac, em.inc("ac"))
                sc2.activation(out=rstd[:], in_=lnv[:], func=AF.Exp,
                               bias=0.0, scale=-0.5).then_inc(ac, 1)
            em.sec("scalar", rstd_f)
            sac = em.inc("ac")

            def norm(v, buf=buf, sac=sac):
                v.wait_ge(ac, sac)
                v.tensor_scalar(out=xnt[:], in0=buf[:], scalar1=negm[:, 0:1],
                                scalar2=rstd[:, 0:1], op0=ALU.add,
                                op1=ALU.mult).then_inc(dv, 1)
                v.wait_ge(dv, em.inc("dv"))
                v.reduce_sum(out=t1[:, 0:32],
                             in_=xnt[:].rearrange("p (g k) -> p g k", g=32),
                             axis=AX.X).then_inc(dv, 1)
                v.wait_ge(dv, em.inc("dv"))
                v.tensor_copy(out=xgs[:], in_=t1[:, 0:32]).then_inc(dv, 1)
            em.sec("vector", norm)
            norm_dv = em.inc("dv")

            if tt + 1 < NT:
                def prefetch(s, tt=tt, w=norm_dv):
                    s.wait_ge(dv, w)
                    load_x(s, tt + 1)
                em.sec("sync", prefetch)

            def gtr(t, w=norm_dv, cw=cstw):
                t.wait_ge(*cw)
                t.wait_ge(dv, w)
                t.transpose(out=ptp[0:32, 0:P], in_=xgs[:],
                            identity=idn[:]).then_inc(pe, 1)
            em.sec("tensor", gtr)
            gpe = em.inc("pe")

            def gcp(v, tt=tt, gpe=gpe):
                v.wait_ge(pe, gpe)
                v.tensor_copy(out=XB[:, bass.ts(tt, P)],
                              in_=ptp[0:32, 0:P]).then_inc(dv, 1)
            em.sec("vector", gcp)
            gcp_dv = em.inc("dv")

            for c in range(16):
                def ptr(t, c=c, w=gcp_dv, base=em.val("dv")):
                    t.wait_ge(dv, w if c == 0 else base)
                    for hh in range(4):
                        for par in range(2):
                            off = 8 * c + 2 * hh + par
                            t.transpose(
                                out=ptp[0:32, (2 * hh + par) * P:
                                        (2 * hh + par + 1) * P],
                                in_=xnt[:, off:D:P],
                                identity=idn[:]).then_inc(pe, 1)
                em.sec("tensor", ptr)
                tb = em.val("pe")
                em.inc("pe", 8)

                def pcp(v, tt=tt, c=c, tb=tb):
                    v.wait_ge(pe, tb + 8)
                    for hh in range(4):
                        for par in range(2):
                            dst = xnE if par == 0 else xnO
                            i_ = v.tensor_copy(
                                out=dst[32 * hh:32 * hh + 32,
                                        c * NTOK + tt * P:c * NTOK + (tt + 1) * P],
                                in_=ptp[0:32, (2 * hh + par) * P:
                                        (2 * hh + par + 1) * P])
                    i_.then_inc(dv, 1)
                em.sec("vector", pcp)
                em.inc("dv")
        ln_done = em.val("dv")

        wload = {}

        def wload_f(g, c):
            em.dma(g, "w0", wwb[0][:], w_p[bass.ts(c, P), :])
            wload[c] = em.wv("w0")
        em.sec("gpsimd", lambda g: wload_f(g, 0))

        dst_so = {}
        for c in range(16):
            for hv in range(2):
                HV = slice(hv * 2048, hv * 2048 + 2048)

                def deq(v, c=c, hv=hv, HV=HV, w=wload[c], sw=se_dv,
                        stw=dst_so.get(c - 1)):
                    v.wait_ge(*w)
                    v.wait_ge(dv, sw)
                    if hv == 0 and stw:
                        v.wait_ge(*stw)
                    v.tensor_scalar(out=t1[:], in0=wwb[c % 2][:, HV],
                                    scalar1=1.0 / 16.0, scalar2=-0.46875,
                                    op0=ALU.mult, op1=ALU.add).then_inc(dv, 1)
                em.sec("vector", deq)
                d1 = em.inc("dv")

                def deqa(sc2, d1=d1):
                    sc2.wait_ge(dv, d1)
                    sc2.activation(out=him[:], in_=t1[:], func=AF.Copy,
                                   bias=129.0, scale=1.0).then_inc(ac, 1)
                em.sec("scalar", deqa)
                d2 = em.inc("ac")

                def deq2(v, c=c, hv=hv, HV=HV, d2=d2):
                    v.wait_ge(ac, d2)
                    v.scalar_tensor_tensor(out=H1[:, HV], in0=him[:],
                                           scalar=-137.0, in1=se1[:, HV],
                                           op0=ALU.add, op1=ALU.mult)
                    v.scalar_tensor_tensor(out=t1[:], in0=him[:],
                                           scalar=-16.0, in1=wwb[c % 2][:, HV],
                                           op0=ALU.mult, op1=ALU.add)
                    v.scalar_tensor_tensor(out=L1b[:, HV], in0=t1[:],
                                           scalar=2056.0, in1=se1[:, HV],
                                           op0=ALU.add,
                                           op1=ALU.mult).then_inc(dv, 1)
                em.sec("vector", deq2)
                dqw = em.inc("dv")

            def wstore(s, c=c, w=dqw):
                s.wait_ge(dv, w)
                em.dma(s, "ws", wl_dram[0, c], H1[:])
                em.dma(s, "ws", wl_dram[1, c], L1b[:])
            em.sec("sync", wstore)
            dst_so[c] = em.wv("ws")

            if c + 1 < 16:
                def wpre(g, c=c, w=dqw):
                    g.wait_ge(dv, w)
                    wload_f(g, c + 1)
                em.sec("gpsimd", wpre)
        deq_done = em.wv("ws")

        wsload = {}

        def wsl_load(g, np_, oq):
            for par in range(2):
                em.dma(g, f"wq{par}",
                       wsl[par][:].rearrange("p (c o) -> p c o", o=1024),
                       wl_dram[par, :, :, oq * 1024:(oq + 1) * 1024].rearrange(
                           "c p o -> p c o"))
            wsload[(np_, oq)] = (em.wv("wq0"), em.wv("wq1"))

        def wc0(g, w=deq_done):
            g.wait_ge(*w)
            wsl_load(g, 0, 0)
        em.sec("gpsimd", wc0)

        ymm_pe = {}
        yev_dv = {}
        yst_so = {}
        xlw = {}
        for np_ in range(4):
            for oq in range(4):
                key = (np_, oq)

                def xl2(s, np_=np_, oq=oq, key=key,
                        w=yev_dv.get((np_ - 1, oq), 0)):
                    if w:
                        s.wait_ge(dv, w)
                    for half in range(2):
                        tt = 2 * np_ + half
                        em.dma(s, f"xs{half}", xb[half][:, 0:1024],
                               x[bass.ts(tt, P), oq * 1024:(oq + 1) * 1024])
                    xlw[key] = (em.wv("xs0"), em.wv("xs1"))
                em.sec("sync", xl2)

                def ymm(t, np_=np_, oq=oq, key=key, w=wsload[key],
                        lnw=ln_done, pw=yev_dv.get((np_ - 1, oq), 0)):
                    t.wait_ge(*w[0])
                    t.wait_ge(*w[1])
                    t.wait_ge(dv, lnw)
                    if pw:
                        t.wait_ge(dv, pw)
                    for nt2 in range(2):
                        tt = 2 * np_ + nt2
                        pA = (pg, pu)[nt2]
                        for c in range(16):
                            for par in range(2):
                                xn_ = xnE if par == 0 else xnO
                                lhsT = xn_[:, c * NTOK + tt * P:
                                           c * NTOK + (tt + 1) * P]
                                rhsw = wsl[par][:, c * 1024:(c + 1) * 1024]
                                st_ = (c == 0 and par == 0)
                                sp_ = (c == 15 and par == 1)
                                m0 = t.matmul(out=pA[:, 0:512], lhsT=lhsT,
                                              rhs=rhsw[:, 0:512], start=st_,
                                              stop=sp_)
                                m1 = t.matmul(out=pA[:, 512:1024], lhsT=lhsT,
                                              rhs=rhsw[:, 512:1024], start=st_,
                                              stop=sp_)
                                if sp_:
                                    m0.then_inc(pe, 1)
                                    m1.then_inc(pe, 1)
                em.sec("tensor", ymm)
                ymm_pe[key] = em.inc("pe", 4)

                nxt = (np_, oq + 1) if oq < 3 else (np_ + 1, 0)
                if nxt[0] < 4:
                    def wpre2(g, nxt=nxt, w=ymm_pe[key]):
                        g.wait_ge(pe, w)
                        wsl_load(g, nxt[0], nxt[1])
                    em.sec("gpsimd", wpre2)

                for nt2 in range(2):
                    tt = 2 * np_ + nt2
                    pA = (pg, pu)[nt2]

                    def yev(v, key=key, nt2=nt2, pA=pA,
                            w=ymm_pe[key], xw2=xlw[key],
                            sw=yst_so.get((np_ - 1, oq, nt2))):
                        v.wait_ge(pe, w)
                        v.wait_ge(*xw2[nt2])
                        if sw:
                            v.wait_ge(*sw)
                        v.tensor_tensor(out=yb[nt2][:], in0=pA[:],
                                        in1=xb[nt2][:, 0:1024],
                                        op=ALU.add).then_inc(dv, 1)
                    em.sec("vector", yev)
                    yev_dv[key] = em.inc("dv")

                    def yst(s, key=key, nt2=nt2, tt=tt, oq=oq,
                            w=yev_dv[key]):
                        s.wait_ge(dv, w)
                        em.dma(s, f"y{nt2}",
                               y[bass.ts(tt, P), oq * 1024:(oq + 1) * 1024],
                               yb[nt2][:])
                    em.sec("sync", yst)
                    yst_so[(np_, oq, nt2)] = em.wv(f"y{nt2}")

        def fin(s, w0=em.wv("y0"), w1=em.wv("y1")):
            s.wait_ge(*w0)
            s.wait_ge(*w1)
        em.sec("sync", fin)
    return nc



def _gather_kv(res1):
    kf_all = [np.zeros((KD, S), BF) for _ in range(2)]
    vt_all = [np.zeros((S, KD), BF) for _ in range(2)]
    for c in range(8):
        b, j = c // 4, c % 4
        rows = _own_rows(j)
        kf_all[b][:, rows] = np.asarray(res1.results[c]["kf"])
        vt_all[b][rows, :] = np.asarray(res1.results[c]["vt"])
    return kf_all, vt_all


def kernel(**inputs):
    f32 = np.float32
    X = np.asarray(inputs["L0_attn_X"], f32)

    wq0 = _pretile(np.asarray(inputs["L0_attn_Wq"], f32))
    wkv0 = _pretile(np.concatenate(
        [np.asarray(inputs["L0_attn_Wk"], f32),
         np.asarray(inputs["L0_attn_Wv"], f32)], axis=1))
    wo0 = np.ascontiguousarray(np.asarray(inputs["L0_attn_Wo"], f32))
    gate_r = _pack_reorder_gu(np.asarray(inputs["L0_ffn_gate_Wq"], np.int32))
    up_r = _pack_reorder_gu(np.asarray(inputs["L0_ffn_up_Wq"], np.int32))
    down_t = np.ascontiguousarray(np.asarray(inputs["L0_ffn_down_Wq"],
                                             np.int32).T)
    sgT = np.ascontiguousarray(np.asarray(inputs["L0_ffn_gate_scale"], f32).T)
    zgT = np.ascontiguousarray(np.asarray(inputs["L0_ffn_gate_zero"], f32).T)
    suT = np.ascontiguousarray(np.asarray(inputs["L0_ffn_up_scale"], f32).T)
    zuT = np.ascontiguousarray(np.asarray(inputs["L0_ffn_up_zero"], f32).T)
    sdT = np.ascontiguousarray(
        np.asarray(inputs["L0_ffn_down_scale"], f32).T.reshape(43, 4, D))
    zdT = np.ascontiguousarray(
        np.asarray(inputs["L0_ffn_down_zero"], f32).T.reshape(43, 4, D))
    wq1 = _pretile(np.ascontiguousarray(np.asarray(inputs["L1_attn_Wq"], f32).T))
    wkr1 = _pretile(np.ascontiguousarray(np.asarray(inputs["L1_attn_Wkr"], f32).T))
    wdk1 = _pretile(np.ascontiguousarray(np.asarray(inputs["L1_attn_Wdk"], f32).T))
    wupk1 = _pretile(np.ascontiguousarray(np.asarray(inputs["L1_attn_Wupk"], f32).T))
    wupv1 = _pretile(np.ascontiguousarray(np.asarray(inputs["L1_attn_Wupv"], f32).T))
    wo1 = np.ascontiguousarray(np.asarray(inputs["L1_attn_Wo"], f32).T)
    w1p = _pack_l1(np.asarray(inputs["L1_ffn_packed_weight"], np.int32))
    sc1 = np.ascontiguousarray(
        np.asarray(inputs["L1_ffn_scales"], f32).reshape(D, 32).T)
    i4 = np.zeros((P, 4), BF)
    for p in range(P):
        i4[p, p // 32] = 1
    i4t = np.ascontiguousarray(i4.T)

    ropes = []
    for j in range(4):
        rows = _own_rows(j)
        cqs, sqs = _rope_tables(rows, 1.0 / np.sqrt(DH))
        cks, sks = _rope_tables(rows, 1.0)
        ropes.append((cqs, sqs, cks, sks))

    nc = build_qkv_l0()
    maps = []
    for c in range(8):
        b, j = c // 4, c % 4
        cqs, sqs, cks, sks = ropes[j]
        maps.append({"x": np.ascontiguousarray(X[b][_own_rows(j)]),
                     "wq": wq0, "wkv": wkv0,
                     "cq": cqs, "sq": sqs, "ck": cks, "sk": sks,
                     "ident": _IDENT})
    r1 = run_bass_kernel_spmd(nc, maps, core_ids=list(range(8)))
    kf_all, vt_all = _gather_kv(r1)

    nc = build_attn()
    maps2 = []
    for c in range(8):
        b, j = c // 4, c % 4
        maps2.append({"qf": np.asarray(r1.results[c]["qf"]),
                      "kf": kf_all[b], "vtok": vt_all[b],
                      "x": maps[c]["x"], "wo": wo0,
                      "masks": _masks(j), "ones": np.ones((P, P), BF)})
    r2 = run_bass_kernel_spmd(nc, maps2, core_ids=list(range(8)))

    nc = build_ffn0()
    maps3 = []
    for c in range(8):
        maps3.append({"x": np.asarray(r2.results[c]["y"], f32),
                      "gate_p": gate_r, "up_p": up_r, "down_p": down_t,
                      "sg": sgT, "zg": zgT, "su": suT, "zu": zuT,
                      "sd": sdT, "zd": zdT,
                      "ind4": i4, "ind4t": i4t, "ident": _IDENT})
    r3 = run_bass_kernel_spmd(nc, maps3, core_ids=list(range(8)))

    nc = build_qkv_l1()
    maps4 = []
    for c in range(8):
        b, j = c // 4, c % 4
        cqs, sqs, cks, sks = ropes[j]
        maps4.append({"x": maps3[c]["x"],
                      "f0": np.ascontiguousarray(
                          np.asarray(r3.results[c]["y"]).T),
                      "wq1": wq1, "wkr": wkr1, "wdk": wdk1,
                      "wupk": wupk1, "wupv": wupv1,
                      "cq": cqs, "sq": sqs, "ck": cks, "sk": sks,
                      "ident": _IDENT})
    r4 = run_bass_kernel_spmd(nc, maps4, core_ids=list(range(8)))
    kf_all, vt_all = _gather_kv(r4)

    nc = build_attn()
    maps5 = []
    for c in range(8):
        b, j = c // 4, c % 4
        maps5.append({"qf": np.asarray(r4.results[c]["qf"]),
                      "kf": kf_all[b], "vtok": vt_all[b],
                      "x": np.asarray(r4.results[c]["x2"], f32), "wo": wo1,
                      "masks": _masks(j), "ones": np.ones((P, P), BF)})
    r5 = run_bass_kernel_spmd(nc, maps5, core_ids=list(range(8)))

    nc = build_ffn1()
    maps6 = []
    for c in range(8):
        maps6.append({"x": np.asarray(r5.results[c]["y"], f32),
                      "w_p": w1p, "sc_in": sc1, "ident": _IDENT})
    r6 = run_bass_kernel_spmd(nc, maps6, core_ids=list(range(8)))

    out = np.zeros((2, S, D), f32)
    for c in range(8):
        b, j = c // 4, c % 4
        out[b][_own_rows(j)] = np.asarray(r6.results[c]["y"], f32)
    return out
